# version 02
# baseline (speedup 1.0000x reference)
import numpy as np
import ml_dtypes
from contextlib import ExitStack

import concourse.bass as bass
import concourse.mybir as mybir
from concourse.bass_utils import run_bass_kernel_spmd

F32 = mybir.dt.float32
BF16 = mybir.dt.bfloat16
F16 = mybir.dt.float16
AF = mybir.ActivationFunctionType
ALU = mybir.AluOpType
AX = mybir.AxisListType
BF16NP = ml_dtypes.bfloat16

B, S, DM = 4, 4096, 1024
TOK = 2048
NCH = DM // 128
NQB = TOK // 512
MAGIC = 12582912.0
LN_EPS = 1e-5
INV_DM = 1.0 / DM

ENGS = ("sy", "v", "a", "t", "g")


class Sem:
    def __init__(self, nc, es, name):
        self.h = es.enter_context(nc.semaphore(name))
        self.n = 0

    def at(self):
        return (self, self.n)


class KB:
    def __init__(self):
        self.nc = bass.Bass("TRN2", target_bir_lowering=False, debug=False)
        self.es = ExitStack()
        self.streams = {e: [] for e in ENGS}
        self.self_sems = {}

    def sb(self, name, shape, dtype):
        return self.es.enter_context(self.nc.sbuf_tensor(name, shape, dtype))

    def psum(self, name, shape, dtype=F32):
        return self.es.enter_context(self.nc.psum_tensor(name, shape, dtype))

    def sem(self, name):
        return Sem(self.nc, self.es, name)

    def op(self, eng, fn, waits=(), inc=None, dma=False):
        wl = []
        auto = self.self_sems.get(eng) if not dma else None
        if auto is not None:
            if auto.n > 0:
                wl.append((auto.h, auto.n))
            if inc is None:
                inc = auto
        for w in waits:
            if isinstance(w, Sem):
                wl.append((w.h, w.n))
            else:
                s, n = w
                if n > 0:
                    wl.append((s.h, n))
        il = None
        if inc is not None:
            k = 16 if dma else 1
            inc.n += k
            il = (inc.h, k)

        def run(e, fn=fn, wl=wl, il=il):
            for h, n in wl:
                if n > 0:
                    e.wait_ge(h, n)
            instr = fn(e)
            if il is not None:
                instr.then_inc(il[0], il[1])
        self.streams[eng].append(run)

    def replay(self):
        block = self.es.enter_context(self.nc.Block())

        def runner(ops):
            def go(e):
                for f in ops:
                    f(e)
            return go
        block.sync(runner(self.streams["sy"]))
        block.vector(runner(self.streams["v"]))
        block.scalar(runner(self.streams["a"]))
        block.tensor(runner(self.streams["t"]))
        block.gpsimd(runner(self.streams["g"]))


def build_program(dbg_outs=False):
    dkind = "ExternalOutput" if dbg_outs else "Internal"
    kb = KB()
    nc = kb.nc
    op = kb.op

    def din(name, shape, dtype=F32):
        return nc.dram_tensor(name, shape, dtype, kind="ExternalInput").ap()

    x_q = din("x_q", [TOK, DM])
    x_seq = din("x_seq", [S, DM])
    kidx = din("kidx", [128, 32])
    W_dq = din("W_dq", [DM, 256])
    W_uq = din("W_uq", [256, DM])
    q_ln_g = din("q_ln_g", [256])
    q_ln_b = din("q_ln_b", [256])
    W_dkv = din("W_dkv", [DM, 256])
    W_ukv = din("W_ukv", [256, 2048])
    kv_ln_g = din("kv_ln_g", [256])
    kv_ln_b = din("kv_ln_b", [256])
    W_o = din("W_o", [DM, DM])
    w1_w = din("w1_w", [4096, DM])
    w1_s = din("w1_s", [4096])
    w2_w = din("w2_w", [4096, DM])
    w2_s = din("w2_s", [4096])
    h1_w = din("h1_w", [4096, 4096])
    h1_s = din("h1_s", [4096])
    h2_w = din("h2_w", [4096, 4096])
    h2_s = din("h2_s", [4096])
    w3_w = din("w3_w", [DM, 4096])
    w3_s = din("w3_s", [DM])
    qkv_w = din("qkv_w", [3072, DM])
    qn_w = din("qn_w", [64])
    qn_b = din("qn_b", [64])
    kn_w = din("kn_w", [64])
    kn_b = din("kn_b", [64])
    sinks = din("sinks", [1, 16])
    proj_w = din("proj_w", [DM, DM])
    proj_b = din("proj_b", [DM])
    w_gate = din("w_gate", [3072, DM])
    w_up = din("w_up", [3072, DM])
    w_down = din("w_down", [DM, 3072])
    cosb = din("cosb", [128, TOK], mybir.dt.float16)
    sinb = din("sinb", [128, TOK], mybir.dt.float16)
    kix1 = din("kix1", [128, 36])
    y = nc.dram_tensor("y", [TOK, DM], F32, kind="ExternalOutput").ap()

    woT_bf = nc.dram_tensor("woT_bf", [DM, DM], BF16)
    w1_bf = nc.dram_tensor("w1_bf", [4096, DM], BF16)
    w2_bf = nc.dram_tensor("w2_bf", [4096, DM], BF16)
    h1_bf = nc.dram_tensor("h1_bf", [4096, 4096], BF16)
    h2_bf = nc.dram_tensor("h2_bf", [4096, 4096], BF16)
    w3_bf = nc.dram_tensor("w3_bf", [DM, 4096], BF16)
    qkv_bf = nc.dram_tensor("qkv_bf", [3072, DM], BF16)
    proj_bf = nc.dram_tensor("proj_bf", [DM, DM], BF16)
    wg_bf = nc.dram_tensor("wg_bf", [3072, DM], BF16, kind=dkind)
    wu_bf = nc.dram_tensor("wu_bf", [3072, DM], BF16, kind=dkind)
    wd_bf = nc.dram_tensor("wd_bf", [DM, 3072], BF16, kind=dkind)
    gbuf = nc.dram_tensor("gbuf", [24, 128, TOK], BF16, kind=dkind)
    g2buf = nc.dram_tensor("g2buf", [24, 128, TOK], BF16, kind=dkind)
    ubuf = nc.dram_tensor("ubuf", [24, 128, TOK], BF16, kind=dkind)
    obuf = nc.dram_tensor("obuf", [24, 128, TOK], BF16, kind=dkind)
    mx_in = [nc.dram_tensor(f"mx_in{i}", [1, 2], F32) for i in range(4)]
    mx_out = [nc.dram_tensor(f"mx_out{i}", [1, 2], F32, kind=dkind)
              for i in range(4)]
    bdk_in = [nc.dram_tensor(f"bdk_in{g}", [128, 256], BF16) for g in range(8)]
    bdk_out = [nc.dram_tensor(f"bdk_out{g}", [256, 256], BF16) for g in range(8)]
    bdv_in = [nc.dram_tensor(f"bdv_in{g}", [256, 130], BF16) for g in range(8)]
    bdv_out = [nc.dram_tensor(f"bdv_out{g}", [512, 130], BF16) for g in range(8)]

    ident_f32 = nc.inline_tensor(np.eye(128, dtype=np.float32), "c_idf")
    ident_bf = nc.inline_tensor(np.eye(128, dtype=BF16NP), "c_idb")
    ones_col = nc.inline_tensor(np.ones((128, 1), dtype=BF16NP), "c_onc")
    ones_row_bf = nc.inline_tensor(np.ones((1, 128), dtype=BF16NP), "c_onrb")
    ones_row_f = nc.inline_tensor(np.ones((1, 128), dtype=np.float32), "c_onrf")
    ones_f_full = nc.inline_tensor(np.ones((128, 64), dtype=np.float32), "c_onff")
    rmat = np.zeros((128, 128), dtype=np.float32)
    for hh in range(2):
        b0 = 64 * hh
        for i in range(32):
            rmat[b0 + i, b0 + 32 + i] = -1.0
            rmat[b0 + 32 + i, b0 + i] = 1.0
    rotT_c = nc.inline_tensor(rmat.T.astype(BF16NP).copy(), "c_rotT")
    qixr = np.arange(TOK, dtype=np.float16)[None, :]
    qixr_c = nc.inline_tensor(qixr, "c_qixr")
    ones_row_h = nc.inline_tensor(np.ones((1, 128), dtype=np.float16), "c_onrh")

    RES = kb.sb("RES", [128, NCH * TOK], F32)
    A1 = kb.sb("A1", [128, 40960], BF16)
    A2 = kb.sb("A2", [128, 4608], BF16)
    SCR = kb.sb("SCR", [128, 4096], F32)
    STT = kb.sb("STT", [128, 1024], F32)
    WS = kb.sb("WS", [128, 6144], BF16)
    WPERS = kb.sb("WPERS", [128, 6144], BF16)
    CW = kb.sb("CW", [128, 64], F32)
    IDF = kb.sb("IDF", [128, 128], F32)
    IDB = kb.sb("IDB", [128, 128], BF16)
    ONC = kb.sb("ONC", [128, 1], BF16)
    ONRB = kb.sb("ONRB", [1, 128], BF16)
    ONRF = kb.sb("ONRF", [1, 128], F32)
    ONFF = kb.sb("ONFF", [128, 64], F32)
    ONRH = kb.sb("ONRH", [1, 128], F16)
    KIX = kb.sb("KIX", [128, 32], F32)
    QIX = kb.sb("QIX", [1, TOK], F16)

    FS1 = kb.sb("FS1", [128, 32], F32)
    FS2 = kb.sb("FS2", [128, 32], F32)
    FH1 = kb.sb("FH1", [128, 32], F32)
    FH2 = kb.sb("FH2", [128, 32], F32)
    FS3 = kb.sb("FS3", [128, 8], F32)
    ROTT = kb.sb("ROTT", [128, 128], BF16)
    KIX1 = kb.sb("KIX1", [128, 36], F32)
    QNWB = kb.sb("QNWB", [128, 4], F32)
    SNK = kb.sb("SNK", [1, 16], F32)
    PBC = kb.sb("PBC", [128, 8], F32)
    ESNK = kb.sb("ESNK", [128, 2], F32)
    BQC = kb.sb("BQC", [128, 8], F32)
    BKC = kb.sb("BKC", [128, 8], F32)
    BVR = kb.sb("BVR", [1, 1024], BF16)
    NEGG = kb.sb("NEGG", [128, 16], F32)

    WDQ = WPERS[:, 0:2048]
    WDKV = WPERS[:, 2048:4096]
    WUQ = WPERS[:, 0:2048]
    WUKV = WPERS[:, 2048:6144]

    CKVT = A1[:, 0:8192].rearrange("p (c t) -> p c t", c=2)
    KT = A1[:, 8192:12288]
    VA = A1[:, 12288:16448].rearrange("p (c t) -> p c t", c=32)
    CQT = A1[:, 16448:20544].rearrange("p (c t) -> p c t", c=2)
    ATT = A1[:, 20544:36928].rearrange("p (c t) -> p c t", c=8)
    ESL = A1[:, 36928:40000].rearrange("p (c t) -> p c t", c=6)
    BO = A1[:, 0:16384].rearrange("p (c t) -> p c t", c=8)
    QTG = WS[:, 0:2048]

    XNT = [A2[:, s * 1024:(s + 1) * 1024].rearrange("p (c t) -> p c t", c=8)
           for s in range(2)]
    XNS = [A2[:, 2048 + s * 1024: 3072 + s * 1024] for s in range(2)]
    CNS = A2[:, 4096:4352]
    SQB = [A2[:, 2048 + s * 512: 2560 + s * 512] for s in range(2)]

    mm = [kb.psum(f"mm{i}", [128, 512], F32) for i in range(4)]
    sc = [kb.psum(f"sc{i}", [128, 512], F32) for i in range(2)]
    bcb = kb.psum("bcb", [128, 512], F32)
    trp = kb.psum("trp", [128, 256], BF16)

    sId = kb.sem("sId")
    sLd = [kb.sem(f"sLd{i}") for i in range(2)]
    sWa = kb.sem("sWa")
    sWo = [kb.sem(f"sWo{i}") for i in range(2)]
    sGpW = kb.sem("sGpW")
    sV = kb.sem("sV")
    sA = kb.sem("sA")
    sT = kb.sem("sT")
    sG = kb.sem("sG")
    sSt = [kb.sem(f"sSt{i}") for i in range(2)]
    sGAtt = [kb.sem(f"sGAtt{i}") for i in range(2)]
    kb.self_sems = {"v": sV, "a": sA, "g": sG}
    kb._oddslot = 0

    for dst, src in ((IDF, ident_f32), (IDB, ident_bf), (ONC, ones_col),
                     (ONRB, ones_row_bf), (ONRF, ones_row_f),
                     (ONFF, ones_f_full),
                     (QIX, qixr_c), (ONRH, ones_row_h)):
        op("sy", lambda e, d=dst, s=src: e.dma_start(out=d[:], in_=s.ap()),
           inc=sId, dma=True)
    op("sy", lambda e: e.dma_start(out=KIX[:], in_=kidx), inc=sId, dma=True)
    op("sy", lambda e: e.dma_start(out=ROTT[:], in_=rotT_c.ap()), inc=sId,
       dma=True)
    op("sy", lambda e: e.dma_start(out=KIX1[:], in_=kix1), inc=sId, dma=True)
    op("sy", lambda e: e.dma_start(out=SNK[:], in_=sinks), inc=sId, dma=True)
    for col, src in ((0, qn_w), (1, qn_b), (2, kn_w), (3, kn_b)):
        for half in range(2):
            op("sy", lambda e, col=col, src=src, half=half: e.dma_start(
                out=QNWB[half * 64:(half + 1) * 64, col:col + 1],
                in_=src.rearrange("(a d) -> d a", a=1),
                allow_slow_non_contiguous=True), inc=sId, dma=True)
    op("sy", lambda e: e.dma_start(
        out=PBC[:], in_=proj_b.rearrange("(c p) -> p c", p=128),
        allow_slow_non_contiguous=True), inc=sId, dma=True)
    GQC, BQ0, GKC, BK0 = CW[:, 0:2], CW[:, 2:4], CW[:, 4:6], CW[:, 6:8]
    for dst, src in ((GQC, q_ln_g), (BQ0, q_ln_b), (GKC, kv_ln_g),
                     (BK0, kv_ln_b), (FS1[:], w1_s), (FS2[:], w2_s),
                     (FH1[:], h1_s), (FH2[:], h2_s), (FS3[:], w3_s)):
        op("sy", lambda e, d=dst, s=src: e.dma_start(
            out=d, in_=s.rearrange("(c p) -> p c", p=128),
            allow_slow_non_contiguous=True), inc=sId, dma=True)

    op("g", lambda e: e.dma_start(
        out=WDQ[:].rearrange("p (c o) -> p c o", o=256),
        in_=W_dq.rearrange("(c p) o -> p c o", p=128)), inc=sGpW, dma=True)
    op("g", lambda e: e.dma_start(
        out=WDKV[:].rearrange("p (c o) -> p c o", o=256),
        in_=W_dkv.rearrange("(c p) o -> p c o", p=128)), inc=sGpW, dma=True)
    sPrA = kb.sem("sPrA")
    sPrB = kb.sem("sPrB")

    def prep_chain(sem, slab, jobs):
        for src, dst, rows, cols in jobs:
            for i in range(rows // 128):
                for cb in range(cols // 1024):
                    op("g", lambda e, src=src, i=i, cb=cb, slab=slab:
                       e.dma_start(
                           out=slab,
                           in_=src[i * 128:(i + 1) * 128,
                                   cb * 1024:(cb + 1) * 1024]),
                       waits=[sem.at()], inc=sem, dma=True)
                    op("g", lambda e, dst=dst, i=i, cb=cb, slab=slab:
                       e.dma_start(
                           out=dst.ap()[i * 128:(i + 1) * 128,
                                        cb * 1024:(cb + 1) * 1024],
                           in_=slab),
                       waits=[sem.at()], inc=sem, dma=True)

    prep_chain(sPrA, WS[:, 2048:3072],
               [(W_o, woT_bf, DM, DM), (w1_w, w1_bf, 4096, DM),
                (w2_w, w2_bf, 4096, DM), (h1_w, h1_bf, 4096, 4096)])
    prep_chain(sPrB, WS[:, 3072:4096],
               [(h2_w, h2_bf, 4096, 4096), (w3_w, w3_bf, DM, 4096),
                (qkv_w, qkv_bf, 3072, DM), (proj_w, proj_bf, DM, DM)])
    prB_qkv_done = (4096 // 128) * 4 * 32 + (DM // 128) * 4 * 32 \
        + (3072 // 128) * 1 * 32
    prB_all = prB_qkv_done + (DM // 128) * 1 * 32
    prA_wo_done = 8 * 2 * 16

    xstg = [SCR[:, 0:1024], SCR[:, 1024:2048]]
    xsq = SCR[:, 2048:3072]
    csq = SCR[:, 3072:3328]
    MUC, RSC = CW[:, 8:9], CW[:, 9:10]
    T0, T1, T2, T3 = (CW[:, 10:11], CW[:, 11:12], CW[:, 12:13], CW[:, 13:14])
    CM, CR = CW[:, 14:15], CW[:, 15:16]
    C0, C1, C2 = CW[:, 16:17], CW[:, 17:18], CW[:, 18:19]

    g_xstg = [[], []]
    g_xnt = [0, 0]
    g_xns = [0, 0]
    g_cns = [0]

    def tok_ln_tile(seq, it, src_dram, src_row):
        sl = sLd[seq % 2]
        st = xstg[seq % 2]
        op("sy", lambda e, st=st: e.dma_start(
            out=st, in_=src_dram[src_row:src_row + 128, :]),
           waits=list(g_xstg[seq % 2]), inc=sl, dma=True)
        ldw = sl.at()
        op("v", lambda e, st=st: e.reduce_sum(T0, st, axis=AX.X),
           waits=[ldw], inc=sV)
        op("a", lambda e, st=st: e.activation(out=xsq, in_=st, func=AF.Square),
           waits=[ldw, sV.at()], inc=sA)
        op("v", lambda e: e.reduce_sum(T1, xsq, axis=AX.X), waits=[sA], inc=sV)
        op("a", lambda e: e.activation(out=MUC, in_=T0, func=AF.Copy,
                                       scale=INV_DM), waits=[sV], inc=sA)
        op("a", lambda e: e.activation(out=T2, in_=MUC, func=AF.Square),
           waits=[sA.at()], inc=sA)
        op("v", lambda e: e.tensor_scalar(out=T3, in0=T1, scalar1=INV_DM,
                                          scalar2=LN_EPS, op0=ALU.mult,
                                          op1=ALU.add), waits=[sA], inc=sV)
        op("v", lambda e: e.tensor_sub(T3, T3, T2), inc=sV)
        op("v", lambda e: e.reciprocal(T3, T3), inc=sV)
        op("a", lambda e: e.activation(out=RSC, in_=T3, func=AF.Sqrt),
           waits=[sV], inc=sA)
        xns = XNS[seq % 2]
        op("v", lambda e, st=st, xns=xns: e.tensor_scalar(
            out=xns, in0=st, scalar1=MUC, scalar2=RSC,
            op0=ALU.subtract, op1=ALU.mult),
           waits=[sA, (sT, g_xns[seq % 2])], inc=sV)
        g_xstg[seq % 2] = [sV.at()]
        xnt = XNT[seq % 2]
        for c in range(NCH):
            op("t", lambda e, xns=xns, c=c: e.transpose(
                trp[:, 0:128], xns[:, c * 128:(c + 1) * 128], IDB[:]),
               waits=[sV, sA, sId.at()], inc=sT)
            w = [sT.at()]
            if c == 0:
                w.append((sT, g_xnt[seq % 2]))
            op("a", lambda e, xnt=xnt, c=c: e.copy(xnt[:, c, 0:128],
                                                   trp[:, 0:128]),
               waits=w, inc=sA)
        g_xns[seq % 2] = sT.n
        return st

    def raw_res_transpose(seq, st, it2):
        for c in range(NCH):
            op("t", lambda e, st=st, c=c: e.transpose(
                bcb[:, 0:128], st[:, c * 128:(c + 1) * 128], IDF[:]),
               waits=[sA], inc=sT)
            op("a", lambda e, c=c, it2=it2: e.copy(
                RES[:, c * TOK + it2 * 128: c * TOK + (it2 + 1) * 128],
                bcb[:, 0:128]), waits=[sT], inc=sA)
        g_xstg[seq % 2].append(sT.at())

    def proj_c_tile(seq, wsb, dst_ct, dst_col):
        xnt = XNT[seq % 2]
        bank = mm[seq % 2]
        for c in range(NCH):
            op("t", lambda e, b=bank, xnt=xnt, c=c, w=wsb: e.matmul(
                b[:, 0:256], xnt[:, c, 0:128], w[:, c * 256:(c + 1) * 256],
                start=(c == 0), stop=(c == NCH - 1)),
               waits=[sA, sV, (sGpW, 32)] if c == 0 else [],
               inc=sT if c == NCH - 1 else None)
        g_xnt[seq % 2] = sT.n
        op("v", lambda e, b=bank: e.reduce_sum(C0, b[:, 0:256], axis=AX.X),
           waits=[sT], inc=sV)
        op("a", lambda e, b=bank: e.activation(out=csq, in_=b[:, 0:256],
                                               func=AF.Square),
           waits=[sT, sV.at()], inc=sA)
        op("v", lambda e: e.reduce_sum(C1, csq, axis=AX.X), waits=[sA], inc=sV)
        inv = 1.0 / 256.0
        op("a", lambda e: e.activation(out=C0, in_=C0, func=AF.Copy,
                                       scale=inv), waits=[sV], inc=sA)
        op("a", lambda e: e.activation(out=C2, in_=C0, func=AF.Square),
           waits=[sA.at()], inc=sA)
        op("v", lambda e: e.tensor_scalar(out=C1, in0=C1, scalar1=inv,
                                          scalar2=LN_EPS, op0=ALU.mult,
                                          op1=ALU.add), waits=[sA], inc=sV)
        op("v", lambda e: e.tensor_sub(C1, C1, C2), inc=sV)
        op("v", lambda e: e.reciprocal(C1, C1), inc=sV)
        op("a", lambda e: e.activation(out=C1, in_=C1, func=AF.Sqrt),
           waits=[sV], inc=sA)
        op("v", lambda e, b=bank: e.tensor_scalar(
            out=CNS, in0=b[:, 0:256], scalar1=C0, scalar2=C1,
            op0=ALU.subtract, op1=ALU.mult),
           waits=[sA, (sT, g_cns[0])], inc=sV)
        for c2 in range(2):
            op("t", lambda e, c2=c2: e.transpose(
                trp[:, 128:256], CNS[:, c2 * 128:(c2 + 1) * 128], IDB[:]),
               waits=[sV, sA], inc=sT)
            op("a", lambda e, c2=c2, dst=dst_ct, dc=dst_col: e.copy(
                dst[:, c2, dc:dc + 128], trp[:, 128:256]), waits=[sT], inc=sA)
        g_cns[0] = sT.n

    seq = 0
    for it in range(S // 128):
        tok_ln_tile(seq, it, x_seq, it * 128)
        proj_c_tile(seq, WDKV, CKVT, it * 128)
        seq += 1
    for it2 in range(TOK // 128):
        st = tok_ln_tile(seq, it2, x_q, it2 * 128)
        raw_res_transpose(seq, st, it2)
        proj_c_tile(seq, WDQ, CQT, it2 * 128)
        seq += 1

    fold_f32 = SCR[:, 0:2048]
    QROW = STT[0:1, 0:1024]
    KVROW = SCR[0:1, 2048:3072]
    VROW = SCR[0:1, 3072:4096]
    BQB = kb.sb("BQB", [128, 2], BF16)
    BKB = kb.sb("BKB", [128, 2], BF16)
    op("v", lambda e: e.tensor_copy(BQB[:], BQ0), waits=[sId.at()], inc=sV)
    op("v", lambda e: e.tensor_copy(BKB[:], BK0), inc=sV)
    for c in range(2):
        op("sy", lambda e, c=c: e.dma_start(
            out=fold_f32[:, 0:1024], in_=W_uq[c * 128:(c + 1) * 128, :]),
           waits=[(sV, sV.n), (sA, sA.n), (sT, sT.n)], inc=sWa, dma=True)
        op("v", lambda e, c=c: e.tensor_scalar(
            out=WUQ[:, c * 1024:(c + 1) * 1024], in0=fold_f32[:, 0:1024],
            scalar1=GQC[:, c:c + 1], scalar2=None, op0=ALU.mult),
           waits=[sWa, sId.at()], inc=sV)
        op("v", lambda e: e.tensor_copy(WS[:, 0:1024], fold_f32[:, 0:1024]),
           inc=sV)
        for half in range(2):
            op("t", lambda e, half=half, c=c: e.matmul(
                sc[half][0:1, 0:512], BQB[:, c:c + 1],
                WS[:, half * 512: half * 512 + 512],
                start=(c == 0), stop=(c == 1)),
               waits=[(sV, sV.n)] if half == 0 else [], inc=sT)
    for half in range(2):
        op("a", lambda e, half=half: e.copy(
            QROW[:, half * 512:(half + 1) * 512], sc[half][0:1, 0:512]),
           waits=[sT], inc=sA)
    for c in range(2):
        op("sy", lambda e, c=c: e.dma_start(
            out=fold_f32, in_=W_ukv[c * 128:(c + 1) * 128, :]),
           waits=[(sV, sV.n), (sA, sA.n), (sT, sT.n)], inc=sWa, dma=True)
        op("v", lambda e, c=c: e.tensor_scalar(
            out=WUKV[:, c * 2048:(c + 1) * 2048], in0=fold_f32,
            scalar1=GKC[:, c:c + 1], scalar2=None, op0=ALU.mult),
           waits=[sWa], inc=sV)
        op("v", lambda e: e.tensor_copy(WS[:, 0:2048], fold_f32), inc=sV)
        for half in range(4):
            op("t", lambda e, half=half, c=c: e.matmul(
                mm[half][0:1, 0:512], BKB[:, c:c + 1],
                WS[:, half * 512: half * 512 + 512],
                start=(c == 0), stop=(c == 1)),
               waits=[(sV, sV.n)] if half == 0 else [], inc=sT)
    for half in range(4):
        dst = (KVROW if half < 2 else VROW)
        col = (half % 2) * 512
        op("a", lambda e, dst=dst, col=col, half=half: e.copy(
            dst[:, col:col + 512], mm[half][0:1, 0:512]), waits=[sT], inc=sA)
    for oc in range(8):
        op("t", lambda e, oc=oc: e.matmul(
            bcb[:, oc:oc + 1], QROW[:, oc * 128:(oc + 1) * 128],
            ONRF[:, 0:1], start=True, stop=True),
           waits=[sA] if oc == 0 else [], inc=sT if oc == 7 else None)
    op("a", lambda e: e.copy(BQC[:], bcb[:, 0:8]), waits=[sT], inc=sA)
    for oc in range(8):
        op("t", lambda e, oc=oc: e.matmul(
            bcb[:, oc:oc + 1], KVROW[:, oc * 128:(oc + 1) * 128],
            ONRF[:, 0:1], start=True, stop=True),
           waits=[sA] if oc == 0 else [], inc=sT if oc == 7 else None)
    op("a", lambda e: e.copy(BKC[:], bcb[:, 0:8]), waits=[sT], inc=sA)
    op("v", lambda e: e.tensor_copy(BVR[:], VROW), waits=[sA], inc=sV)

    ws_raw_free = (sT, sT.n)

    for kc in range(32):
        op("v", lambda e, kc=kc: e.memset(
            VA[:, kc, 0:130].rearrange("p (h z) -> p h z", h=2)[:, :, 64:65],
            1.0), inc=sV)

    w1T = nc.dram_tensor("w1T", [DM, 4096], BF16)
    w2T = nc.dram_tensor("w2T", [DM, 4096], BF16)
    h1T = nc.dram_tensor("h1T", [4096, 4096], BF16)
    h2T = nc.dram_tensor("h2T", [4096, 4096], BF16)
    w3T = nc.dram_tensor("w3T", [4096, DM], BF16)
    sTlA, sTpA = kb.sem("sTlA"), kb.sem("sTpA")
    sTlS, sTpS = kb.sem("sTlS"), kb.sem("sTpS")
    TPA = [WS[:, 2048 + s * 512: 2048 + (s + 1) * 512] for s in range(2)]
    TPS = [WS[:, 3072 + s * 512: 3072 + (s + 1) * 512] for s in range(2)]

    def tp_loads(eng, slots, sL, sP, jobs, first_waits):
        n = 0
        for wbf, wT, O, I in jobs:
            for i in range(I // 128):
                for o in range(O // 512):
                    op(eng, lambda e, wbf=wbf, i=i, o=o, sl=n % 2:
                       e.dma_start(
                           out=slots[sl],
                           in_=wbf.ap()[o * 512:(o + 1) * 512,
                                        i * 128:(i + 1) * 128],
                           transpose=True),
                       waits=(list(first_waits) if n == 0 else [])
                       + ([(sP, (n // 2) * 2 * 16)] if n >= 2 else []),
                       inc=sL, dma=True)
                    n += 1

    def tp_stores(slots, sL, sP, jobs):
        n = 0
        for wbf, wT, O, I in jobs:
            for i in range(I // 128):
                for o in range(O // 512):
                    op("g", lambda e, wT=wT, i=i, o=o, sl=n % 2:
                       e.dma_start(
                           out=wT.ap()[i * 128:(i + 1) * 128,
                                       o * 512:(o + 1) * 512],
                           in_=slots[sl]),
                       waits=[(sL, ((n // 2) * 2 + 2) * 16)], inc=sP,
                       dma=True)
                    n += 1
        return n

    tpa_jobs = [(h1_bf, h1T, 4096, 4096), (h2_bf, h2T, 4096, 4096)]
    tps_jobs = [(w1_bf, w1T, 4096, DM), (w2_bf, w2T, 4096, DM),
                (w3_bf, w3T, DM, 4096)]
    tp_loads("a", TPA, sTlA, sTpA, tpa_jobs,
             [(sPrA, 6400), (sPrB, 4096), ws_raw_free])
    tp_loads("sy", TPS, sTlS, sTpS, tps_jobs,
             [(sPrA, 6400), (sPrB, 6144), ws_raw_free])
    n_tps = tp_stores(TPS, sTlS, sTpS, tps_jobs)
    n_tpa = tp_stores(TPA, sTlA, sTpA, tpa_jobs)
    tp_done = [(sTpA, n_tpa * 16), (sTpS, n_tps * 16)]

    n_kt = [20 + 4 * j for j in range(NQB)]
    g_kt = 0
    g_negg = 0

    for grp in range(8):
        for tt in range(NQB):
            bank = mm[2 + tt % 2]
            for c in range(2):
                op("t", lambda e, b=bank, grp=grp, tt=tt, c=c: e.matmul(
                    b[:], WUQ[:, c * 1024 + grp * 128: c * 1024 + grp * 128 + 128],
                    CQT[:, c, tt * 512:(tt + 1) * 512],
                    start=(c == 0), stop=(c == 1)),
                   waits=([(sA, sA.n), (sV, sV.n)] if c == 0 else []) +
                         ([(sT, g_kt)] if (tt == 0 and c == 0) else []),
                   inc=sT if c == 1 else None)
            op("a", lambda e, b=bank, grp=grp, tt=tt: e.activation(
                out=QTG[:, tt * 512:(tt + 1) * 512], in_=b[:],
                func=AF.Identity, bias=BQC[:, grp:grp + 1], scale=1.0),
               waits=[sT], inc=sA)
        for tt in range(8):
            bank = mm[2 + tt % 2]
            for c in range(2):
                op("t", lambda e, b=bank, grp=grp, tt=tt, c=c: e.matmul(
                    b[:], WUKV[:, c * 2048 + grp * 128: c * 2048 + grp * 128 + 128],
                    CKVT[:, c, tt * 512:(tt + 1) * 512],
                    start=(c == 0), stop=(c == 1)),
                   waits=[(sA, sA.n)] if c == 0 else [],
                   inc=sT if c == 1 else None)
            op("a", lambda e, b=bank, grp=grp, tt=tt: e.activation(
                out=KT[:, tt * 512:(tt + 1) * 512], in_=b[:],
                func=AF.Identity, bias=BKC[:, grp:grp + 1], scale=1.0),
               waits=[sT], inc=sA)
        for kc in range(32):
            bank = mm[2 + kc % 2]
            for c in range(2):
                op("t", lambda e, b=bank, grp=grp, kc=kc, c=c: e.matmul(
                    b[:, 0:128], CKVT[:, c, kc * 128:(kc + 1) * 128],
                    WUKV[:, c * 2048 + 1024 + grp * 128:
                         c * 2048 + 1024 + grp * 128 + 128],
                    start=(c == 0), stop=False),
                   waits=[(sV, sV.n), (sA, sA.n)] if c == 0 else [],
                   inc=None)
            op("t", lambda e, b=bank, grp=grp: e.matmul(
                b[:, 0:128], ONRB[:, 0:128],
                BVR[:, grp * 128:(grp + 1) * 128],
                start=False, stop=True), inc=sT)
            op("v", lambda e, b=bank, kc=kc: e.tensor_copy(
                VA[:, kc, 0:130].rearrange("p (h z) -> p h z", h=2)[:, :, 0:64],
                b[:, 0:128].rearrange("p (h d) -> p h d", h=2)),
               waits=[sT], inc=sV)

        for hh in range(2):
            base = 64 * hh
            for tt in range(NQB):
                op("a", lambda e, base=base, tt=tt: e.activation(
                    out=SCR[base:base + 64, 0:512],
                    in_=QTG[base:base + 64, tt * 512:(tt + 1) * 512],
                    func=AF.Square), waits=[(sV, sV.n)], inc=sA)
                op("v", lambda e, base=base: e.tensor_copy(
                    SQB[0][base:base + 64, :], SCR[base:base + 64, 0:512]),
                   waits=[sA], inc=sV)
                op("t", lambda e, base=base: e.matmul(
                    sc[0][0:1, 0:512], ONC[base:base + 64, :],
                    SQB[0][base:base + 64, :], start=True, stop=True),
                   waits=[sV], inc=sT)
                op("v", lambda e, hh=hh, tt=tt: e.reduce_max(
                    STT[0:1, 526 + hh * 16 + tt: 527 + hh * 16 + tt],
                    sc[0][0:1, 0:512], axis=AX.X), waits=[sT], inc=sV)
            for tt in range(8):
                op("a", lambda e, base=base, tt=tt: e.activation(
                    out=SCR[base:base + 64, 0:512],
                    in_=KT[base:base + 64, tt * 512:(tt + 1) * 512],
                    func=AF.Square), waits=[(sV, sV.n)], inc=sA)
                op("v", lambda e, base=base: e.tensor_copy(
                    SQB[0][base:base + 64, :], SCR[base:base + 64, 0:512]),
                   waits=[sA], inc=sV)
                op("t", lambda e, base=base: e.matmul(
                    sc[0][0:1, 0:512], ONC[base:base + 64, :],
                    SQB[0][base:base + 64, :], start=True, stop=True),
                   waits=[sV], inc=sT)
                op("v", lambda e, hh=hh, tt=tt: e.reduce_max(
                    STT[0:1, 534 + hh * 16 + tt: 535 + hh * 16 + tt],
                    sc[0][0:1, 0:512], axis=AX.X), waits=[sT], inc=sV)
            op("v", lambda e, hh=hh: e.reduce_max(
                STT[0:1, 516 + hh:517 + hh], STT[0:1, 526 + hh * 16:526 + hh * 16 + NQB],
                axis=AX.X), inc=sV)
            op("v", lambda e, hh=hh: e.reduce_max(
                STT[0:1, 518 + hh:519 + hh], STT[0:1, 534 + hh * 16:534 + hh * 16 + 8],
                axis=AX.X), inc=sV)
        op("a", lambda e: e.activation(out=STT[0:1, 520:524], in_=STT[0:1, 516:520],
                                       func=AF.Sqrt), waits=[sV], inc=sA)
        op("v", lambda e: e.tensor_mul(STT[0:1, 524:526], STT[0:1, 520:522],
                                       STT[0:1, 522:524]), waits=[sA], inc=sV)
        op("v", lambda e: e.tensor_scalar(
            out=STT[0:1, 512:514], in0=STT[0:1, 524:526], scalar1=-0.125,
            scalar2=None, op0=ALU.mult), inc=sV)
        op("t", lambda e: e.matmul(bcb[:, 0:2], ONRF[:, 0:128],
                                   STT[0:1, 512:514], start=True, stop=True),
           waits=[sV, (sA, g_negg)], inc=sT)
        op("a", lambda e, grp=grp: e.copy(NEGG[:, grp * 2:(grp + 1) * 2],
                                          bcb[:, 0:2]), waits=[sT], inc=sA)
        g_negg = sA.n

        for j in range(NQB):
            op("t", lambda e, j=j: e.matmul(
                bcb[:], ONRH[:, 0:128], QIX[:, j * 512:(j + 1) * 512],
                start=True, stop=True), waits=[sA, sV], inc=sT)
            qidx_done = sT.n
            for kt in range(n_kt[j]):
                msl = ESL[:, 4 + (kt % 2), 0:512]
                op("v", lambda e, msl=msl, kt=kt: e.tensor_scalar(
                    out=msl, in0=bcb[:], scalar1=KIX[:, kt:kt + 1],
                    scalar2=None, op0=ALU.is_ge),
                   waits=[(sT, qidx_done)], inc=sV)
                for sub in range(2):
                    base = 64 * sub
                    head = 2 * grp + sub
                    esl = ESL[:, sub, 0:512]
                    op("t", lambda e, base=base, j=j, kt=kt, sub=sub:
                       e.matmul(sc[sub][:],
                                KT[base:base + 64, kt * 128:(kt + 1) * 128],
                                QTG[base:base + 64, j * 512:(j + 1) * 512],
                                start=True, stop=True,
                                tile_position=(base, 0)),
                       waits=[(sA, sA.n)], inc=sT)
                    op("a", lambda e, sub=sub, esl=esl, head=head:
                       e.activation(out=esl, in_=sc[sub][:], func=AF.Exp,
                                    scale=0.125, bias=NEGG[:, head:head + 1]),
                       waits=[sT], inc=sA)
                    op("v", lambda e, esl=esl, msl=msl: e.tensor_mul(
                        esl, esl, msl), waits=[sA], inc=sV)
                    op("t", lambda e, sub=sub, kt=kt, j=j, esl=esl,
                       nkt=n_kt[j]: e.matmul(
                           mm[sub][0:65, :],
                           VA[:, kt, sub * 65: sub * 65 + 65],
                           esl, start=(kt == 0), stop=(kt == nkt - 1)),
                       waits=[sV] + ([(sA, sA.n)] if kt == 0 else []),
                       inc=sT)
            for hh in range(2):
                head = 2 * grp + hh
                op("a", lambda e, hh=hh: e.copy(SCR[0:65, 0:512],
                                                mm[hh][0:65, :]),
                   waits=[sT, (sV, sV.n)], inc=sA)
                op("v", lambda e: e.reciprocal(SCR[64:65, 512:1024],
                                               SCR[64:65, 0:512]),
                   waits=[sA], inc=sV)
                op("t", lambda e: e.matmul(bcb[0:64, :], ONFF[64:65, 0:64],
                                           SCR[64:65, 512:1024],
                                           start=True, stop=True,
                                           tile_position=(64, 0)),
                   waits=[sV], inc=sT)
                dch, dbase = head // 2, 64 * (head % 2)
                if dbase == 0:
                    op("v", lambda e, dch=dch, j=j: e.tensor_mul(
                        ATT[0:64, dch, j * 512:(j + 1) * 512],
                        SCR[0:64, 0:512], bcb[0:64, :]), waits=[sT], inc=sV)
                else:
                    oslot = kb._oddslot
                    osem = sGAtt[oslot]
                    op("v", lambda e, oslot=oslot: e.tensor_mul(
                        ESL[0:64, 2 + oslot, 0:512], SCR[0:64, 0:512],
                        bcb[0:64, :]), waits=[sT, osem.at()], inc=sV)
                    op("g", lambda e, oslot=oslot, dch=dch, j=j: e.dma_start(
                        out=ATT[64:128, dch, j * 512:(j + 1) * 512],
                        in_=ESL[0:64, 2 + oslot, 0:512]),
                       waits=[(sV, sV.n)], inc=osem, dma=True)
                    kb._oddslot = 1 - oslot
        g_kt = sT.n

    attn_done = [(sT, sT.n), (sA, sA.n), (sV, sV.n)]
    wo_guard = 0
    for oh in range(2):
        swo = sWo[oh]
        for d in range(8):
            woff = d * 512 if d < 4 else 4096 + (d - 4) * 512
            op("sy", lambda e, oh=oh, d=d, woff=woff: e.dma_start(
                out=WS[:, woff:woff + 512],
                in_=woT_bf.ap()[oh * 512:(oh + 1) * 512,
                                d * 128:(d + 1) * 128],
                transpose=True),
               waits=[(sPrA, prA_wo_done), ws_raw_free] + attn_done
               + ([(sT, wo_guard)] if oh == 1 else []),
               inc=swo, dma=True)
        for oc in range(4):
            o = oh * 4 + oc
            for tt in range(NQB):
                bank = mm[tt % 2]
                for d in range(8):
                    woff = d * 512 if d < 4 else 4096 + (d - 4) * 512
                    op("t", lambda e, b=bank, d=d, oc=oc, tt=tt, woff=woff:
                       e.matmul(
                        b[:], WS[:, woff + oc * 128: woff + oc * 128 + 128],
                        ATT[:, d, tt * 512:(tt + 1) * 512],
                        start=(d == 0), stop=(d == 7)),
                       waits=([(swo, swo.n), (sV, sV.n), (sA, sA.n),
                               sGAtt[0].at(), sGAtt[1].at()]
                              if d == 0 else []),
                       inc=sT if d == 7 else None)
                op("a", lambda e, b=bank, o=o, tt=tt: e.copy(
                    BO[:, o, tt * 512:(tt + 1) * 512], b[:]),
                   waits=[sT], inc=sA)
        if oh == 0:
            wo_guard = sT.n

    def post_ln_residual(src_bf):
        for tt in range(NQB):
            for c in range(NCH):
                op("a", lambda e, c=c, tt=tt, src=src_bf: e.activation(
                    out=SQB[c % 2], in_=src[:, c, tt * 512:(tt + 1) * 512],
                    func=AF.Square), waits=[(sT, sT.n), (sV, sV.n)], inc=sA)
                op("t", lambda e, c=c, tt=tt, src=src_bf: e.matmul(
                    sc[0][0:1, :], ONC[:], src[:, c, tt * 512:(tt + 1) * 512],
                    start=(c == 0), stop=(c == NCH - 1)),
                   waits=[(sV, sV.n)] if c == 0 else [], inc=sT)
                op("t", lambda e, c=c: e.matmul(
                    sc[1][0:1, :], ONC[:], SQB[c % 2],
                    start=(c == 0), stop=(c == NCH - 1)),
                   waits=[sA], inc=sT)
            op("a", lambda e: e.activation(out=SCR[0:1, 0:512],
                                           in_=sc[0][0:1, :], func=AF.Copy,
                                           scale=INV_DM), waits=[sT], inc=sA)
            op("a", lambda e: e.activation(out=SCR[0:1, 512:1024],
                                           in_=SCR[0:1, 0:512],
                                           func=AF.Square),
               waits=[sA.at()], inc=sA)
            op("v", lambda e: e.tensor_scalar(
                out=SCR[0:1, 1024:1536], in0=sc[1][0:1, :], scalar1=INV_DM,
                scalar2=LN_EPS, op0=ALU.mult, op1=ALU.add),
               waits=[sA], inc=sV)
            op("v", lambda e: e.tensor_sub(SCR[0:1, 1024:1536],
                                           SCR[0:1, 1024:1536],
                                           SCR[0:1, 512:1024]), inc=sV)
            op("v", lambda e: e.reciprocal(SCR[0:1, 1024:1536],
                                           SCR[0:1, 1024:1536]), inc=sV)
            op("a", lambda e: e.activation(out=SCR[0:1, 1024:1536],
                                           in_=SCR[0:1, 1024:1536],
                                           func=AF.Sqrt),
               waits=[sV], inc=sA)
            op("v", lambda e: e.tensor_mul(SCR[0:1, 1536:2048],
                                           SCR[0:1, 0:512],
                                           SCR[0:1, 1024:1536]), waits=[sA],
               inc=sV)
            op("t", lambda e: e.matmul(sc[0][:], ONRF[:, 0:128],
                                       SCR[0:1, 1024:1536], start=True,
                                       stop=True), waits=[sV], inc=sT)
            op("t", lambda e: e.matmul(sc[1][:], ONRF[:, 0:128],
                                       SCR[0:1, 1536:2048], start=True,
                                       stop=True), inc=sT)
            for c in range(NCH):
                op("v", lambda e, c=c, tt=tt, src=src_bf: e.tensor_mul(
                    SCR[:, 3328:3840], src[:, c, tt * 512:(tt + 1) * 512],
                    sc[0][:]), waits=[sT], inc=sV)
                op("v", lambda e: e.tensor_sub(
                    SCR[:, 3328:3840], SCR[:, 3328:3840], sc[1][:]), inc=sV)
                op("v", lambda e, c=c, tt=tt: e.tensor_add(
                    RES[:, c * TOK + tt * 512: c * TOK + (tt + 1) * 512],
                    RES[:, c * TOK + tt * 512: c * TOK + (tt + 1) * 512],
                    SCR[:, 3328:3840]), inc=sV)

    post_ln_residual(BO)

    H_R = A1[:, 0:16384].rearrange("p (c t) -> p c t", c=32)
    G_R = A1[:, 16384:32768].rearrange("p (c t) -> p c t", c=32)
    FSL = [A1[:, 32768 + i * 4096: 32768 + (i + 1) * 4096] for i in range(2)]
    XF = A2[:, 0:4096].rearrange("p (c t) -> p c t", c=8)
    BOT = A1[:, 16384:20480].rearrange("p (c t) -> p c t", c=8)
    XB = SQB[0]

    sFF = [kb.sem(f"sFF{i}") for i in range(2)]

    def ln_rows_from(stat_src_bf, tt, chunks, inv):
        for c in range(chunks):
            op("a", lambda e, c=c: e.activation(
                out=SQB[1], in_=stat_src_bf(c), func=AF.Square),
               waits=[(sT, sT.n), (sV, sV.n)], inc=sA)
            op("t", lambda e, c=c: e.matmul(
                sc[0][0:1, :], ONC[:], stat_src_bf(c),
                start=(c == 0), stop=(c == chunks - 1)),
               waits=[(sV, sV.n)] if c == 0 else [], inc=sT)
            op("t", lambda e, c=c: e.matmul(
                sc[1][0:1, :], ONC[:], SQB[1],
                start=(c == 0), stop=(c == chunks - 1)),
               waits=[sA], inc=sT)
        op("a", lambda e: e.activation(out=SCR[0:1, 0:512],
                                       in_=sc[0][0:1, :], func=AF.Copy,
                                       scale=inv), waits=[sT], inc=sA)
        op("a", lambda e: e.activation(out=SCR[0:1, 512:1024],
                                       in_=SCR[0:1, 0:512], func=AF.Square),
           waits=[sA.at()], inc=sA)
        op("v", lambda e: e.tensor_scalar(
            out=SCR[0:1, 1024:1536], in0=sc[1][0:1, :], scalar1=inv,
            scalar2=LN_EPS, op0=ALU.mult, op1=ALU.add), waits=[sA], inc=sV)
        op("v", lambda e: e.tensor_sub(SCR[0:1, 1024:1536],
                                       SCR[0:1, 1024:1536],
                                       SCR[0:1, 512:1024]), inc=sV)
        op("v", lambda e: e.reciprocal(SCR[0:1, 1024:1536],
                                       SCR[0:1, 1024:1536]), inc=sV)
        op("a", lambda e: e.activation(out=SCR[0:1, 1024:1536],
                                       in_=SCR[0:1, 1024:1536],
                                       func=AF.Sqrt), waits=[sV], inc=sA)
        op("v", lambda e: e.tensor_mul(SCR[0:1, 1536:2048], SCR[0:1, 0:512],
                                       SCR[0:1, 1024:1536]), waits=[sA],
           inc=sV)
        op("t", lambda e: e.matmul(sc[0][:], ONRF[:, 0:128],
                                   SCR[0:1, 1024:1536], start=True,
                                   stop=True), waits=[sV], inc=sT)
        op("t", lambda e: e.matmul(sc[1][:], ONRF[:, 0:128],
                                   SCR[0:1, 1536:2048], start=True,
                                   stop=True), inc=sT)

    def ffn_mm_phase(wsrc, n_out, n_k, scol, evac, tt, act_chunks,
                     start_wait=()):
        n_ob = n_out // 256
        n_kh = max(1, n_k // 16)
        kh_sz = min(n_k, 16)
        unit = 0
        for ob in range(n_ob):
            for kh in range(n_kh):
                sl = unit % 2
                sem = sFF[sl]
                for k2 in range(kh_sz):
                    k = kh * kh_sz + k2
                    op("sy", lambda e, ob=ob, k=k, k2=k2, sl=sl, wsrc=wsrc:
                       e.dma_start(
                           out=FSL[sl][:, k2 * 256:(k2 + 1) * 256],
                           in_=wsrc.ap()[k * 128:(k + 1) * 128,
                                         ob * 256:(ob + 1) * 256]),
                       waits=list(start_wait)
                       + [(sT, kb._ffn_guard[sl])] if k2 == 0 else [],
                       inc=sem, dma=True)
                ldw = sem.at()
                for oc in range(2):
                    bank = mm[(2 * (unit % 2) + oc) % 4] if False else mm[
                        (ob % 2) * 2 + oc]
                    for k2 in range(kh_sz):
                        k = kh * kh_sz + k2
                        op("t", lambda e, b=bank, k2=k2, oc=oc, sl=sl, tt=tt,
                           k=k, ac=act_chunks: e.matmul(
                               b[:], FSL[sl][:, k2 * 256 + oc * 128:
                                             k2 * 256 + oc * 128 + 128],
                               ac(k),
                               start=(k == 0), stop=(k == n_k - 1)),
                           waits=([ldw] if k2 == 0 else [])
                           + ([(sA, sA.n), (sV, sV.n)]
                              if (k == 0 and k2 == 0) else []),
                           inc=sT)
                    if kh == n_kh - 1:
                        evac(bank, ob * 2 + oc)
                kb._ffn_guard[sl] = sT.n
                unit += 1

    kb._ffn_guard = [0, 0]

    kb._ffn_guard = [sT.n, sT.n]
    for tt in range(NQB):
        def res_cast(c, tt=tt):
            return RES[:, c * TOK + tt * 512: c * TOK + (tt + 1) * 512]
        for c in range(NCH):
            op("v", lambda e, c=c, rc=res_cast: e.tensor_copy(XB, rc(c)),
               waits=[(sT, sT.n)], inc=sV)
            op("t", lambda e, c=c: e.matmul(
                sc[0][0:1, :], ONC[:], XB, start=(c == 0), stop=(c == NCH - 1)),
               waits=[sV], inc=sT)
            op("a", lambda e, c=c: e.activation(out=SQB[1], in_=XB,
                                                func=AF.Square),
               waits=[(sV, sV.n)], inc=sA)
            op("t", lambda e, c=c: e.matmul(
                sc[1][0:1, :], ONC[:], SQB[1], start=(c == 0),
                stop=(c == NCH - 1)), waits=[sA], inc=sT)
        op("a", lambda e: e.activation(out=SCR[0:1, 0:512], in_=sc[0][0:1, :],
                                       func=AF.Copy, scale=INV_DM),
           waits=[sT], inc=sA)
        op("a", lambda e: e.activation(out=SCR[0:1, 512:1024],
                                       in_=SCR[0:1, 0:512], func=AF.Square),
           waits=[sA.at()], inc=sA)
        op("v", lambda e: e.tensor_scalar(
            out=SCR[0:1, 1024:1536], in0=sc[1][0:1, :], scalar1=INV_DM,
            scalar2=LN_EPS, op0=ALU.mult, op1=ALU.add), waits=[sA], inc=sV)
        op("v", lambda e: e.tensor_sub(SCR[0:1, 1024:1536],
                                       SCR[0:1, 1024:1536],
                                       SCR[0:1, 512:1024]), inc=sV)
        op("v", lambda e: e.reciprocal(SCR[0:1, 1024:1536],
                                       SCR[0:1, 1024:1536]), inc=sV)
        op("a", lambda e: e.activation(out=SCR[0:1, 1024:1536],
                                       in_=SCR[0:1, 1024:1536], func=AF.Sqrt),
           waits=[sV], inc=sA)
        op("v", lambda e: e.tensor_mul(SCR[0:1, 1536:2048], SCR[0:1, 0:512],
                                       SCR[0:1, 1024:1536]), waits=[sA],
           inc=sV)
        op("t", lambda e: e.matmul(sc[0][:], ONRF[:, 0:128],
                                   SCR[0:1, 1024:1536], start=True,
                                   stop=True), waits=[sV], inc=sT)
        op("t", lambda e: e.matmul(sc[1][:], ONRF[:, 0:128],
                                   SCR[0:1, 1536:2048], start=True,
                                   stop=True), inc=sT)
        for c in range(NCH):
            op("v", lambda e, c=c, rc=res_cast: e.tensor_mul(
                SCR[:, 3328:3840], rc(c), sc[0][:]), waits=[sT], inc=sV)
            op("v", lambda e, c=c: e.tensor_sub(
                SCR[:, 3328:3840], SCR[:, 3328:3840], sc[1][:]), inc=sV)
            op("v", lambda e, c=c: e.tensor_copy(XF[:, c, 0:512],
                                                 SCR[:, 3328:3840]), inc=sV)

        def xf_chunk(k):
            return XF[:, k, 0:512]

        def evac_u1(bank, o, tt=tt):
            op("a", lambda e, b=bank, o=o: e.activation(
                out=G_R[:, o, 0:512], in_=b[:], func=AF.Silu,
                scale=FS1[:, o:o + 1]), waits=[sT], inc=sA)
        ffn_mm_phase(w1T, 4096, NCH, FS1, evac_u1, tt, xf_chunk,
                     start_wait=tp_done)
        def evac_u2(bank, o, tt=tt):
            op("v", lambda e, b=bank, o=o: e.scalar_tensor_tensor(
                out=H_R[:, o, 0:512], in0=b[:], scalar=FS2[:, o:o + 1],
                in1=G_R[:, o, 0:512], op0=ALU.mult, op1=ALU.mult),
               waits=[sT, (sA, sA.n)], inc=sV)
        ffn_mm_phase(w2T, 4096, NCH, FS2, evac_u2, tt, xf_chunk,
                     start_wait=tp_done)

        def h_chunk(k):
            return H_R[:, k, 0:512]

        def g_chunk(k):
            return G_R[:, k, 0:512]

        def evac_g1(bank, o, tt=tt):
            op("a", lambda e, b=bank, o=o: e.activation(
                out=G_R[:, o, 0:512], in_=b[:], func=AF.Gelu,
                scale=FH1[:, o:o + 1]), waits=[sT, (sV, sV.n)], inc=sA)
        ffn_mm_phase(h1T, 4096, 32, FH1, evac_g1, tt, h_chunk,
                     start_wait=tp_done)
        def evac_g2(bank, o, tt=tt):
            op("a", lambda e, b=bank, o=o: e.activation(
                out=H_R[:, o, 0:512], in_=b[:], func=AF.Gelu,
                scale=FH2[:, o:o + 1]), waits=[sT, (sV, sV.n)], inc=sA)
        ffn_mm_phase(h2T, 4096, 32, FH2, evac_g2, tt, g_chunk,
                     start_wait=tp_done)
        def evac_w3(bank, o, tt=tt):
            op("a", lambda e, b=bank, o=o: e.activation(
                out=BOT[:, o, 0:512], in_=b[:], func=AF.Identity,
                scale=FS3[:, o:o + 1], bias=0.0), waits=[sT, (sV, sV.n)],
               inc=sA)
        ffn_mm_phase(w3T, DM, 32, FS3, evac_w3, tt, h_chunk,
                     start_wait=tp_done)

        def bot_chunk(c):
            return BOT[:, c, 0:512]
        ln_rows_from(bot_chunk, tt, NCH, INV_DM)
        for c in range(NCH):
            op("v", lambda e, c=c, tt=tt: e.tensor_mul(
                SCR[:, 3328:3840], BOT[:, c, 0:512], sc[0][:]),
               waits=[sT], inc=sV)
            op("v", lambda e: e.tensor_sub(SCR[:, 3328:3840],
                                           SCR[:, 3328:3840], sc[1][:]),
               inc=sV)
            op("v", lambda e, c=c, tt=tt: e.tensor_add(
                RES[:, c * TOK + tt * 512: c * TOK + (tt + 1) * 512],
                RES[:, c * TOK + tt * 512: c * TOK + (tt + 1) * 512],
                SCR[:, 3328:3840]), inc=sV)

    Q2 = A1[:, 0:2048]
    KL2 = A1[:, 2048:4352]
    VL2 = A1[:, 4352:6692].rearrange("p (c t) -> p c t", c=18)
    EM1 = A1[:, 6692:9764].rearrange("p (c t) -> p c t", c=6)
    ATT1 = A1[:, 24576:40960].rearrange("p (c t) -> p c t", c=8)
    BO1 = A1[:, 0:16384].rearrange("p (c t) -> p c t", c=8)
    QKVQ = WS[:, 0:2048]
    QKVV = WS[:, 4096:5120]
    CSLD = WS[:, 5120:6144].bitcast(F16)
    xn_dram = nc.dram_tensor("xn_dram", [NCH, 128, TOK], BF16)

    LQB0 = WS[:, 2048:2560]
    LQB1 = WS[:, 2560:3072]
    sXn = kb.sem("sXn")
    sXf = [kb.sem(f"sXf{i}") for i in range(2)]
    sQk = kb.sem("sQk")
    sCs = kb.sem("sCs")
    sBd = kb.sem("sBd")
    sPj = [kb.sem(f"sPj{i}") for i in range(2)]

    for tt in range(NQB):
        def res_cast2(c, tt=tt):
            return RES[:, c * TOK + tt * 512: c * TOK + (tt + 1) * 512]
        for c in range(NCH):
            op("v", lambda e, c=c, rc=res_cast2: e.tensor_copy(XB, rc(c)),
               waits=[(sT, sT.n)] + ([(sXn, sXn.n)] if c == 0 else []),
               inc=sV)
            op("t", lambda e, c=c: e.matmul(
                sc[0][0:1, :], ONC[:], XB, start=(c == 0), stop=(c == NCH - 1)),
               waits=[sV], inc=sT)
            op("a", lambda e, c=c: e.activation(out=SQB[1], in_=XB,
                                                func=AF.Square),
               waits=[(sV, sV.n), (sT, sT.n)], inc=sA)
            op("t", lambda e, c=c: e.matmul(
                sc[1][0:1, :], ONC[:], SQB[1], start=(c == 0),
                stop=(c == NCH - 1)), waits=[sA], inc=sT)
        op("a", lambda e: e.activation(out=SCR[0:1, 0:512], in_=sc[0][0:1, :],
                                       func=AF.Copy, scale=INV_DM),
           waits=[sT], inc=sA)
        op("a", lambda e: e.activation(out=SCR[0:1, 512:1024],
                                       in_=SCR[0:1, 0:512], func=AF.Square),
           waits=[sA.at()], inc=sA)
        op("v", lambda e: e.tensor_scalar(
            out=SCR[0:1, 1024:1536], in0=sc[1][0:1, :], scalar1=INV_DM,
            scalar2=LN_EPS, op0=ALU.mult, op1=ALU.add), waits=[sA], inc=sV)
        op("v", lambda e: e.tensor_sub(SCR[0:1, 1024:1536],
                                       SCR[0:1, 1024:1536],
                                       SCR[0:1, 512:1024]), inc=sV)
        op("v", lambda e: e.reciprocal(SCR[0:1, 1024:1536],
                                       SCR[0:1, 1024:1536]), inc=sV)
        op("a", lambda e: e.activation(out=SCR[0:1, 1024:1536],
                                       in_=SCR[0:1, 1024:1536], func=AF.Sqrt),
           waits=[sV], inc=sA)
        op("v", lambda e: e.tensor_mul(SCR[0:1, 1536:2048], SCR[0:1, 0:512],
                                       SCR[0:1, 1024:1536]), waits=[sA],
           inc=sV)
        op("t", lambda e: e.matmul(sc[0][:], ONRF[:, 0:128],
                                   SCR[0:1, 1024:1536], start=True,
                                   stop=True), waits=[sV], inc=sT)
        op("t", lambda e: e.matmul(sc[1][:], ONRF[:, 0:128],
                                   SCR[0:1, 1536:2048], start=True,
                                   stop=True), inc=sT)
        for c in range(NCH):
            op("v", lambda e, c=c, rc=res_cast2: e.tensor_mul(
                SCR[:, 3328:3840], rc(c), sc[0][:]), waits=[sT], inc=sV)
            op("v", lambda e, c=c: e.tensor_sub(
                SCR[:, 3328:3840], SCR[:, 3328:3840], sc[1][:]), inc=sV)
            op("v", lambda e, c=c: e.tensor_copy(XF[:, c, 0:512],
                                                 SCR[:, 3328:3840]), inc=sV)
            op("sy", lambda e, c=c, tt=tt: e.dma_start(
                out=xn_dram.ap()[c, :, tt * 512:(tt + 1) * 512],
                in_=XF[:, c, 0:512]), waits=[(sV, sV.n)], inc=sXn, dma=True)
    xn_all = sXn.at()

    n_kt1 = 6
    kb._cs_guard = 0
    g_xf1 = [0]
    g_qkv = 0
    for grp in range(8):
        for part, (dst, col0) in enumerate(
                ((QKVQ, grp * 128), (QKVQ, 1024 + grp * 128),
                 (QKVV, 2048 + grp * 128))):
            off = 0 if part < 2 else 0
            for d in range(NCH):
                dsl = dst[:, (part % 2) * 1024 + d * 128:(part % 2) * 1024 + d * 128 + 128] if part < 2 else dst[:, d * 128:(d + 1) * 128]
                op("sy", lambda e, dsl=dsl, col0=col0, d=d: e.dma_start(
                    out=dsl,
                    in_=qkv_bf.ap()[col0:col0 + 128, d * 128:(d + 1) * 128],
                    transpose=True),
                   waits=[(sPrB, prB_qkv_done), (sT, g_qkv),
                          ], inc=sQk, dma=True)
        qkv_ld = sQk.at()

        for tt in range(NQB):
            xsl = sXf[tt % 2]
            for c in range(NCH):
                op("sy", lambda e, c=c, tt=tt: e.dma_start(
                    out=XF[:, c, 0:512],
                    in_=xn_dram.ap()[c, :, tt * 512:(tt + 1) * 512]),
                   waits=[xn_all, (sT, g_xf1[0])], inc=xsl, dma=True)
            xf_ld = xsl.at()
            op("sy", lambda e, tt=tt: e.dma_start(
                out=CSLD[:, 0:512], in_=cosb[:, tt * 512:(tt + 1) * 512]),
               waits=[(sV, kb._cs_guard)], inc=sCs, dma=True)
            op("sy", lambda e, tt=tt: e.dma_start(
                out=CSLD[:, 512:1024], in_=sinb[:, tt * 512:(tt + 1) * 512]),
               inc=sCs, dma=True)
            cs_ld = sCs.at()
            for qk in range(2):
                bank = mm[2 + qk]
                for d in range(NCH):
                    op("t", lambda e, b=bank, qk=qk, d=d, tt=tt: e.matmul(
                        b[:], QKVQ[:, qk * 1024 + d * 128:
                                   qk * 1024 + d * 128 + 128],
                        XF[:, d, 0:512], start=(d == 0), stop=(d == NCH - 1)),
                       waits=[qkv_ld, xf_ld, (sA, sA.n), (sV, sV.n)]
                       if d == 0 else [], inc=sT)
                dst0 = Q2 if qk == 0 else KL2
                colw = tt * 512 if qk == 0 else 256 + tt * 512
                SCRq = SCR[:, 2048:2560]
                SCRt = SCR[:, 2560:3072]
                SCRu = SCR[:, 3072:3584]
                op("a", lambda e, b=bank: e.copy(SCRq, b[:]),
                   waits=[sT, (sPrA, sPrA.n)], inc=sA)
                op("a", lambda e: e.activation(out=SCRt, in_=SCRq,
                                               func=AF.Square),
                   waits=[sA.at()], inc=sA)
                op("v", lambda e: e.tensor_copy(LQB0, SCRq), waits=[sA],
                   inc=sV)
                op("v", lambda e: e.tensor_copy(LQB1, SCRt), inc=sV)
                for hh in range(2):
                    b0 = 64 * hh
                    onc_s = ONC[0:64, :] if b0 == 0 else ONC[64:128, :]
                    op("t", lambda e, b0=b0, onc_s=onc_s: e.matmul(
                        sc[0][0:1, 0:512], onc_s, LQB0[b0:b0 + 64, :],
                        start=True, stop=True), waits=[(sV, sV.n)], inc=sT)
                    op("t", lambda e, b0=b0, onc_s=onc_s: e.matmul(
                        sc[1][0:1, 0:512], onc_s, LQB1[b0:b0 + 64, :],
                        start=True, stop=True), inc=sT)
                    op("a", lambda e: e.activation(
                        out=SCR[0:1, 0:512], in_=sc[0][0:1, :], func=AF.Copy,
                        scale=1.0 / 64), waits=[sT], inc=sA)
                    op("a", lambda e: e.activation(
                        out=SCR[0:1, 512:1024], in_=SCR[0:1, 0:512],
                        func=AF.Square), waits=[sA.at()], inc=sA)
                    op("v", lambda e: e.tensor_scalar(
                        out=SCR[0:1, 1024:1536], in0=sc[1][0:1, :],
                        scalar1=1.0 / 64, scalar2=LN_EPS, op0=ALU.mult,
                        op1=ALU.add), waits=[sA], inc=sV)
                    op("v", lambda e: e.tensor_sub(
                        SCR[0:1, 1024:1536], SCR[0:1, 1024:1536],
                        SCR[0:1, 512:1024]), inc=sV)
                    op("v", lambda e: e.reciprocal(
                        SCR[0:1, 1024:1536], SCR[0:1, 1024:1536]), inc=sV)
                    op("a", lambda e: e.activation(
                        out=SCR[0:1, 1024:1536], in_=SCR[0:1, 1024:1536],
                        func=AF.Sqrt), waits=[sV], inc=sA)
                    op("v", lambda e: e.tensor_mul(
                        SCR[0:1, 1536:2048], SCR[0:1, 0:512],
                        SCR[0:1, 1024:1536]), waits=[sA], inc=sV)
                    op("t", lambda e, b0=b0: e.matmul(
                        bcb[b0:b0 + 64, :][0:64, :] if b0 else bcb[0:64, :],
                        ONRF[:, 0:64], SCR[0:1, 1024:1536],
                        start=True, stop=True, tile_position=(0, b0)),
                       waits=[sV, (sA, sA.n), (sV, sV.n)], inc=sT)
                    op("t", lambda e, b0=b0: e.matmul(
                        mm[0][b0:b0 + 64, :][0:64, :] if b0 else mm[0][0:64, :],
                        ONRF[:, 0:64], SCR[0:1, 1536:2048],
                        start=True, stop=True, tile_position=(0, b0)),
                       inc=sT)
                gcol = QNWB[:, 2 * qk:2 * qk + 1]
                bcol = QNWB[:, 2 * qk + 1:2 * qk + 2]
                op("v", lambda e: e.tensor_mul(SCRt, SCRq, bcb[:]),
                   waits=[sT], inc=sV)
                op("v", lambda e: e.tensor_sub(SCRt, SCRt, mm[0][:]), inc=sV)
                op("a", lambda e, gcol=gcol, bcol=bcol: e.activation(
                    out=SCRq, in_=SCRt, func=AF.Identity, scale=gcol,
                    bias=bcol), waits=[sV], inc=sA)
                op("v", lambda e: e.tensor_copy(LQB1, SCRq), waits=[sA],
                   inc=sV)
                op("t", lambda e: e.matmul(
                    mm[0][:], ROTT[:], LQB1, start=True, stop=True),
                   waits=[sV, (sA, sA.n), (sV, sV.n)], inc=sT)
                op("v", lambda e: e.tensor_mul(
                    SCRt, SCRq, CSLD[:, 0:512]),
                   waits=[(sCs, cs_ld[1])], inc=sV)
                op("v", lambda e: e.tensor_mul(SCRu, mm[0][:],
                                               CSLD[:, 512:1024]),
                   waits=[sT], inc=sV)
                op("v", lambda e, dst0=dst0, colw=colw: e.tensor_add(
                    dst0[:, colw:colw + 512], SCRt, SCRu),
                   waits=[(sT, g_qkv)], inc=sV)
            for sub in range(4):
                bank = mm[1]
                for d in range(NCH):
                    op("t", lambda e, b=bank, sub=sub, d=d: e.matmul(
                        b[:, 0:128], XF[:, d, sub * 128:(sub + 1) * 128],
                        QKVV[:, d * 128:(d + 1) * 128],
                        start=(d == 0), stop=(d == NCH - 1)),
                       waits=[(sV, sV.n)] if d == 0 else [], inc=sT)
                r = 2 + tt * 4 + sub
                op("v", lambda e, b=bank, r=r: e.tensor_copy(
                    VL2[:, r, 0:130].rearrange("p (h z) -> p h z", h=2)[:, :, 0:64],
                    b[:, 0:128].rearrange("p (h d) -> p h d", h=2)),
                   waits=[sT], inc=sV)
            g_xf1[0] = sT.n
            kb._cs_guard = sV.n
        for r in range(18):
            op("v", lambda e, r=r: e.memset(
                VL2[:, r, 0:130].rearrange("p (h z) -> p h z", h=2)[:, :, 64:65],
                1.0), inc=sV)
        op("g", lambda e, grp=grp: e.dma_start(
            out=bdk_in[grp].ap(), in_=KL2[:, 2048:2304]),
           waits=[(sV, sV.n)], inc=sBd, dma=True)
        op("g", lambda e, grp=grp: e.dma_start(
            out=bdv_in[grp].ap()[0:128, :], in_=VL2[:, 16, 0:130]),
           waits=[sBd.at()], inc=sBd, dma=True)
        op("g", lambda e, grp=grp: e.dma_start(
            out=bdv_in[grp].ap()[128:256, :], in_=VL2[:, 17, 0:130]),
           waits=[sBd.at()], inc=sBd, dma=True)
        op("g", lambda e, grp=grp: e.collective_compute(
            "AllGather", ALU.bypass,
            replica_groups=[[0, 1], [2, 3], [4, 5], [6, 7]],
            ins=[bdk_in[grp].ap().opt()], outs=[bdk_out[grp].ap().opt()]),
           waits=[sBd.at()], inc=sG)
        op("g", lambda e, grp=grp: e.collective_compute(
            "AllGather", ALU.bypass,
            replica_groups=[[0, 1], [2, 3], [4, 5], [6, 7]],
            ins=[bdv_in[grp].ap().opt()], outs=[bdv_out[grp].ap().opt()]),
           inc=sG)
        op("g", lambda e, grp=grp: e.dma_start(
            out=KL2[:, 0:256], in_=bdk_out[grp].ap()[0:128, :]),
           waits=[sG.at()], inc=sBd, dma=True)
        op("g", lambda e, grp=grp: e.dma_start(
            out=VL2[:, 0, 0:130], in_=bdv_out[grp].ap()[0:128, :]),
           waits=[sBd.at()], inc=sBd, dma=True)
        op("g", lambda e, grp=grp: e.dma_start(
            out=VL2[:, 1, 0:130], in_=bdv_out[grp].ap()[128:256, :]),
           waits=[sBd.at()], inc=sBd, dma=True)
        bd_done = sBd.at()

        for hh in range(2):
            b0 = 64 * hh
            cnt = 0
            for col0, w in [(0, 512), (512, 512), (1024, 512), (1536, 512)]:
                op("a", lambda e, b0=b0, col0=col0, w=w: e.activation(
                    out=SCR[b0:b0 + 64, 0:512][0:64, 0:w],
                    in_=Q2[b0:b0 + 64, col0:col0 + w], func=AF.Square),
                   waits=[(sV, sV.n)], inc=sA)
                op("v", lambda e, b0=b0, w=w: e.tensor_copy(
                    SQB[0][b0:b0 + 64, 0:w], SCR[b0:b0 + 64, 0:w]),
                   waits=[sA], inc=sV)
                op("t", lambda e, b0=b0, w=w: e.matmul(
                    sc[0][0:1, 0:w], ONC[b0:b0 + 64, :]
                    if b0 == 0 else ONC[64:128, :],
                    SQB[0][b0:b0 + 64, 0:w], start=True, stop=True),
                   waits=[sV], inc=sT)
                op("v", lambda e, hh=hh, cnt=cnt, w=w: e.reduce_max(
                    STT[0:1, 526 + hh * 16 + cnt:527 + hh * 16 + cnt],
                    sc[0][0:1, 0:w], axis=AX.X), waits=[sT], inc=sV)
                cnt += 1
            for col0, w in [(0, 512), (512, 512), (1024, 512), (1536, 512),
                            (2048, 256)]:
                op("a", lambda e, b0=b0, col0=col0, w=w: e.activation(
                    out=SCR[b0:b0 + 64, 0:512][0:64, 0:w],
                    in_=KL2[b0:b0 + 64, col0:col0 + w], func=AF.Square),
                   waits=[(sV, sV.n), bd_done], inc=sA)
                op("v", lambda e, b0=b0, w=w: e.tensor_copy(
                    SQB[0][b0:b0 + 64, 0:w], SCR[b0:b0 + 64, 0:w]),
                   waits=[sA], inc=sV)
                op("t", lambda e, b0=b0, w=w: e.matmul(
                    sc[0][0:1, 0:w], ONC[b0:b0 + 64, :]
                    if b0 == 0 else ONC[64:128, :],
                    SQB[0][b0:b0 + 64, 0:w], start=True, stop=True),
                   waits=[sV], inc=sT)
                op("v", lambda e, hh=hh, cnt=cnt, w=w: e.reduce_max(
                    STT[0:1, 526 + hh * 16 + cnt:527 + hh * 16 + cnt],
                    sc[0][0:1, 0:w], axis=AX.X), waits=[sT], inc=sV)
                cnt += 1
            op("v", lambda e, hh=hh: e.reduce_max(
                STT[0:1, 516 + hh:517 + hh],
                STT[0:1, 526 + hh * 16:526 + hh * 16 + 4], axis=AX.X), inc=sV)
            op("v", lambda e, hh=hh: e.reduce_max(
                STT[0:1, 518 + hh:519 + hh],
                STT[0:1, 526 + hh * 16 + 4:526 + hh * 16 + 9], axis=AX.X),
               inc=sV)
        op("a", lambda e: e.activation(out=STT[0:1, 520:524],
                                       in_=STT[0:1, 516:520], func=AF.Sqrt),
           waits=[sV], inc=sA)
        op("v", lambda e: e.tensor_mul(STT[0:1, 524:526], STT[0:1, 520:522],
                                       STT[0:1, 522:524]), waits=[sA], inc=sV)
        op("v", lambda e: e.tensor_scalar(
            out=STT[0:1, 512:514], in0=STT[0:1, 524:526], scalar1=-0.125,
            scalar2=None, op0=ALU.mult), inc=sV)
        op("v", lambda e, grp=grp: e.tensor_add(
            STT[0:1, 514:516], STT[0:1, 512:514], SNK[:, 2 * grp:2 * grp + 2]),
           inc=sV)
        op("a", lambda e: e.activation(out=STT[0:1, 514:516],
                                       in_=STT[0:1, 514:516], func=AF.Exp),
           waits=[sV], inc=sA)
        op("t", lambda e: e.matmul(bcb[:, 0:2], ONRF[:, 0:128],
                                   STT[0:1, 512:514], start=True, stop=True),
           waits=[sV, sA, (sA, g_negg), (sV, sV.n)], inc=sT)
        op("t", lambda e: e.matmul(bcb[:, 2:4], ONRF[:, 0:128],
                                   STT[0:1, 514:516], start=True, stop=True),
           inc=sT)
        op("a", lambda e, grp=grp: e.copy(NEGG[:, 0:2], bcb[:, 0:2]),
           waits=[sT], inc=sA)
        op("a", lambda e: e.copy(ESNK[:], bcb[:, 2:4]), waits=[sA.at()],
           inc=sA)
        g_negg = sA.n

        for j in range(NQB):
            op("t", lambda e, j=j: e.matmul(
                bcb[:], ONRH[:, 0:128], QIX[:, j * 512:(j + 1) * 512],
                start=True, stop=True), waits=[sA, sV], inc=sT)
            qidx_done = sT.n
            for kt in range(n_kt1):
                vchunk = 4 * j + kt
                msl = EM1[:, 4 + (kt % 2), 0:512]
                op("v", lambda e, vchunk=vchunk: e.tensor_scalar(
                    out=SCR[:, 2048:2560], in0=bcb[:],
                    scalar1=KIX1[:, vchunk:vchunk + 1], scalar2=None,
                    op0=ALU.is_ge), waits=[(sT, qidx_done)], inc=sV)
                op("v", lambda e, vchunk=vchunk: e.tensor_scalar(
                    out=SCR[:, 2560:3072], in0=bcb[:],
                    scalar1=KIX1[:, 18 + vchunk:19 + vchunk], scalar2=None,
                    op0=ALU.is_le), inc=sV)
                op("v", lambda e, msl=msl: e.tensor_mul(
                    msl, SCR[:, 2048:2560], SCR[:, 2560:3072]), inc=sV)
                for sub in range(2):
                    b0 = 64 * sub
                    esl = EM1[:, sub, 0:512]
                    op("t", lambda e, b0=b0, j=j, kt=kt, sub=sub: e.matmul(
                        sc[sub][:],
                        KL2[b0:b0 + 64, 512 * j + 128 * kt:
                            512 * j + 128 * kt + 128],
                        Q2[b0:b0 + 64, j * 512:(j + 1) * 512],
                        start=True, stop=True, tile_position=(b0, 0)),
                       waits=[(sA, sA.n)], inc=sT)
                    op("a", lambda e, sub=sub, esl=esl: e.activation(
                        out=esl, in_=sc[sub][:], func=AF.Exp, scale=0.125,
                        bias=NEGG[:, sub:sub + 1]), waits=[sT], inc=sA)
                    op("v", lambda e, esl=esl, msl=msl: e.tensor_mul(
                        esl, esl, msl), waits=[sA], inc=sV)
                    op("t", lambda e, sub=sub, kt=kt, j=j, esl=esl: e.matmul(
                        mm[sub][0:65, :],
                        VL2[:, 4 * j + kt, sub * 65: sub * 65 + 65],
                        esl, start=(kt == 0), stop=(kt == n_kt1 - 1)),
                       waits=[sV] + ([(sA, sA.n)] if kt == 0 else []),
                       inc=sT)
            for hh in range(2):
                head = 2 * grp + hh
                op("a", lambda e, hh=hh: e.copy(SCR[0:65, 0:512],
                                                mm[hh][0:65, :]),
                   waits=[sT, (sV, sV.n)], inc=sA)
                op("v", lambda e, hh=hh: e.tensor_scalar(
                    out=SCR[64:65, 0:512], in0=SCR[64:65, 0:512],
                    scalar1=ESNK[64:65, hh:hh + 1], scalar2=None,
                    op0=ALU.add), waits=[sA], inc=sV)
                op("v", lambda e: e.reciprocal(SCR[64:65, 512:1024],
                                               SCR[64:65, 0:512]),
                   inc=sV)
                op("t", lambda e: e.matmul(bcb[0:64, :], ONFF[64:65, 0:64],
                                           SCR[64:65, 512:1024],
                                           start=True, stop=True,
                                           tile_position=(64, 0)),
                   waits=[sV], inc=sT)
                dch, dbase = head // 2, 64 * (head % 2)
                if dbase == 0:
                    op("v", lambda e, dch=dch, j=j: e.tensor_mul(
                        ATT1[0:64, dch, j * 512:(j + 1) * 512],
                        SCR[0:64, 0:512], bcb[0:64, :]), waits=[sT], inc=sV)
                else:
                    oslot = kb._oddslot
                    osem = sGAtt[oslot]
                    op("v", lambda e, oslot=oslot: e.tensor_mul(
                        EM1[0:64, 2 + oslot, 0:512], SCR[0:64, 0:512],
                        bcb[0:64, :]), waits=[sT, osem.at()], inc=sV)
                    op("g", lambda e, oslot=oslot, dch=dch, j=j: e.dma_start(
                        out=ATT1[64:128, dch, j * 512:(j + 1) * 512],
                        in_=EM1[0:64, 2 + oslot, 0:512]),
                       waits=[(sV, sV.n)], inc=osem, dma=True)
                    kb._oddslot = 1 - oslot
        g_qkv = sT.n
        g_kv2 = sT.n

    attn1_done = [(sT, sT.n), (sA, sA.n), (sV, sV.n),
                  sGAtt[0].at(), sGAtt[1].at()]
    pj_guard = 0
    for oh in range(2):
        spj = sPj[oh]
        for d in range(8):
            woff = d * 512 if d < 4 else 4096 + (d - 4) * 512
            op("sy", lambda e, oh=oh, d=d, woff=woff: e.dma_start(
                out=WS[:, woff:woff + 512],
                in_=proj_bf.ap()[oh * 512:(oh + 1) * 512,
                                 d * 128:(d + 1) * 128],
                transpose=True),
               waits=[(sPrB, prB_all)] + attn1_done
               + ([(sT, pj_guard)] if oh == 1 else []),
               inc=spj, dma=True)
        for oc in range(4):
            o = oh * 4 + oc
            for tt in range(NQB):
                bank = mm[tt % 2]
                for d in range(8):
                    woff = d * 512 if d < 4 else 4096 + (d - 4) * 512
                    op("t", lambda e, b=bank, d=d, oc=oc, tt=tt, woff=woff:
                       e.matmul(
                           b[:], WS[:, woff + oc * 128: woff + oc * 128 + 128],
                           ATT1[:, d, tt * 512:(tt + 1) * 512],
                           start=(d == 0), stop=(d == 7)),
                       waits=([(spj, spj.n), (sV, sV.n), (sA, sA.n)]
                              + attn1_done if d == 0 else []),
                       inc=sT if d == 7 else None)
                op("a", lambda e, b=bank, o=o, tt=tt: e.activation(
                    out=BO1[:, o, tt * 512:(tt + 1) * 512], in_=b[:],
                    func=AF.Identity, bias=PBC[:, o:o + 1], scale=1.0),
                   waits=[sT], inc=sA)
        if oh == 0:
            pj_guard = sT.n
    post_ln_residual(BO1)

    QMAXI = 1.0 / 32767.0
    XQ = A1[:, 0:16384].rearrange("p (c t) -> p c t", c=8)
    BOTF = A1[:, 0:16384].rearrange("p (c t) -> p c t", c=8)
    TG = A1[:, 16384:28672].rearrange("p (c t) -> p c t", c=24)
    TU = A1[:, 28672:40960].rearrange("p (c t) -> p c t", c=24)
    WQO = A2[:, 0:1024]
    sWq = kb.sem("sWq")
    sWqS = kb.sem("sWqS")
    sSp = kb.sem("sSp")
    sSpH = [kb.sem(f"sSpH{i}") for i in range(2)]
    sRl = [kb.sem(f"sRl{i}") for i in range(2)]
    sMx = kb.sem("sMx")
    sCc = kb.sem("sCc")

    ffn1_begin = [(sT, sT.n), (sA, sA.n), (sV, sV.n)]

    WMX = SCR[:, 1024:1056]
    WSC = SCR[:, 1056:1088]
    WRC = SCR[:, 1088:1120]
    WF = SCR[:, 2048:3072]
    wprep_done = {}

    def lpbq_prep(src, dst, nrow, ncb, eng):
        for i in range(nrow // 128):
            for cb in range(ncb):
                op("sy", lambda e, src=src, i=i, cb=cb: e.dma_start(
                    out=SCR[:, 0:1024],
                    in_=src[i * 128:(i + 1) * 128, cb * 1024:(cb + 1) * 1024]),
                   waits=ffn1_begin + [(kb.self_sems[eng], kb._wp_guard)],
                   inc=sWq, dma=True)
                ldw = sWq.at()
                op(eng, lambda e: e.tensor_copy(WF, SCR[:, 0:1024]),
                   waits=[ldw], inc=None)
                op(eng, lambda e: e.reduce_max(
                    WMX, WF.rearrange("p (g i) -> p g i", i=32),
                    axis=AX.X, apply_absolute_value=True), inc=None)
                op(eng, lambda e: e.tensor_scalar(
                    out=WSC, in0=WMX, scalar1=1.0 / 7.0, scalar2=1e-12,
                    op0=ALU.mult, op1=ALU.max), inc=None)
                op(eng, lambda e: e.reciprocal(WRC, WSC), inc=None)
                op(eng, lambda e: e.tensor_tensor(
                    WF.rearrange("p (g i) -> p g i", i=32), 
                    WF.rearrange("p (g i) -> p g i", i=32),
                    WRC[:, :, None].broadcast_to([128, 32, 32]),
                    op=ALU.mult), inc=None)
                op(eng, lambda e: e.tensor_scalar(
                    out=WF, in0=WF, scalar1=MAGIC, scalar2=MAGIC,
                    op0=ALU.add, op1=ALU.subtract), inc=None)
                op(eng, lambda e: e.tensor_scalar(
                    out=WF, in0=WF, scalar1=7.0, scalar2=-8.0,
                    op0=ALU.min, op1=ALU.max), inc=None)
                op(eng, lambda e: e.tensor_tensor(
                    WQO.rearrange("p (g i) -> p g i", i=32),
                    WF.rearrange("p (g i) -> p g i", i=32),
                    WSC[:, :, None].broadcast_to([128, 32, 32]),
                    op=ALU.mult), waits=[(sWqS, sWqS.n)], inc=None)
                kb._wp_guard = kb.self_sems[eng].n
                op("g", lambda e, dst=dst, i=i, cb=cb: e.dma_start(
                    out=dst.ap()[i * 128:(i + 1) * 128,
                                 cb * 1024:(cb + 1) * 1024],
                    in_=WQO),
                   waits=[(kb.self_sems[eng], kb._wp_guard)], inc=sWqS,
                   dma=True)

    kb._wp_guard = 0
    lpbq_prep(w_gate, wg_bf, 3072, 1, "v")
    wprep_done["g"] = sWqS.at()
    lpbq_prep(w_up, wu_bf, 3072, 1, "v")
    wprep_done["u"] = sWqS.at()
    lpbq_prep(w_down, wd_bf, 1024, 3, "v")
    wprep_done["d"] = sWqS.at()

    MAXC = CW[:, 24:25]
    op("v", lambda e: e.memset(MAXC, 0.0), inc=sV)
    op("v", lambda e: e.memset(STT[0:1, 560:566], 0.0), inc=sV)
    for tt in range(NQB):
        def res_c3(c, tt=tt):
            return RES[:, c * TOK + tt * 512: c * TOK + (tt + 1) * 512]
        for c in range(NCH):
            op("v", lambda e, c=c, rc=res_c3: e.tensor_copy(XB, rc(c)),
               waits=[(sT, sT.n), (sSp, sSp.n)], inc=sV)
            op("t", lambda e, c=c: e.matmul(
                sc[0][0:1, :], ONC[:], XB, start=(c == 0), stop=(c == NCH - 1)),
               waits=[sV], inc=sT)
            op("a", lambda e, c=c: e.activation(out=SQB[1], in_=XB,
                                                func=AF.Square),
               waits=[(sV, sV.n), (sT, sT.n)], inc=sA)
            op("t", lambda e, c=c: e.matmul(
                sc[1][0:1, :], ONC[:], SQB[1], start=(c == 0),
                stop=(c == NCH - 1)), waits=[sA], inc=sT)
        op("a", lambda e: e.activation(out=SCR[0:1, 0:512], in_=sc[0][0:1, :],
                                       func=AF.Copy, scale=INV_DM),
           waits=[sT], inc=sA)
        op("a", lambda e: e.activation(out=SCR[0:1, 512:1024],
                                       in_=SCR[0:1, 0:512], func=AF.Square),
           waits=[sA.at()], inc=sA)
        op("v", lambda e: e.tensor_scalar(
            out=SCR[0:1, 1024:1536], in0=sc[1][0:1, :], scalar1=INV_DM,
            scalar2=LN_EPS, op0=ALU.mult, op1=ALU.add), waits=[sA], inc=sV)
        op("v", lambda e: e.tensor_sub(SCR[0:1, 1024:1536],
                                       SCR[0:1, 1024:1536],
                                       SCR[0:1, 512:1024]), inc=sV)
        op("v", lambda e: e.reciprocal(SCR[0:1, 1024:1536],
                                       SCR[0:1, 1024:1536]), inc=sV)
        op("a", lambda e: e.activation(out=SCR[0:1, 1024:1536],
                                       in_=SCR[0:1, 1024:1536], func=AF.Sqrt),
           waits=[sV], inc=sA)
        op("v", lambda e: e.tensor_mul(SCR[0:1, 1536:2048], SCR[0:1, 0:512],
                                       SCR[0:1, 1024:1536]), waits=[sA],
           inc=sV)
        op("t", lambda e: e.matmul(sc[0][:], ONRF[:, 0:128],
                                   SCR[0:1, 1024:1536], start=True,
                                   stop=True), waits=[sV], inc=sT)
        op("t", lambda e: e.matmul(sc[1][:], ONRF[:, 0:128],
                                   SCR[0:1, 1536:2048], start=True,
                                   stop=True), inc=sT)
        for c in range(NCH):
            op("v", lambda e, c=c, rc=res_c3: e.tensor_mul(
                SCR[:, 3328:3840], rc(c), sc[0][:]), waits=[sT], inc=sV)
            op("v", lambda e, c=c: e.tensor_sub(
                SCR[:, 3328:3840], SCR[:, 3328:3840], sc[1][:]), inc=sV)
            op("v", lambda e, c=c, tt=tt: e.tensor_copy(XQ[:, c, tt * 512:
                                                           (tt + 1) * 512],
                                                        SCR[:, 3328:3840]),
               inc=sV)
            op("v", lambda e: e.reduce_max(CW[:, 25:26], SCR[:, 3328:3840],
                                           axis=AX.X,
                                           apply_absolute_value=True), inc=sV)
            op("v", lambda e: e.tensor_max(MAXC, MAXC, CW[:, 25:26]), inc=sV)

    def finish_max(dst_col, src_col=None):
        op("t", lambda e: e.transpose(bcb[0:1, 0:128].bitcast(F32)
                                      if False else bcb[0:128, 0:128][0:1, 0:128],
                                      MAXC, IDF[:])
           if False else e.transpose(bcb[0:1, 0:128], MAXC, IDF[:]),
           waits=[sV, (sA, sA.n)], inc=sT)
        op("v", lambda e, dc=dst_col: e.reduce_max(
            STT[0:1, 560 + dc:561 + dc], bcb[0:1, 0:128], axis=AX.X),
           waits=[sT], inc=sV)

    finish_max(0)
    op("sy", lambda e: e.dma_start(out=mx_in[0].ap(),
                                   in_=STT[0:1, 560:562]),
       waits=[(sV, sV.n)], inc=sMx, dma=True)
    op("g", lambda e: e.collective_compute(
        "AllReduce", ALU.max, replica_groups=[list(range(8))],
        ins=[mx_in[0].ap().opt()], outs=[mx_out[0].ap().opt()]),
       waits=[sMx.at()], inc=sCc)
    op("sy", lambda e: e.dma_start(out=STT[0:1, 560:562],
                                   in_=mx_out[0].ap()),
       waits=[(sCc, sCc.n)], inc=sMx, dma=True)
    mx1 = sMx.at()

    def scale_cols(src_off, inv_qmax, sx_col, rx_col, extra_wait=()):
        op("v", lambda e, so=src_off, iq=inv_qmax: e.tensor_scalar(
            out=STT[0:1, 564:565], in0=STT[0:1, 560 + so:561 + so],
            scalar1=iq, scalar2=1e-12, op0=ALU.mult, op1=ALU.max),
           waits=list(extra_wait) + [(sT, sT.n)], inc=sV)
        op("v", lambda e: e.reciprocal(STT[0:1, 565:566], STT[0:1, 564:565]),
           inc=sV)
        op("t", lambda e: e.matmul(bcb[:, 0:2], ONRF[:, 0:128],
                                   STT[0:1, 564:566], start=True, stop=True),
           waits=[sV, (sA, sA.n)], inc=sT)
        op("a", lambda e, sx=sx_col, rx=rx_col: e.copy(sx, bcb[:, 0:1]),
           waits=[sT], inc=sA)
        op("a", lambda e, sx=sx_col, rx=rx_col: e.copy(rx, bcb[:, 1:2]),
           waits=[sA.at()], inc=sA)

    SXC, RXC = CW[:, 26:27], CW[:, 27:28]
    SGC, RGC = CW[:, 28:29], CW[:, 29:30]
    SUC, RUC = CW[:, 30:31], CW[:, 31:32]
    S2C, R2C = CW[:, 32:33], CW[:, 33:34]
    SOC, ROC = CW[:, 34:35], CW[:, 35:36]
    scale_cols(0, QMAXI, SXC, RXC, extra_wait=[mx1])
    for tt in range(NQB):
        for c in range(NCH):
            sl = XQ[:, c, tt * 512:(tt + 1) * 512]
            op("v", lambda e, sl=sl: e.tensor_scalar(
                out=SCR[:, 3328:3840], in0=sl, scalar1=RXC, scalar2=MAGIC,
                op0=ALU.mult, op1=ALU.add), waits=[(sA, sA.n)], inc=sV)
            op("v", lambda e: e.tensor_scalar(
                out=SCR[:, 3328:3840], in0=SCR[:, 3328:3840], scalar1=MAGIC,
                scalar2=32767.0, op0=ALU.subtract, op1=ALU.min), inc=sV)
            op("v", lambda e, sl=sl: e.tensor_scalar(
                out=sl, in0=SCR[:, 3328:3840], scalar1=-32768.0, scalar2=SXC,
                op0=ALU.max, op1=ALU.mult), inc=sV)

    if dbg_outs:
        xq_dbg = nc.dram_tensor("xq_dbg", [NCH, 128, TOK], BF16,
                                kind="ExternalOutput")
        for c in range(NCH):
            op("sy", lambda e, c=c: e.dma_start(
                out=xq_dbg.ap()[c], in_=XQ[:, c, :]),
               waits=[(sV, sV.n)], inc=sMx, dma=True)

    def ffn1_mm(wsrc, spill, wready, track_max):
        if track_max:
            op("v", lambda e: e.memset(MAXC, 0.0), inc=sV)
        unit = 0
        for ob in range(12):
            for tt in range(NQB):
                sl = unit % 2
                if tt == 0:
                    for k in range(NCH):
                        woff = (k % 4) * 512 if k < 4 else 4096 + (k - 8 % 4) * 512
                        woff = (k * 512) if k < 4 else 4096 + (k - 4) * 512
                        op("sy", lambda e, ob=ob, k=k, woff=woff, wsrc=wsrc:
                           e.dma_start(
                               out=WS[:, woff:woff + 256],
                               in_=wsrc.ap()[ob * 256:(ob + 1) * 256,
                                             k * 128:(k + 1) * 128],
                               transpose=True),
                           waits=([wready] if (ob == 0 and k == 0) else [])
                           + ([(sT, kb._f1_guard)] if k == 0 else []),
                           inc=sWq, dma=True)
                    kb._f1_ld = sWq.at()
                for oc in range(2):
                    bank = mm[(ob % 2) * 2 + oc]
                    for k in range(NCH):
                        woff = (k * 512) if k < 4 else 4096 + (k - 4) * 512
                        op("t", lambda e, b=bank, k=k, oc=oc, tt=tt,
                           woff=woff: e.matmul(
                               b[:], WS[:, woff + oc * 128:
                                        woff + oc * 128 + 128],
                               XQ[:, k, tt * 512:(tt + 1) * 512],
                               start=(k == 0), stop=(k == NCH - 1)),
                           waits=([kb._f1_ld] if k == 0 else [])
                           + ([(sA, sA.n), (sV, sV.n)] if k == 0 else []),
                           inc=sT)
                    o = ob * 2 + oc
                    op("a", lambda e, b=bank, oc=oc: e.copy(SQB[oc], b[:]),
                       waits=[sT], inc=sA)
                    if track_max:
                        op("v", lambda e, oc=oc: e.reduce_max(
                            CW[:, 25:26], SQB[oc], axis=AX.X,
                            apply_absolute_value=True), waits=[sA], inc=sV)
                        op("v", lambda e: e.tensor_max(MAXC, MAXC,
                                                       CW[:, 25:26]), inc=sV)
                    op("v", lambda e, oc=oc: e.tensor_copy(
                        WQO[:, oc * 512:(oc + 1) * 512], SQB[oc]),
                       waits=[sA, (sWqS, sWqS.n),
                              (sSpH[oc], kb._sp_guard[oc])],
                       inc=sV)
                    op("sy", lambda e, o=o, tt=tt, spill=spill, oc=oc:
                       e.dma_start(
                           out=spill.ap()[o, :, tt * 512:(tt + 1) * 512],
                           in_=WQO[:, oc * 512:(oc + 1) * 512]),
                       waits=[(sV, sV.n)], inc=sSpH[oc], dma=True)
                    kb._sp_guard[oc] = sSpH[oc].n
                kb._f1_guard = sT.n
                unit += 1

    kb._f1_guard = 0
    kb._fd_guard = [0, 0]
    kb._sp_guard = [0, 0]
    kb._f1_ld = (sWq, 0)
    ffn1_mm(wg_bf, gbuf, wprep_done["g"], True)
    finish_max(1)
    ffn1_mm(wu_bf, ubuf, wprep_done["u"], True)
    finish_max(2)
    op("sy", lambda e: e.dma_start(out=mx_in[1].ap(),
                                   in_=STT[0:1, 561:563]),
       waits=[(sV, sV.n)], inc=sMx, dma=True)
    op("g", lambda e: e.collective_compute(
        "AllReduce", ALU.max, replica_groups=[list(range(8))],
        ins=[mx_in[1].ap().opt()], outs=[mx_out[1].ap().opt()]),
       waits=[sMx.at()], inc=sCc)
    op("sy", lambda e: e.dma_start(out=STT[0:1, 561:563],
                                   in_=mx_out[1].ap()),
       waits=[(sCc, sCc.n)], inc=sMx, dma=True)
    mx2 = sMx.at()
    scale_cols(1, QMAXI, SGC, RGC, extra_wait=[mx2])
    scale_cols(2, QMAXI, SUC, RUC)

    op("v", lambda e: e.memset(MAXC, 0.0), waits=[(sA, sA.n)], inc=sV)
    g_rel = [0, 0]
    for oc in range(24):
        for tt in range(NQB):
            u = (oc * NQB + tt) % 2
            srel = sRl[u]
            tg = TG[:, u, 0:512] if False else TG[:, u, 0:512]
            op("sy", lambda e, oc=oc, tt=tt, u=u: e.dma_start(
                out=TG[:, u, 0:512],
                in_=gbuf.ap()[oc, :, tt * 512:(tt + 1) * 512]),
               waits=[(sSp, sSp.n), sSpH[0].at(), sSpH[1].at(),
                      (sV, g_rel[u])], inc=srel, dma=True)
            rl = srel.at()
            op("v", lambda e, u=u: e.tensor_scalar(
                out=SCR[:, 2048:2560], in0=TG[:, u, 0:512], scalar1=RGC,
                scalar2=MAGIC, op0=ALU.mult, op1=ALU.add),
               waits=[rl, (sA, sA.n)], inc=sV)
            op("v", lambda e: e.tensor_scalar(
                out=SCR[:, 2048:2560], in0=SCR[:, 2048:2560], scalar1=MAGIC,
                scalar2=32767.0, op0=ALU.subtract, op1=ALU.min), inc=sV)
            op("v", lambda e: e.tensor_scalar(
                out=SCR[:, 2048:2560], in0=SCR[:, 2048:2560],
                scalar1=-32768.0, scalar2=SGC, op0=ALU.max, op1=ALU.mult),
               inc=sV)
            op("a", lambda e: e.activation(out=SCR[:, 2560:3072],
                                           in_=SCR[:, 2048:2560],
                                           func=AF.Sigmoid),
               waits=[(sV, sV.n)], inc=sA)
            op("v", lambda e: e.tensor_scalar(
                out=SCR[:, 2560:3072], in0=SCR[:, 2560:3072], scalar1=65536.0,
                scalar2=MAGIC, op0=ALU.mult, op1=ALU.add), waits=[sA], inc=sV)
            op("v", lambda e: e.tensor_scalar(
                out=SCR[:, 2560:3072], in0=SCR[:, 2560:3072], scalar1=MAGIC,
                scalar2=65535.0, op0=ALU.subtract, op1=ALU.min), inc=sV)
            op("v", lambda e: e.tensor_scalar(
                out=SCR[:, 2560:3072], in0=SCR[:, 2560:3072], scalar1=0.0,
                scalar2=1.0 / 65536.0, op0=ALU.max, op1=ALU.mult), inc=sV)
            op("v", lambda e, u=u: e.tensor_mul(
                TG[:, u, 0:512], SCR[:, 2048:2560], SCR[:, 2560:3072]),
               inc=sV)
            op("v", lambda e, u=u: e.reduce_max(
                CW[:, 25:26], TG[:, u, 0:512], axis=AX.X,
                apply_absolute_value=True), inc=sV)
            op("v", lambda e: e.tensor_max(MAXC, MAXC, CW[:, 25:26]), inc=sV)
            g_rel[u] = sV.n
            op("sy", lambda e, oc=oc, tt=tt, u=u: e.dma_start(
                out=g2buf.ap()[oc, :, tt * 512:(tt + 1) * 512],
                in_=TG[:, u, 0:512]),
               waits=[(sV, sV.n)], inc=sSp, dma=True)
    finish_max(3)
    op("sy", lambda e: e.dma_start(out=mx_in[2].ap(),
                                   in_=STT[0:1, 563:565])
       if False else e.dma_start(out=mx_in[2].ap(), in_=STT[0:1, 563:565]),
       waits=[(sV, sV.n)], inc=sMx, dma=True)
    op("g", lambda e: e.collective_compute(
        "AllReduce", ALU.max, replica_groups=[list(range(8))],
        ins=[mx_in[2].ap().opt()], outs=[mx_out[2].ap().opt()]),
       waits=[sMx.at()], inc=sCc)
    op("sy", lambda e: e.dma_start(out=STT[0:1, 563:565],
                                   in_=mx_out[2].ap()),
       waits=[(sCc, sCc.n)], inc=sMx, dma=True)
    mx3 = sMx.at()
    scale_cols(3, QMAXI, S2C, R2C, extra_wait=[mx3])

    op("v", lambda e: e.memset(MAXC, 0.0), waits=[(sA, sA.n)], inc=sV)
    o_rel = [0, 0]
    for oc in range(24):
        for tt in range(NQB):
            u = (oc * NQB + tt) % 2
            srel = sRl[u]
            op("sy", lambda e, oc=oc, tt=tt, u=u: e.dma_start(
                out=TG[:, u, 0:512],
                in_=g2buf.ap()[oc, :, tt * 512:(tt + 1) * 512]),
               waits=[(sSp, sSp.n), sSpH[0].at(), sSpH[1].at(),
                      (sV, o_rel[u])], inc=srel, dma=True)
            op("sy", lambda e, oc=oc, tt=tt, u=u: e.dma_start(
                out=TU[:, u, 0:512],
                in_=ubuf.ap()[oc, :, tt * 512:(tt + 1) * 512]),
               inc=srel, dma=True)
            rl = srel.at()
            op("v", lambda e, u=u: e.tensor_scalar(
                out=SCR[:, 2048:2560], in0=TG[:, u, 0:512], scalar1=R2C,
                scalar2=MAGIC, op0=ALU.mult, op1=ALU.add),
               waits=[rl, (sA, sA.n)], inc=sV)
            op("v", lambda e: e.tensor_scalar(
                out=SCR[:, 2048:2560], in0=SCR[:, 2048:2560], scalar1=MAGIC,
                scalar2=32767.0, op0=ALU.subtract, op1=ALU.min), inc=sV)
            op("v", lambda e: e.tensor_scalar(
                out=SCR[:, 2048:2560], in0=SCR[:, 2048:2560],
                scalar1=-32768.0, scalar2=S2C, op0=ALU.max, op1=ALU.mult),
               inc=sV)
            op("v", lambda e, u=u: e.tensor_scalar(
                out=SCR[:, 2560:3072], in0=TU[:, u, 0:512], scalar1=RUC,
                scalar2=MAGIC, op0=ALU.mult, op1=ALU.add), inc=sV)
            op("v", lambda e: e.tensor_scalar(
                out=SCR[:, 2560:3072], in0=SCR[:, 2560:3072], scalar1=MAGIC,
                scalar2=32767.0, op0=ALU.subtract, op1=ALU.min), inc=sV)
            op("v", lambda e: e.tensor_scalar(
                out=SCR[:, 2560:3072], in0=SCR[:, 2560:3072],
                scalar1=-32768.0, scalar2=SUC, op0=ALU.max, op1=ALU.mult),
               inc=sV)
            op("v", lambda e, u=u: e.tensor_mul(
                TG[:, u, 0:512], SCR[:, 2048:2560], SCR[:, 2560:3072]),
               inc=sV)
            op("v", lambda e, u=u: e.reduce_max(
                CW[:, 25:26], TG[:, u, 0:512], axis=AX.X,
                apply_absolute_value=True), inc=sV)
            op("v", lambda e: e.tensor_max(MAXC, MAXC, CW[:, 25:26]), inc=sV)
            o_rel[u] = sV.n
            op("sy", lambda e, oc=oc, tt=tt, u=u: e.dma_start(
                out=obuf.ap()[oc, :, tt * 512:(tt + 1) * 512],
                in_=TG[:, u, 0:512]),
               waits=[(sV, sV.n)], inc=sSp, dma=True)
    finish_max(4)
    op("sy", lambda e: e.dma_start(out=mx_in[3].ap(), in_=STT[0:1, 564:566]),
       waits=[(sV, sV.n)], inc=sMx, dma=True)
    op("g", lambda e: e.collective_compute(
        "AllReduce", ALU.max, replica_groups=[list(range(8))],
        ins=[mx_in[3].ap().opt()], outs=[mx_out[3].ap().opt()]),
       waits=[sMx.at()], inc=sCc)
    op("sy", lambda e: e.dma_start(out=STT[0:1, 564:566],
                                   in_=mx_out[3].ap()),
       waits=[(sCc, sCc.n)], inc=sMx, dma=True)
    mx4 = sMx.at()
    scale_cols(4, QMAXI, SOC, ROC, extra_wait=[mx4])

    o_rel2 = 0
    for tt in range(NQB):
        for oc in range(24):
            op("sy", lambda e, oc=oc, tt=tt: e.dma_start(
                out=TU[:, oc, 0:512],
                in_=obuf.ap()[oc, :, tt * 512:(tt + 1) * 512]),
               waits=[(sSp, sSp.n), sSpH[0].at(), sSpH[1].at(),
                      (sT, o_rel2)], inc=sRl[0], dma=True)
        rl = sRl[0].at()
        for oc in range(24):
            op("v", lambda e, oc=oc: e.tensor_scalar(
                out=SCR[:, 2048:2560], in0=TU[:, oc, 0:512], scalar1=ROC,
                scalar2=MAGIC, op0=ALU.mult, op1=ALU.add),
               waits=[rl, (sA, sA.n)] if oc == 0 else [], inc=sV)
            op("v", lambda e: e.tensor_scalar(
                out=SCR[:, 2048:2560], in0=SCR[:, 2048:2560], scalar1=MAGIC,
                scalar2=32767.0, op0=ALU.subtract, op1=ALU.min), inc=sV)
            op("v", lambda e, oc=oc: e.tensor_scalar(
                out=TU[:, oc, 0:512], in0=SCR[:, 2048:2560],
                scalar1=-32768.0, scalar2=SOC, op0=ALU.max, op1=ALU.mult),
               inc=sV)
        for ob in range(4):
            for kh in range(2):
                for k2 in range(12):
                    k = kh * 12 + k2
                    woff = k2 * 256 if k2 < 8 else 4096 + (k2 - 8) * 256
                    op("sy", lambda e, ob=ob, k=k, woff=woff, k2=k2:
                       e.dma_start(
                           out=WS[:, woff:woff + 256],
                           in_=wd_bf.ap()[ob * 256:(ob + 1) * 256,
                                          k * 128:(k + 1) * 128],
                           transpose=True),
                       waits=([wprep_done["d"]] if (tt == 0 and ob == 0
                                                    and k == 0) else [])
                       + ([(sT, kb._fd_guard[0])] if k2 == 0 else []),
                       inc=sWq, dma=True)
                ldw2 = sWq.at()
                for oc2 in range(2):
                    bank = mm[(ob % 2) * 2 + oc2]
                    for k2 in range(12):
                        k = kh * 12 + k2
                        woff = k2 * 256 if k2 < 8 else 4096 + (k2 - 8) * 256
                        op("t", lambda e, b=bank, k=k, k2=k2, oc2=oc2,
                           tt=tt, woff=woff: e.matmul(
                               b[:], WS[:, woff + oc2 * 128:
                                        woff + oc2 * 128 + 128],
                               TU[:, k, 0:512],
                               start=(k == 0), stop=(k == 23)),
                           waits=([ldw2] if k2 == 0 else [])
                           + ([(sA, sA.n), (sV, sV.n)]
                              if (k == 0 and k2 == 0) else []),
                           inc=sT)
                    if kh == 1:
                        o = ob * 2 + oc2
                        op("a", lambda e, b=bank, o=o, tt=tt: e.copy(
                            BOTF[:, o, tt * 512:(tt + 1) * 512], b[:]),
                           waits=[sT], inc=sA)
                kb._fd_guard[0] = sT.n
        o_rel2 = sT.n
        def bot3(c, tt=tt):
            return BOTF[:, c, tt * 512:(tt + 1) * 512]
        ln_rows_from(bot3, tt, NCH, INV_DM)
        for c in range(NCH):
            op("v", lambda e, c=c, tt=tt: e.tensor_mul(
                SCR[:, 3328:3840], BOTF[:, c, tt * 512:(tt + 1) * 512],
                sc[0][:]), waits=[sT], inc=sV)
            op("v", lambda e: e.tensor_sub(SCR[:, 3328:3840],
                                           SCR[:, 3328:3840], sc[1][:]),
               inc=sV)
            op("v", lambda e, c=c, tt=tt: e.tensor_add(
                RES[:, c * TOK + tt * 512: c * TOK + (tt + 1) * 512],
                RES[:, c * TOK + tt * 512: c * TOK + (tt + 1) * 512],
                SCR[:, 3328:3840]), inc=sV)

    ostg = [SCR[:, 0:1024], SCR[:, 1024:2048]]
    for it in range(TOK // 128):
        ot = ostg[it % 2]
        ss = sSt[it % 2]
        for c in range(NCH):
            op("t", lambda e, c=c, it=it: e.transpose(
                bcb[:, 0:128],
                RES[:, c * TOK + it * 128: c * TOK + (it + 1) * 128],
                IDF[:]), waits=[sA, sV], inc=sT)
            w = [(sT, sT.n)]
            if it >= 2 and c == 0:
                w.append(ss.at())
            op("a", lambda e, ot=ot, c=c: e.copy(
                ot[:, c * 128:(c + 1) * 128], bcb[:, 0:128]),
               waits=w, inc=sA)
        op("sy", lambda e, ot=ot, it=it: e.dma_start(
            out=y[it * 128:(it + 1) * 128, :], in_=ot),
           waits=[sA], inc=ss, dma=True)
    op("sy", lambda e: e.nop(), waits=[sSt[0], sSt[1]])

    kb.replay()
    kb.es.close()
    return kb.nc


_PROG = None


def _get_prog():
    global _PROG
    if _PROG is None:
        _PROG = build_program()
    return _PROG


def make_in_maps(inputs):
    def f32(k):
        return np.ascontiguousarray(np.asarray(inputs[k], dtype=np.float32))
    x = f32("L0_attn_x")
    shared = {
        "W_dq": f32("L0_attn_W_dq"), "W_uq": f32("L0_attn_W_uq"),
        "q_ln_g": f32("L0_attn_q_ln_g"), "q_ln_b": f32("L0_attn_q_ln_b"),
        "W_dkv": f32("L0_attn_W_dkv"), "W_ukv": f32("L0_attn_W_ukv"),
        "kv_ln_g": f32("L0_attn_kv_ln_g"), "kv_ln_b": f32("L0_attn_kv_ln_b"),
        "W_o": f32("L0_attn_W_o"),
        "w1_w": f32("L0_ffn_w1_w"), "w1_s": f32("L0_ffn_w1_s"),
        "w2_w": f32("L0_ffn_w2_w"), "w2_s": f32("L0_ffn_w2_s"),
        "h1_w": f32("L0_ffn_h1_w"), "h1_s": f32("L0_ffn_h1_s"),
        "h2_w": f32("L0_ffn_h2_w"), "h2_s": f32("L0_ffn_h2_s"),
        "w3_w": f32("L0_ffn_w3_w"), "w3_s": f32("L0_ffn_w3_s"),
        "qkv_w": f32("L1_attn_qkv_w"),
        "qn_w": f32("L1_attn_qn_w"), "qn_b": f32("L1_attn_qn_b"),
        "kn_w": f32("L1_attn_kn_w"), "kn_b": f32("L1_attn_kn_b"),
        "sinks": f32("L1_attn_sinks").reshape(1, 16),
        "proj_w": f32("L1_attn_proj_w"), "proj_b": f32("L1_attn_proj_b"),
        "w_gate": f32("L1_ffn_w_gate"), "w_up": f32("L1_ffn_w_up"),
        "w_down": f32("L1_ffn_w_down"),
    }
    inv_freq = 1.0 / (10000.0 ** (np.arange(0, 64, 2, dtype=np.float32) / 64))
    in_maps = []
    for core in range(8):
        b, h = core // 2, core % 2
        m = dict(shared)
        m["x_q"] = np.ascontiguousarray(x[b, h * TOK:(h + 1) * TOK, :])
        m["x_seq"] = np.ascontiguousarray(x[b])
        kx = (np.arange(128)[:, None] + 128 * np.arange(32)[None, :]
              - 2048 * h).astype(np.float32)
        m["kidx"] = kx
        pos = (h * TOK + np.arange(TOK, dtype=np.float32))
        fr = pos[None, :] * inv_freq[:, None]
        m["cosb"] = np.tile(np.cos(fr), (4, 1)).astype(np.float16)
        m["sinb"] = np.tile(np.sin(fr), (4, 1)).astype(np.float16)
        kl = (np.arange(128)[:, None]
              + 128 * (np.arange(18)[None, :] - 2)).astype(np.float32)
        k1 = np.concatenate([kl, kl + 255.0], axis=1)
        if h == 0:
            k1[:, 0:2] = 1e9
            k1[:, 18:20] = 1e9
        m["kix1"] = k1
        in_maps.append(m)
    return in_maps


def kernel(**inputs):
    nc = _get_prog()
    in_maps = make_in_maps(inputs)
    res = run_bass_kernel_spmd(nc, in_maps, core_ids=list(range(8)))
    out = np.empty((B, S, DM), dtype=np.float32)
    for core in range(8):
        b, h = core // 2, core % 2
        out[b, h * TOK:(h + 1) * TOK, :] = res.results[core]["y"]
    return out


# version 03
# speedup vs baseline: 1.1091x; 1.1091x over previous
import numpy as np
import ml_dtypes
from contextlib import ExitStack

import concourse.bass as bass
import concourse.mybir as mybir
from concourse.bass_utils import run_bass_kernel_spmd

F32 = mybir.dt.float32
BF16 = mybir.dt.bfloat16
F16 = mybir.dt.float16
AF = mybir.ActivationFunctionType
ALU = mybir.AluOpType
AX = mybir.AxisListType
BF16NP = ml_dtypes.bfloat16

B, S, DM = 4, 4096, 1024
TOK = 2048
NCH = DM // 128
NQB = TOK // 512
MAGIC = 12582912.0
LN_EPS = 1e-5
INV_DM = 1.0 / DM

ENGS = ("sy", "v", "a", "t", "g")


class Sem:
    def __init__(self, nc, es, name):
        self.h = es.enter_context(nc.semaphore(name))
        self.n = 0

    def at(self):
        return (self, self.n)


class KB:
    def __init__(self):
        self.nc = bass.Bass("TRN2", target_bir_lowering=False, debug=False)
        self.es = ExitStack()
        self.streams = {e: [] for e in ENGS}
        self.self_sems = {}

    def sb(self, name, shape, dtype):
        return self.es.enter_context(self.nc.sbuf_tensor(name, shape, dtype))

    def psum(self, name, shape, dtype=F32):
        return self.es.enter_context(self.nc.psum_tensor(name, shape, dtype))

    def sem(self, name):
        return Sem(self.nc, self.es, name)

    def op(self, eng, fn, waits=(), inc=None, dma=False):
        wl = []
        auto = self.self_sems.get(eng) if not dma else None
        if auto is not None:
            if auto.n > 0:
                wl.append((auto.h, auto.n))
            if inc is None:
                inc = auto
        for w in waits:
            if isinstance(w, Sem):
                wl.append((w.h, w.n))
            else:
                s, n = w
                if n > 0:
                    wl.append((s.h, n))
        il = None
        if inc is not None:
            k = 16 if dma else 1
            inc.n += k
            il = (inc.h, k)

        def run(e, fn=fn, wl=wl, il=il):
            for h, n in wl:
                if n > 0:
                    e.wait_ge(h, n)
            instr = fn(e)
            if il is not None:
                instr.then_inc(il[0], il[1])
        self.streams[eng].append(run)

    def replay(self):
        block = self.es.enter_context(self.nc.Block())

        def runner(ops):
            def go(e):
                for f in ops:
                    f(e)
            return go
        block.sync(runner(self.streams["sy"]))
        block.vector(runner(self.streams["v"]))
        block.scalar(runner(self.streams["a"]))
        block.tensor(runner(self.streams["t"]))
        block.gpsimd(runner(self.streams["g"]))


def build_program(dbg_outs=False):
    dkind = "ExternalOutput" if dbg_outs else "Internal"
    kb = KB()
    nc = kb.nc
    op = kb.op

    def din(name, shape, dtype=F32):
        return nc.dram_tensor(name, shape, dtype, kind="ExternalInput").ap()

    x_q = din("x_q", [TOK, DM])
    x_seq = din("x_seq", [S, DM])
    kidx = din("kidx", [128, 32])
    W_dq = din("W_dq", [DM, 256])
    W_uq = din("W_uq", [256, DM])
    q_ln_g = din("q_ln_g", [256])
    q_ln_b = din("q_ln_b", [256])
    W_dkv = din("W_dkv", [DM, 256])
    W_ukv = din("W_ukv", [256, 2048])
    kv_ln_g = din("kv_ln_g", [256])
    kv_ln_b = din("kv_ln_b", [256])
    W_o = din("W_o", [DM, DM])
    w1_w = din("w1_w", [4096, DM])
    w1_s = din("w1_s", [4096])
    w2_w = din("w2_w", [4096, DM])
    w2_s = din("w2_s", [4096])
    h1_w = din("h1_w", [4096, 4096])
    h1_s = din("h1_s", [4096])
    h2_w = din("h2_w", [4096, 4096])
    h2_s = din("h2_s", [4096])
    w3_w = din("w3_w", [DM, 4096])
    w3_s = din("w3_s", [DM])
    qkv_w = din("qkv_w", [3072, DM])
    qn_w = din("qn_w", [64])
    qn_b = din("qn_b", [64])
    kn_w = din("kn_w", [64])
    kn_b = din("kn_b", [64])
    sinks = din("sinks", [1, 16])
    proj_w = din("proj_w", [DM, DM])
    proj_b = din("proj_b", [DM])
    w_gate = din("w_gate", [3072, DM])
    w_up = din("w_up", [3072, DM])
    w_down = din("w_down", [DM, 3072])
    cosb = din("cosb", [128, TOK], mybir.dt.float16)
    sinb = din("sinb", [128, TOK], mybir.dt.float16)
    kix1 = din("kix1", [128, 36])
    y = nc.dram_tensor("y", [TOK, DM], F32, kind="ExternalOutput").ap()

    woT_bf = nc.dram_tensor("woT_bf", [DM, DM], BF16)
    w1_bf = nc.dram_tensor("w1_bf", [4096, DM], BF16)
    w2_bf = nc.dram_tensor("w2_bf", [4096, DM], BF16)
    h1_bf = nc.dram_tensor("h1_bf", [4096, 4096], BF16)
    h2_bf = nc.dram_tensor("h2_bf", [4096, 4096], BF16)
    w3_bf = nc.dram_tensor("w3_bf", [DM, 4096], BF16)
    qkv_bf = nc.dram_tensor("qkv_bf", [3072, DM], BF16)
    proj_bf = nc.dram_tensor("proj_bf", [DM, DM], BF16)
    wg_bf = nc.dram_tensor("wg_bf", [3072, DM], BF16, kind=dkind)
    wu_bf = nc.dram_tensor("wu_bf", [3072, DM], BF16, kind=dkind)
    wd_bf = nc.dram_tensor("wd_bf", [DM, 3072], BF16, kind=dkind)
    gbuf = nc.dram_tensor("gbuf", [24, 128, TOK], BF16, kind=dkind)
    g2buf = nc.dram_tensor("g2buf", [24, 128, TOK], BF16, kind=dkind)
    ubuf = nc.dram_tensor("ubuf", [24, 128, TOK], BF16, kind=dkind)
    obuf = nc.dram_tensor("obuf", [24, 128, TOK], BF16, kind=dkind)
    mx_in = [nc.dram_tensor(f"mx_in{i}", [1, 2], F32) for i in range(4)]
    mx_out = [nc.dram_tensor(f"mx_out{i}", [1, 2], F32, kind=dkind)
              for i in range(4)]
    bdk_in = [nc.dram_tensor(f"bdk_in{g}", [128, 256], BF16) for g in range(8)]
    bdk_out = [nc.dram_tensor(f"bdk_out{g}", [256, 256], BF16) for g in range(8)]
    bdv_in = [nc.dram_tensor(f"bdv_in{g}", [256, 130], BF16) for g in range(8)]
    bdv_out = [nc.dram_tensor(f"bdv_out{g}", [512, 130], BF16) for g in range(8)]

    ident_f32 = nc.inline_tensor(np.eye(128, dtype=np.float32), "c_idf")
    ident_bf = nc.inline_tensor(np.eye(128, dtype=BF16NP), "c_idb")
    ones_col = nc.inline_tensor(np.ones((128, 1), dtype=BF16NP), "c_onc")
    ones_row_bf = nc.inline_tensor(np.ones((1, 128), dtype=BF16NP), "c_onrb")
    ones_row_f = nc.inline_tensor(np.ones((1, 128), dtype=np.float32), "c_onrf")
    ones_f_full = nc.inline_tensor(np.ones((128, 64), dtype=np.float32), "c_onff")
    rmat = np.zeros((128, 128), dtype=np.float32)
    for hh in range(2):
        b0 = 64 * hh
        for i in range(32):
            rmat[b0 + i, b0 + 32 + i] = -1.0
            rmat[b0 + 32 + i, b0 + i] = 1.0
    rotT_c = nc.inline_tensor(rmat.T.astype(BF16NP).copy(), "c_rotT")
    qixr = np.arange(TOK, dtype=np.float16)[None, :]
    qixr_c = nc.inline_tensor(qixr, "c_qixr")
    ones_row_h = nc.inline_tensor(np.ones((1, 128), dtype=np.float16), "c_onrh")

    RES = kb.sb("RES", [128, NCH * TOK], F32)
    A1 = kb.sb("A1", [128, 40960], BF16)
    A2 = kb.sb("A2", [128, 4608], BF16)
    SCR = kb.sb("SCR", [128, 4096], F32)
    STT = kb.sb("STT", [128, 1024], F32)
    WS = kb.sb("WS", [128, 6144], BF16)
    WPERS = kb.sb("WPERS", [128, 6144], BF16)
    CW = kb.sb("CW", [128, 64], F32)
    IDF = kb.sb("IDF", [128, 128], F32)
    IDB = kb.sb("IDB", [128, 128], BF16)
    ONC = kb.sb("ONC", [128, 1], BF16)
    ONRB = kb.sb("ONRB", [1, 128], BF16)
    ONRF = kb.sb("ONRF", [1, 128], F32)
    ONFF = kb.sb("ONFF", [128, 64], F32)
    ONRH = kb.sb("ONRH", [1, 128], F16)
    KIX = kb.sb("KIX", [128, 32], F32)
    QIX = kb.sb("QIX", [1, TOK], F16)

    FS1 = kb.sb("FS1", [128, 32], F32)
    FS2 = kb.sb("FS2", [128, 32], F32)
    FH1 = kb.sb("FH1", [128, 32], F32)
    FH2 = kb.sb("FH2", [128, 32], F32)
    FS3 = kb.sb("FS3", [128, 8], F32)
    ROTT = kb.sb("ROTT", [128, 128], BF16)
    KIX1 = kb.sb("KIX1", [128, 36], F32)
    QNWB = kb.sb("QNWB", [128, 4], F32)
    SNK = kb.sb("SNK", [1, 16], F32)
    PBC = kb.sb("PBC", [128, 8], F32)
    ESNK = kb.sb("ESNK", [128, 2], F32)
    BQC = kb.sb("BQC", [128, 8], F32)
    BKC = kb.sb("BKC", [128, 8], F32)
    BVR = kb.sb("BVR", [1, 1024], BF16)
    NEGG = kb.sb("NEGG", [128, 16], F32)

    WDQ = WPERS[:, 0:2048]
    WDKV = WPERS[:, 2048:4096]
    WUQ = WPERS[:, 0:2048]
    WUKV = WPERS[:, 2048:6144]

    CKVT = A1[:, 0:8192].rearrange("p (c t) -> p c t", c=2)
    KT = A1[:, 8192:12288]
    VA = A1[:, 12288:16448].rearrange("p (c t) -> p c t", c=32)
    CQT = A1[:, 16448:20544].rearrange("p (c t) -> p c t", c=2)
    ATT = A1[:, 20544:36928].rearrange("p (c t) -> p c t", c=8)
    ESL = A1[:, 36928:40000].rearrange("p (c t) -> p c t", c=6)
    BO = A1[:, 0:16384].rearrange("p (c t) -> p c t", c=8)
    QTG = WS[:, 0:2048]

    XNT = [A2[:, s * 1024:(s + 1) * 1024].rearrange("p (c t) -> p c t", c=8)
           for s in range(2)]
    XNS = [A2[:, 2048 + s * 1024: 3072 + s * 1024] for s in range(2)]
    CNS = A2[:, 4096:4352]
    SQB = [A2[:, 2048 + s * 512: 2560 + s * 512] for s in range(2)]

    mm = [kb.psum(f"mm{i}", [128, 512], F32) for i in range(4)]
    sc = [kb.psum(f"sc{i}", [128, 512], F32) for i in range(2)]
    bcb = kb.psum("bcb", [128, 512], F32)
    trp = kb.psum("trp", [128, 256], BF16)

    sId = kb.sem("sId")
    sLd = [kb.sem(f"sLd{i}") for i in range(2)]
    sWa = kb.sem("sWa")
    sWo = [kb.sem(f"sWo{i}") for i in range(2)]
    sGpW = kb.sem("sGpW")
    sV = kb.sem("sV")
    sA = kb.sem("sA")
    sT = kb.sem("sT")
    sG = kb.sem("sG")
    sSt = [kb.sem(f"sSt{i}") for i in range(2)]
    sGAtt = [kb.sem(f"sGAtt{i}") for i in range(2)]
    kb.self_sems = {"v": sV, "a": sA, "g": sG}
    kb._oddslot = 0

    for dst, src in ((IDF, ident_f32), (IDB, ident_bf), (ONC, ones_col),
                     (ONRB, ones_row_bf), (ONRF, ones_row_f),
                     (ONFF, ones_f_full),
                     (QIX, qixr_c), (ONRH, ones_row_h)):
        op("sy", lambda e, d=dst, s=src: e.dma_start(out=d[:], in_=s.ap()),
           inc=sId, dma=True)
    op("sy", lambda e: e.dma_start(out=KIX[:], in_=kidx), inc=sId, dma=True)
    op("sy", lambda e: e.dma_start(out=ROTT[:], in_=rotT_c.ap()), inc=sId,
       dma=True)
    op("sy", lambda e: e.dma_start(out=KIX1[:], in_=kix1), inc=sId, dma=True)
    op("sy", lambda e: e.dma_start(out=SNK[:], in_=sinks), inc=sId, dma=True)
    for col, src in ((0, qn_w), (1, qn_b), (2, kn_w), (3, kn_b)):
        for half in range(2):
            op("sy", lambda e, col=col, src=src, half=half: e.dma_start(
                out=QNWB[half * 64:(half + 1) * 64, col:col + 1],
                in_=src.rearrange("(a d) -> d a", a=1),
                allow_slow_non_contiguous=True), inc=sId, dma=True)
    op("sy", lambda e: e.dma_start(
        out=PBC[:], in_=proj_b.rearrange("(c p) -> p c", p=128),
        allow_slow_non_contiguous=True), inc=sId, dma=True)
    GQC, BQ0, GKC, BK0 = CW[:, 0:2], CW[:, 2:4], CW[:, 4:6], CW[:, 6:8]
    for dst, src in ((GQC, q_ln_g), (BQ0, q_ln_b), (GKC, kv_ln_g),
                     (BK0, kv_ln_b), (FS1[:], w1_s), (FS2[:], w2_s),
                     (FH1[:], h1_s), (FH2[:], h2_s), (FS3[:], w3_s)):
        op("sy", lambda e, d=dst, s=src: e.dma_start(
            out=d, in_=s.rearrange("(c p) -> p c", p=128),
            allow_slow_non_contiguous=True), inc=sId, dma=True)

    op("g", lambda e: e.dma_start(
        out=WDQ[:].rearrange("p (c o) -> p c o", o=256),
        in_=W_dq.rearrange("(c p) o -> p c o", p=128)), inc=sGpW, dma=True)
    op("g", lambda e: e.dma_start(
        out=WDKV[:].rearrange("p (c o) -> p c o", o=256),
        in_=W_dkv.rearrange("(c p) o -> p c o", p=128)), inc=sGpW, dma=True)
    sPrA = kb.sem("sPrA")
    sPrB = kb.sem("sPrB")

    def prep_chain(sem, slab, jobs):
        for src, dst, rows, cols in jobs:
            for i in range(rows // 128):
                for cb in range(cols // 1024):
                    op("g", lambda e, src=src, i=i, cb=cb, slab=slab:
                       e.dma_start(
                           out=slab,
                           in_=src[i * 128:(i + 1) * 128,
                                   cb * 1024:(cb + 1) * 1024]),
                       waits=[sem.at()], inc=sem, dma=True)
                    op("g", lambda e, dst=dst, i=i, cb=cb, slab=slab:
                       e.dma_start(
                           out=dst.ap()[i * 128:(i + 1) * 128,
                                        cb * 1024:(cb + 1) * 1024],
                           in_=slab),
                       waits=[sem.at()], inc=sem, dma=True)

    prep_chain(sPrA, WS[:, 2048:3072],
               [(W_o, woT_bf, DM, DM), (w1_w, w1_bf, 4096, DM),
                (w2_w, w2_bf, 4096, DM), (h1_w, h1_bf, 4096, 4096)])
    prep_chain(sPrB, WS[:, 3072:4096],
               [(h2_w, h2_bf, 4096, 4096), (w3_w, w3_bf, DM, 4096),
                (qkv_w, qkv_bf, 3072, DM), (proj_w, proj_bf, DM, DM)])
    prB_qkv_done = (4096 // 128) * 4 * 32 + (DM // 128) * 4 * 32 \
        + (3072 // 128) * 1 * 32
    prB_all = prB_qkv_done + (DM // 128) * 1 * 32
    prA_wo_done = 8 * 2 * 16

    xstg = [SCR[:, 0:1024], SCR[:, 1024:2048]]
    xsq = SCR[:, 2048:3072]
    csq = SCR[:, 3072:3328]
    MUC, RSC = CW[:, 8:9], CW[:, 9:10]
    T0, T1, T2, T3 = (CW[:, 10:11], CW[:, 11:12], CW[:, 12:13], CW[:, 13:14])
    CM, CR = CW[:, 14:15], CW[:, 15:16]
    C0, C1, C2 = CW[:, 16:17], CW[:, 17:18], CW[:, 18:19]

    g_xstg = [[], []]
    g_xnt = [0, 0]
    g_xns = [0, 0]
    g_cns = [0]

    def tok_ln_tile(seq, it, src_dram, src_row):
        sl = sLd[seq % 2]
        st = xstg[seq % 2]
        op("sy", lambda e, st=st: e.dma_start(
            out=st, in_=src_dram[src_row:src_row + 128, :]),
           waits=list(g_xstg[seq % 2]), inc=sl, dma=True)
        ldw = sl.at()
        op("v", lambda e, st=st: e.reduce_sum(T0, st, axis=AX.X),
           waits=[ldw], inc=sV)
        op("a", lambda e, st=st: e.activation(out=xsq, in_=st, func=AF.Square),
           waits=[ldw, sV.at()], inc=sA)
        op("v", lambda e: e.reduce_sum(T1, xsq, axis=AX.X), waits=[sA], inc=sV)
        op("a", lambda e: e.activation(out=MUC, in_=T0, func=AF.Copy,
                                       scale=INV_DM), waits=[sV], inc=sA)
        op("a", lambda e: e.activation(out=T2, in_=MUC, func=AF.Square),
           waits=[sA.at()], inc=sA)
        op("v", lambda e: e.tensor_scalar(out=T3, in0=T1, scalar1=INV_DM,
                                          scalar2=LN_EPS, op0=ALU.mult,
                                          op1=ALU.add), waits=[sA], inc=sV)
        op("v", lambda e: e.tensor_sub(T3, T3, T2), inc=sV)
        op("v", lambda e: e.reciprocal(T3, T3), inc=sV)
        op("a", lambda e: e.activation(out=RSC, in_=T3, func=AF.Sqrt),
           waits=[sV], inc=sA)
        xns = XNS[seq % 2]
        op("v", lambda e, st=st, xns=xns: e.tensor_scalar(
            out=xns, in0=st, scalar1=MUC, scalar2=RSC,
            op0=ALU.subtract, op1=ALU.mult),
           waits=[sA, (sT, g_xns[seq % 2])], inc=sV)
        g_xstg[seq % 2] = [sV.at()]
        xnt = XNT[seq % 2]
        for c in range(NCH):
            op("t", lambda e, xns=xns, c=c: e.transpose(
                trp[:, 0:128], xns[:, c * 128:(c + 1) * 128], IDB[:]),
               waits=[sV, sA, sId.at()], inc=sT)
            w = [sT.at()]
            if c == 0:
                w.append((sT, g_xnt[seq % 2]))
            op("a", lambda e, xnt=xnt, c=c: e.copy(xnt[:, c, 0:128],
                                                   trp[:, 0:128]),
               waits=w, inc=sA)
        g_xns[seq % 2] = sT.n
        return st

    def raw_res_transpose(seq, st, it2):
        for c in range(NCH):
            op("t", lambda e, st=st, c=c: e.transpose(
                bcb[:, 0:128], st[:, c * 128:(c + 1) * 128], IDF[:]),
               waits=[sA], inc=sT)
            op("a", lambda e, c=c, it2=it2: e.copy(
                RES[:, c * TOK + it2 * 128: c * TOK + (it2 + 1) * 128],
                bcb[:, 0:128]), waits=[sT], inc=sA)
        g_xstg[seq % 2].append(sT.at())

    def proj_c_tile(seq, wsb, dst_ct, dst_col):
        xnt = XNT[seq % 2]
        bank = mm[seq % 2]
        for c in range(NCH):
            op("t", lambda e, b=bank, xnt=xnt, c=c, w=wsb: e.matmul(
                b[:, 0:256], xnt[:, c, 0:128], w[:, c * 256:(c + 1) * 256],
                start=(c == 0), stop=(c == NCH - 1)),
               waits=[sA, sV, (sGpW, 32)] if c == 0 else [],
               inc=sT if c == NCH - 1 else None)
        g_xnt[seq % 2] = sT.n
        op("v", lambda e, b=bank: e.reduce_sum(C0, b[:, 0:256], axis=AX.X),
           waits=[sT], inc=sV)
        op("a", lambda e, b=bank: e.activation(out=csq, in_=b[:, 0:256],
                                               func=AF.Square),
           waits=[sT, sV.at()], inc=sA)
        op("v", lambda e: e.reduce_sum(C1, csq, axis=AX.X), waits=[sA], inc=sV)
        inv = 1.0 / 256.0
        op("a", lambda e: e.activation(out=C0, in_=C0, func=AF.Copy,
                                       scale=inv), waits=[sV], inc=sA)
        op("a", lambda e: e.activation(out=C2, in_=C0, func=AF.Square),
           waits=[sA.at()], inc=sA)
        op("v", lambda e: e.tensor_scalar(out=C1, in0=C1, scalar1=inv,
                                          scalar2=LN_EPS, op0=ALU.mult,
                                          op1=ALU.add), waits=[sA], inc=sV)
        op("v", lambda e: e.tensor_sub(C1, C1, C2), inc=sV)
        op("v", lambda e: e.reciprocal(C1, C1), inc=sV)
        op("a", lambda e: e.activation(out=C1, in_=C1, func=AF.Sqrt),
           waits=[sV], inc=sA)
        op("v", lambda e, b=bank: e.tensor_scalar(
            out=CNS, in0=b[:, 0:256], scalar1=C0, scalar2=C1,
            op0=ALU.subtract, op1=ALU.mult),
           waits=[sA, (sT, g_cns[0])], inc=sV)
        for c2 in range(2):
            op("t", lambda e, c2=c2: e.transpose(
                trp[:, 128:256], CNS[:, c2 * 128:(c2 + 1) * 128], IDB[:]),
               waits=[sV, sA], inc=sT)
            op("a", lambda e, c2=c2, dst=dst_ct, dc=dst_col: e.copy(
                dst[:, c2, dc:dc + 128], trp[:, 128:256]), waits=[sT], inc=sA)
        g_cns[0] = sT.n

    seq = 0
    for it in range(S // 128):
        tok_ln_tile(seq, it, x_seq, it * 128)
        proj_c_tile(seq, WDKV, CKVT, it * 128)
        seq += 1
    for it2 in range(TOK // 128):
        st = tok_ln_tile(seq, it2, x_q, it2 * 128)
        raw_res_transpose(seq, st, it2)
        proj_c_tile(seq, WDQ, CQT, it2 * 128)
        seq += 1

    fold_f32 = SCR[:, 0:2048]
    QROW = STT[0:1, 0:1024]
    KVROW = SCR[0:1, 2048:3072]
    VROW = SCR[0:1, 3072:4096]
    BQB = kb.sb("BQB", [128, 2], BF16)
    BKB = kb.sb("BKB", [128, 2], BF16)
    op("v", lambda e: e.tensor_copy(BQB[:], BQ0), waits=[sId.at()], inc=sV)
    op("v", lambda e: e.tensor_copy(BKB[:], BK0), inc=sV)
    for c in range(2):
        op("sy", lambda e, c=c: e.dma_start(
            out=fold_f32[:, 0:1024], in_=W_uq[c * 128:(c + 1) * 128, :]),
           waits=[(sV, sV.n), (sA, sA.n), (sT, sT.n)], inc=sWa, dma=True)
        op("v", lambda e, c=c: e.tensor_scalar(
            out=WUQ[:, c * 1024:(c + 1) * 1024], in0=fold_f32[:, 0:1024],
            scalar1=GQC[:, c:c + 1], scalar2=None, op0=ALU.mult),
           waits=[sWa, sId.at()], inc=sV)
        op("v", lambda e: e.tensor_copy(WS[:, 0:1024], fold_f32[:, 0:1024]),
           inc=sV)
        for half in range(2):
            op("t", lambda e, half=half, c=c: e.matmul(
                sc[half][0:1, 0:512], BQB[:, c:c + 1],
                WS[:, half * 512: half * 512 + 512],
                start=(c == 0), stop=(c == 1)),
               waits=[(sV, sV.n)] if half == 0 else [], inc=sT)
    for half in range(2):
        op("a", lambda e, half=half: e.copy(
            QROW[:, half * 512:(half + 1) * 512], sc[half][0:1, 0:512]),
           waits=[sT], inc=sA)
    for c in range(2):
        op("sy", lambda e, c=c: e.dma_start(
            out=fold_f32, in_=W_ukv[c * 128:(c + 1) * 128, :]),
           waits=[(sV, sV.n), (sA, sA.n), (sT, sT.n)], inc=sWa, dma=True)
        op("v", lambda e, c=c: e.tensor_scalar(
            out=WUKV[:, c * 2048:(c + 1) * 2048], in0=fold_f32,
            scalar1=GKC[:, c:c + 1], scalar2=None, op0=ALU.mult),
           waits=[sWa], inc=sV)
        op("v", lambda e: e.tensor_copy(WS[:, 0:2048], fold_f32), inc=sV)
        for half in range(4):
            op("t", lambda e, half=half, c=c: e.matmul(
                mm[half][0:1, 0:512], BKB[:, c:c + 1],
                WS[:, half * 512: half * 512 + 512],
                start=(c == 0), stop=(c == 1)),
               waits=[(sV, sV.n)] if half == 0 else [], inc=sT)
    for half in range(4):
        dst = (KVROW if half < 2 else VROW)
        col = (half % 2) * 512
        op("a", lambda e, dst=dst, col=col, half=half: e.copy(
            dst[:, col:col + 512], mm[half][0:1, 0:512]), waits=[sT], inc=sA)
    for oc in range(8):
        op("t", lambda e, oc=oc: e.matmul(
            bcb[:, oc:oc + 1], QROW[:, oc * 128:(oc + 1) * 128],
            ONRF[:, 0:1], start=True, stop=True),
           waits=[sA] if oc == 0 else [], inc=sT if oc == 7 else None)
    op("a", lambda e: e.copy(BQC[:], bcb[:, 0:8]), waits=[sT], inc=sA)
    for oc in range(8):
        op("t", lambda e, oc=oc: e.matmul(
            bcb[:, oc:oc + 1], KVROW[:, oc * 128:(oc + 1) * 128],
            ONRF[:, 0:1], start=True, stop=True),
           waits=[sA] if oc == 0 else [], inc=sT if oc == 7 else None)
    op("a", lambda e: e.copy(BKC[:], bcb[:, 0:8]), waits=[sT], inc=sA)
    op("v", lambda e: e.tensor_copy(BVR[:], VROW), waits=[sA], inc=sV)

    ws_raw_free = (sT, sT.n)

    for kc in range(32):
        op("v", lambda e, kc=kc: e.memset(
            VA[:, kc, 0:130].rearrange("p (h z) -> p h z", h=2)[:, :, 64:65],
            1.0), inc=sV)

    w1T = nc.dram_tensor("w1T", [DM, 4096], BF16)
    w2T = nc.dram_tensor("w2T", [DM, 4096], BF16)
    h1T = nc.dram_tensor("h1T", [4096, 4096], BF16)
    h2T = nc.dram_tensor("h2T", [4096, 4096], BF16)
    w3T = nc.dram_tensor("w3T", [4096, DM], BF16)
    sGst = [kb.sem(f"sGst{i}") for i in range(2)]

    n_kt = [20 + 4 * j for j in range(NQB)]
    g_kt = 0
    g_negg = 0

    for grp in range(8):
        for tt in range(NQB):
            bank = mm[2 + tt % 2]
            for c in range(2):
                op("t", lambda e, b=bank, grp=grp, tt=tt, c=c: e.matmul(
                    b[:], WUQ[:, c * 1024 + grp * 128: c * 1024 + grp * 128 + 128],
                    CQT[:, c, tt * 512:(tt + 1) * 512],
                    start=(c == 0), stop=(c == 1)),
                   waits=([(sA, sA.n), (sV, sV.n)] if c == 0 else []) +
                         ([(sT, g_kt)] if (tt == 0 and c == 0) else []),
                   inc=sT if c == 1 else None)
            op("a", lambda e, b=bank, grp=grp, tt=tt: e.activation(
                out=QTG[:, tt * 512:(tt + 1) * 512], in_=b[:],
                func=AF.Identity, bias=BQC[:, grp:grp + 1], scale=1.0),
               waits=[sT], inc=sA)
        for tt in range(8):
            bank = mm[2 + tt % 2]
            for c in range(2):
                op("t", lambda e, b=bank, grp=grp, tt=tt, c=c: e.matmul(
                    b[:], WUKV[:, c * 2048 + grp * 128: c * 2048 + grp * 128 + 128],
                    CKVT[:, c, tt * 512:(tt + 1) * 512],
                    start=(c == 0), stop=(c == 1)),
                   waits=[(sA, sA.n)] if c == 0 else [],
                   inc=sT if c == 1 else None)
            op("a", lambda e, b=bank, grp=grp, tt=tt: e.activation(
                out=KT[:, tt * 512:(tt + 1) * 512], in_=b[:],
                func=AF.Identity, bias=BKC[:, grp:grp + 1], scale=1.0),
               waits=[sT], inc=sA)
        for kc in range(32):
            bank = mm[2 + kc % 2]
            for c in range(2):
                op("t", lambda e, b=bank, grp=grp, kc=kc, c=c: e.matmul(
                    b[:, 0:128], CKVT[:, c, kc * 128:(kc + 1) * 128],
                    WUKV[:, c * 2048 + 1024 + grp * 128:
                         c * 2048 + 1024 + grp * 128 + 128],
                    start=(c == 0), stop=False),
                   waits=[(sV, sV.n), (sA, sA.n)] if c == 0 else [],
                   inc=None)
            op("t", lambda e, b=bank, grp=grp: e.matmul(
                b[:, 0:128], ONRB[:, 0:128],
                BVR[:, grp * 128:(grp + 1) * 128],
                start=False, stop=True), inc=sT)
            op("v", lambda e, b=bank, kc=kc: e.tensor_copy(
                VA[:, kc, 0:130].rearrange("p (h z) -> p h z", h=2)[:, :, 0:64],
                b[:, 0:128].rearrange("p (h d) -> p h d", h=2)),
               waits=[sT], inc=sV)

        for hh in range(2):
            base = 64 * hh
            for tt in range(NQB):
                op("a", lambda e, base=base, tt=tt: e.activation(
                    out=SCR[base:base + 64, 0:512],
                    in_=QTG[base:base + 64, tt * 512:(tt + 1) * 512],
                    func=AF.Square), waits=[(sV, sV.n)], inc=sA)
                op("v", lambda e, base=base: e.tensor_copy(
                    SQB[0][base:base + 64, :], SCR[base:base + 64, 0:512]),
                   waits=[sA], inc=sV)
                op("t", lambda e, base=base: e.matmul(
                    sc[0][0:1, 0:512], ONC[base:base + 64, :],
                    SQB[0][base:base + 64, :], start=True, stop=True),
                   waits=[sV], inc=sT)
                op("v", lambda e, hh=hh, tt=tt: e.reduce_max(
                    STT[0:1, 526 + hh * 16 + tt: 527 + hh * 16 + tt],
                    sc[0][0:1, 0:512], axis=AX.X), waits=[sT], inc=sV)
            for tt in range(8):
                op("a", lambda e, base=base, tt=tt: e.activation(
                    out=SCR[base:base + 64, 0:512],
                    in_=KT[base:base + 64, tt * 512:(tt + 1) * 512],
                    func=AF.Square), waits=[(sV, sV.n)], inc=sA)
                op("v", lambda e, base=base: e.tensor_copy(
                    SQB[0][base:base + 64, :], SCR[base:base + 64, 0:512]),
                   waits=[sA], inc=sV)
                op("t", lambda e, base=base: e.matmul(
                    sc[0][0:1, 0:512], ONC[base:base + 64, :],
                    SQB[0][base:base + 64, :], start=True, stop=True),
                   waits=[sV], inc=sT)
                op("v", lambda e, hh=hh, tt=tt: e.reduce_max(
                    STT[0:1, 534 + hh * 16 + tt: 535 + hh * 16 + tt],
                    sc[0][0:1, 0:512], axis=AX.X), waits=[sT], inc=sV)
            op("v", lambda e, hh=hh: e.reduce_max(
                STT[0:1, 516 + hh:517 + hh], STT[0:1, 526 + hh * 16:526 + hh * 16 + NQB],
                axis=AX.X), inc=sV)
            op("v", lambda e, hh=hh: e.reduce_max(
                STT[0:1, 518 + hh:519 + hh], STT[0:1, 534 + hh * 16:534 + hh * 16 + 8],
                axis=AX.X), inc=sV)
        op("a", lambda e: e.activation(out=STT[0:1, 520:524], in_=STT[0:1, 516:520],
                                       func=AF.Sqrt), waits=[sV], inc=sA)
        op("v", lambda e: e.tensor_mul(STT[0:1, 524:526], STT[0:1, 520:522],
                                       STT[0:1, 522:524]), waits=[sA], inc=sV)
        op("v", lambda e: e.tensor_scalar(
            out=STT[0:1, 512:514], in0=STT[0:1, 524:526], scalar1=-0.125,
            scalar2=None, op0=ALU.mult), inc=sV)
        op("t", lambda e: e.matmul(bcb[:, 0:2], ONRF[:, 0:128],
                                   STT[0:1, 512:514], start=True, stop=True),
           waits=[sV, (sA, g_negg)], inc=sT)
        op("a", lambda e, grp=grp: e.copy(NEGG[:, grp * 2:(grp + 1) * 2],
                                          bcb[:, 0:2]), waits=[sT], inc=sA)
        g_negg = sA.n

        for j in range(NQB):
            op("t", lambda e, j=j: e.matmul(
                bcb[:], ONRH[:, 0:128], QIX[:, j * 512:(j + 1) * 512],
                start=True, stop=True), waits=[sA, sV], inc=sT)
            qidx_done = sT.n
            for kt in range(n_kt[j]):
                msl = ESL[:, 4 + (kt % 2), 0:512]
                op("v", lambda e, msl=msl, kt=kt: e.tensor_scalar(
                    out=msl, in0=bcb[:], scalar1=KIX[:, kt:kt + 1],
                    scalar2=None, op0=ALU.is_ge),
                   waits=[(sT, qidx_done)], inc=sV)
                for sub in range(2):
                    base = 64 * sub
                    head = 2 * grp + sub
                    esl = ESL[:, sub, 0:512]
                    op("t", lambda e, base=base, j=j, kt=kt, sub=sub:
                       e.matmul(sc[sub][:],
                                KT[base:base + 64, kt * 128:(kt + 1) * 128],
                                QTG[base:base + 64, j * 512:(j + 1) * 512],
                                start=True, stop=True,
                                tile_position=(base, 0)),
                       waits=[(sA, sA.n)], inc=sT)
                    op("a", lambda e, sub=sub, esl=esl, head=head:
                       e.activation(out=esl, in_=sc[sub][:], func=AF.Exp,
                                    scale=0.125, bias=NEGG[:, head:head + 1]),
                       waits=[sT], inc=sA)
                    op("v", lambda e, esl=esl, msl=msl: e.tensor_mul(
                        esl, esl, msl), waits=[sA], inc=sV)
                    op("t", lambda e, sub=sub, kt=kt, j=j, esl=esl,
                       nkt=n_kt[j]: e.matmul(
                           mm[sub][0:65, :],
                           VA[:, kt, sub * 65: sub * 65 + 65],
                           esl, start=(kt == 0), stop=(kt == nkt - 1)),
                       waits=[sV] + ([(sA, sA.n)] if kt == 0 else []),
                       inc=sT)
            for hh in range(2):
                head = 2 * grp + hh
                op("a", lambda e, hh=hh: e.copy(SCR[0:65, 0:512],
                                                mm[hh][0:65, :]),
                   waits=[sT, (sV, sV.n)], inc=sA)
                op("v", lambda e: e.reciprocal(SCR[64:65, 512:1024],
                                               SCR[64:65, 0:512]),
                   waits=[sA], inc=sV)
                op("t", lambda e: e.matmul(bcb[0:64, :], ONFF[64:65, 0:64],
                                           SCR[64:65, 512:1024],
                                           start=True, stop=True,
                                           tile_position=(64, 0)),
                   waits=[sV], inc=sT)
                dch, dbase = head // 2, 64 * (head % 2)
                if dbase == 0:
                    op("v", lambda e, dch=dch, j=j: e.tensor_mul(
                        ATT[0:64, dch, j * 512:(j + 1) * 512],
                        SCR[0:64, 0:512], bcb[0:64, :]), waits=[sT], inc=sV)
                else:
                    oslot = kb._oddslot
                    osem = sGAtt[oslot]
                    op("v", lambda e, oslot=oslot: e.tensor_mul(
                        ESL[0:64, 2 + oslot, 0:512], SCR[0:64, 0:512],
                        bcb[0:64, :]), waits=[sT, osem.at()], inc=sV)
                    op("g", lambda e, oslot=oslot, dch=dch, j=j: e.dma_start(
                        out=ATT[64:128, dch, j * 512:(j + 1) * 512],
                        in_=ESL[0:64, 2 + oslot, 0:512]),
                       waits=[(sV, sV.n)], inc=osem, dma=True)
                    kb._oddslot = 1 - oslot
        g_kt = sT.n

    attn_done = [(sT, sT.n), (sA, sA.n), (sV, sV.n)]
    wo_guard = 0
    for oh in range(2):
        swo = sWo[oh]
        for d in range(8):
            woff = d * 512 if d < 4 else 4096 + (d - 4) * 512
            op("sy", lambda e, oh=oh, d=d, woff=woff: e.dma_start(
                out=WS[:, woff:woff + 512],
                in_=woT_bf.ap()[oh * 512:(oh + 1) * 512,
                                d * 128:(d + 1) * 128],
                transpose=True),
               waits=[(sPrA, prA_wo_done), ws_raw_free] + attn_done
               + ([(sT, wo_guard)] if oh == 1 else []),
               inc=swo, dma=True)
        for oc in range(4):
            o = oh * 4 + oc
            for tt in range(NQB):
                bank = mm[tt % 2]
                for d in range(8):
                    woff = d * 512 if d < 4 else 4096 + (d - 4) * 512
                    op("t", lambda e, b=bank, d=d, oc=oc, tt=tt, woff=woff:
                       e.matmul(
                        b[:], WS[:, woff + oc * 128: woff + oc * 128 + 128],
                        ATT[:, d, tt * 512:(tt + 1) * 512],
                        start=(d == 0), stop=(d == 7)),
                       waits=([(swo, swo.n), (sV, sV.n), (sA, sA.n),
                               sGAtt[0].at(), sGAtt[1].at()]
                              if d == 0 else []),
                       inc=sT if d == 7 else None)
                op("a", lambda e, b=bank, o=o, tt=tt: e.copy(
                    BO[:, o, tt * 512:(tt + 1) * 512], b[:]),
                   waits=[sT], inc=sA)
        if oh == 0:
            wo_guard = sT.n

    def post_ln_residual(src_bf):
        for tt in range(NQB):
            for c in range(NCH):
                op("a", lambda e, c=c, tt=tt, src=src_bf: e.activation(
                    out=SQB[c % 2], in_=src[:, c, tt * 512:(tt + 1) * 512],
                    func=AF.Square), waits=[(sT, sT.n), (sV, sV.n)], inc=sA)
                op("t", lambda e, c=c, tt=tt, src=src_bf: e.matmul(
                    sc[0][0:1, :], ONC[:], src[:, c, tt * 512:(tt + 1) * 512],
                    start=(c == 0), stop=(c == NCH - 1)),
                   waits=[(sV, sV.n)] if c == 0 else [], inc=sT)
                op("t", lambda e, c=c: e.matmul(
                    sc[1][0:1, :], ONC[:], SQB[c % 2],
                    start=(c == 0), stop=(c == NCH - 1)),
                   waits=[sA], inc=sT)
            op("a", lambda e: e.activation(out=SCR[0:1, 0:512],
                                           in_=sc[0][0:1, :], func=AF.Copy,
                                           scale=INV_DM), waits=[sT], inc=sA)
            op("a", lambda e: e.activation(out=SCR[0:1, 512:1024],
                                           in_=SCR[0:1, 0:512],
                                           func=AF.Square),
               waits=[sA.at()], inc=sA)
            op("v", lambda e: e.tensor_scalar(
                out=SCR[0:1, 1024:1536], in0=sc[1][0:1, :], scalar1=INV_DM,
                scalar2=LN_EPS, op0=ALU.mult, op1=ALU.add),
               waits=[sA], inc=sV)
            op("v", lambda e: e.tensor_sub(SCR[0:1, 1024:1536],
                                           SCR[0:1, 1024:1536],
                                           SCR[0:1, 512:1024]), inc=sV)
            op("v", lambda e: e.reciprocal(SCR[0:1, 1024:1536],
                                           SCR[0:1, 1024:1536]), inc=sV)
            op("a", lambda e: e.activation(out=SCR[0:1, 1024:1536],
                                           in_=SCR[0:1, 1024:1536],
                                           func=AF.Sqrt),
               waits=[sV], inc=sA)
            op("v", lambda e: e.tensor_mul(SCR[0:1, 1536:2048],
                                           SCR[0:1, 0:512],
                                           SCR[0:1, 1024:1536]), waits=[sA],
               inc=sV)
            op("t", lambda e: e.matmul(sc[0][:], ONRF[:, 0:128],
                                       SCR[0:1, 1024:1536], start=True,
                                       stop=True), waits=[sV], inc=sT)
            op("t", lambda e: e.matmul(sc[1][:], ONRF[:, 0:128],
                                       SCR[0:1, 1536:2048], start=True,
                                       stop=True), inc=sT)
            for c in range(NCH):
                op("v", lambda e, c=c, tt=tt, src=src_bf: e.tensor_mul(
                    SCR[:, 3328:3840], src[:, c, tt * 512:(tt + 1) * 512],
                    sc[0][:]), waits=[sT], inc=sV)
                op("v", lambda e: e.tensor_sub(
                    SCR[:, 3328:3840], SCR[:, 3328:3840], sc[1][:]), inc=sV)
                op("v", lambda e, c=c, tt=tt: e.tensor_add(
                    RES[:, c * TOK + tt * 512: c * TOK + (tt + 1) * 512],
                    RES[:, c * TOK + tt * 512: c * TOK + (tt + 1) * 512],
                    SCR[:, 3328:3840]), inc=sV)

    post_ln_residual(BO)

    H_R = A1[:, 0:16384].rearrange("p (c t) -> p c t", c=32)
    G_R = A1[:, 16384:32768].rearrange("p (c t) -> p c t", c=32)
    FSL = [A1[:, 32768 + i * 4096: 32768 + (i + 1) * 4096] for i in range(2)]
    XF = A2[:, 0:4096].rearrange("p (c t) -> p c t", c=8)
    BOT = A1[:, 16384:20480].rearrange("p (c t) -> p c t", c=8)
    XB = SQB[0]

    sFF = [kb.sem(f"sFF{i}") for i in range(2)]

    def ln_rows_from(stat_src_bf, tt, chunks, inv):
        for c in range(chunks):
            op("a", lambda e, c=c: e.activation(
                out=SQB[1], in_=stat_src_bf(c), func=AF.Square),
               waits=[(sT, sT.n), (sV, sV.n)], inc=sA)
            op("t", lambda e, c=c: e.matmul(
                sc[0][0:1, :], ONC[:], stat_src_bf(c),
                start=(c == 0), stop=(c == chunks - 1)),
               waits=[(sV, sV.n)] if c == 0 else [], inc=sT)
            op("t", lambda e, c=c: e.matmul(
                sc[1][0:1, :], ONC[:], SQB[1],
                start=(c == 0), stop=(c == chunks - 1)),
               waits=[sA], inc=sT)
        op("a", lambda e: e.activation(out=SCR[0:1, 0:512],
                                       in_=sc[0][0:1, :], func=AF.Copy,
                                       scale=inv), waits=[sT], inc=sA)
        op("a", lambda e: e.activation(out=SCR[0:1, 512:1024],
                                       in_=SCR[0:1, 0:512], func=AF.Square),
           waits=[sA.at()], inc=sA)
        op("v", lambda e: e.tensor_scalar(
            out=SCR[0:1, 1024:1536], in0=sc[1][0:1, :], scalar1=inv,
            scalar2=LN_EPS, op0=ALU.mult, op1=ALU.add), waits=[sA], inc=sV)
        op("v", lambda e: e.tensor_sub(SCR[0:1, 1024:1536],
                                       SCR[0:1, 1024:1536],
                                       SCR[0:1, 512:1024]), inc=sV)
        op("v", lambda e: e.reciprocal(SCR[0:1, 1024:1536],
                                       SCR[0:1, 1024:1536]), inc=sV)
        op("a", lambda e: e.activation(out=SCR[0:1, 1024:1536],
                                       in_=SCR[0:1, 1024:1536],
                                       func=AF.Sqrt), waits=[sV], inc=sA)
        op("v", lambda e: e.tensor_mul(SCR[0:1, 1536:2048], SCR[0:1, 0:512],
                                       SCR[0:1, 1024:1536]), waits=[sA],
           inc=sV)
        op("t", lambda e: e.matmul(sc[0][:], ONRF[:, 0:128],
                                   SCR[0:1, 1024:1536], start=True,
                                   stop=True), waits=[sV], inc=sT)
        op("t", lambda e: e.matmul(sc[1][:], ONRF[:, 0:128],
                                   SCR[0:1, 1536:2048], start=True,
                                   stop=True), inc=sT)

    def ffn_mm_phase(wsrc, n_out, n_k, scol, evac, tt, act_chunks,
                     start_wait=(), wT=None):
        n_ob = n_out // 256
        n_kh = max(1, n_k // 16)
        kh_sz = min(n_k, 16)
        unit = 0
        wkey = id(wsrc)
        for ob in range(n_ob):
            for kh in range(n_kh):
                sl = unit % 2
                sem = sFF[sl]
                for k2 in range(kh_sz):
                    k = kh * kh_sz + k2
                    if tt == 0 or wT is None:
                        op("sy", lambda e, ob=ob, k=k, k2=k2, sl=sl,
                           wsrc=wsrc: e.dma_start(
                               out=FSL[sl][:, k2 * 256:(k2 + 1) * 256],
                               in_=wsrc.ap()[ob * 256:(ob + 1) * 256,
                                             k * 128:(k + 1) * 128],
                               transpose=True),
                           waits=(list(start_wait)
                                  + [(sT, kb._ffn_guard[sl]),
                                     (sGst[sl], kb._gst_slot[sl])])
                           if k2 == 0 else [],
                           inc=sem, dma=True)
                    else:
                        op("sy", lambda e, ob=ob, k=k, k2=k2, sl=sl,
                           wT=wT: e.dma_start(
                               out=FSL[sl][:, k2 * 256:(k2 + 1) * 256],
                               in_=wT.ap()[k * 128:(k + 1) * 128,
                                           ob * 256:(ob + 1) * 256]),
                           waits=(list(start_wait)
                                  + [(sT, kb._ffn_guard[sl]),
                                     (sGst[sl], kb._gst_slot[sl])]
                                  + kb._gst_full.get(wkey, []))
                           if k2 == 0 else [],
                           inc=sem, dma=True)
                ldw = sem.at()
                if tt == 0 and wT is not None:
                    for k2 in range(kh_sz):
                        k = kh * kh_sz + k2
                        op("g", lambda e, ob=ob, k=k, k2=k2, sl=sl, wT=wT:
                           e.dma_start(
                               out=wT.ap()[k * 128:(k + 1) * 128,
                                           ob * 256:(ob + 1) * 256],
                               in_=FSL[sl][:, k2 * 256:(k2 + 1) * 256]),
                           waits=[ldw],
                           inc=sGst[sl], dma=True)
                    kb._gst_slot[sl] = sGst[sl].n
                for oc in range(2):
                    bank = mm[(2 * (unit % 2) + oc) % 4] if False else mm[
                        (ob % 2) * 2 + oc]
                    for k2 in range(kh_sz):
                        k = kh * kh_sz + k2
                        op("t", lambda e, b=bank, k2=k2, oc=oc, sl=sl, tt=tt,
                           k=k, ac=act_chunks: e.matmul(
                               b[:], FSL[sl][:, k2 * 256 + oc * 128:
                                             k2 * 256 + oc * 128 + 128],
                               ac(k),
                               start=(k == 0), stop=(k == n_k - 1)),
                           waits=([ldw] if k2 == 0 else [])
                           + ([(sA, sA.n), (sV, sV.n)]
                              if (k == 0 and k2 == 0) else []),
                           inc=sT)
                    if kh == n_kh - 1:
                        evac(bank, ob * 2 + oc)
                kb._ffn_guard[sl] = sT.n
                unit += 1
        if tt == 0 and wT is not None:
            kb._gst_full[wkey] = [(sGst[0], sGst[0].n), (sGst[1], sGst[1].n)]

    kb._ffn_guard = [0, 0]
    kb._gst_slot = [0, 0]
    kb._gst_full = {}

    kb._ffn_guard = [sT.n, sT.n]
    for tt in range(NQB):
        def res_cast(c, tt=tt):
            return RES[:, c * TOK + tt * 512: c * TOK + (tt + 1) * 512]
        for c in range(NCH):
            op("v", lambda e, c=c, rc=res_cast: e.tensor_copy(XB, rc(c)),
               waits=[(sT, sT.n)], inc=sV)
            op("t", lambda e, c=c: e.matmul(
                sc[0][0:1, :], ONC[:], XB, start=(c == 0), stop=(c == NCH - 1)),
               waits=[sV], inc=sT)
            op("a", lambda e, c=c: e.activation(out=SQB[1], in_=XB,
                                                func=AF.Square),
               waits=[(sV, sV.n)], inc=sA)
            op("t", lambda e, c=c: e.matmul(
                sc[1][0:1, :], ONC[:], SQB[1], start=(c == 0),
                stop=(c == NCH - 1)), waits=[sA], inc=sT)
        op("a", lambda e: e.activation(out=SCR[0:1, 0:512], in_=sc[0][0:1, :],
                                       func=AF.Copy, scale=INV_DM),
           waits=[sT], inc=sA)
        op("a", lambda e: e.activation(out=SCR[0:1, 512:1024],
                                       in_=SCR[0:1, 0:512], func=AF.Square),
           waits=[sA.at()], inc=sA)
        op("v", lambda e: e.tensor_scalar(
            out=SCR[0:1, 1024:1536], in0=sc[1][0:1, :], scalar1=INV_DM,
            scalar2=LN_EPS, op0=ALU.mult, op1=ALU.add), waits=[sA], inc=sV)
        op("v", lambda e: e.tensor_sub(SCR[0:1, 1024:1536],
                                       SCR[0:1, 1024:1536],
                                       SCR[0:1, 512:1024]), inc=sV)
        op("v", lambda e: e.reciprocal(SCR[0:1, 1024:1536],
                                       SCR[0:1, 1024:1536]), inc=sV)
        op("a", lambda e: e.activation(out=SCR[0:1, 1024:1536],
                                       in_=SCR[0:1, 1024:1536], func=AF.Sqrt),
           waits=[sV], inc=sA)
        op("v", lambda e: e.tensor_mul(SCR[0:1, 1536:2048], SCR[0:1, 0:512],
                                       SCR[0:1, 1024:1536]), waits=[sA],
           inc=sV)
        op("t", lambda e: e.matmul(sc[0][:], ONRF[:, 0:128],
                                   SCR[0:1, 1024:1536], start=True,
                                   stop=True), waits=[sV], inc=sT)
        op("t", lambda e: e.matmul(sc[1][:], ONRF[:, 0:128],
                                   SCR[0:1, 1536:2048], start=True,
                                   stop=True), inc=sT)
        for c in range(NCH):
            op("v", lambda e, c=c, rc=res_cast: e.tensor_mul(
                SCR[:, 3328:3840], rc(c), sc[0][:]), waits=[sT], inc=sV)
            op("v", lambda e, c=c: e.tensor_sub(
                SCR[:, 3328:3840], SCR[:, 3328:3840], sc[1][:]), inc=sV)
            op("v", lambda e, c=c: e.tensor_copy(XF[:, c, 0:512],
                                                 SCR[:, 3328:3840]), inc=sV)

        def xf_chunk(k):
            return XF[:, k, 0:512]

        def evac_u1(bank, o, tt=tt):
            op("a", lambda e, b=bank, o=o: e.activation(
                out=G_R[:, o, 0:512], in_=b[:], func=AF.Silu,
                scale=FS1[:, o:o + 1]), waits=[sT], inc=sA)
        ffn_mm_phase(w1_bf, 4096, NCH, FS1, evac_u1, tt, xf_chunk,
                     start_wait=[(sPrA, 1280)], wT=w1T)
        def evac_u2(bank, o, tt=tt):
            op("v", lambda e, b=bank, o=o: e.scalar_tensor_tensor(
                out=H_R[:, o, 0:512], in0=b[:], scalar=FS2[:, o:o + 1],
                in1=G_R[:, o, 0:512], op0=ALU.mult, op1=ALU.mult),
               waits=[sT, (sA, sA.n)], inc=sV)
        ffn_mm_phase(w2_bf, 4096, NCH, FS2, evac_u2, tt, xf_chunk,
                     start_wait=[(sPrA, 2304)], wT=w2T)

        def h_chunk(k):
            return H_R[:, k, 0:512]

        def g_chunk(k):
            return G_R[:, k, 0:512]

        def evac_g1(bank, o, tt=tt):
            op("a", lambda e, b=bank, o=o: e.activation(
                out=G_R[:, o, 0:512], in_=b[:], func=AF.Gelu,
                scale=FH1[:, o:o + 1]), waits=[sT, (sV, sV.n)], inc=sA)
        ffn_mm_phase(h1_bf, 4096, 32, FH1, evac_g1, tt, h_chunk,
                     start_wait=[(sPrA, 6400)], wT=h1T)
        def evac_g2(bank, o, tt=tt):
            op("a", lambda e, b=bank, o=o: e.activation(
                out=H_R[:, o, 0:512], in_=b[:], func=AF.Gelu,
                scale=FH2[:, o:o + 1]), waits=[sT, (sV, sV.n)], inc=sA)
        ffn_mm_phase(h2_bf, 4096, 32, FH2, evac_g2, tt, g_chunk,
                     start_wait=[(sPrB, 4096)], wT=h2T)
        def evac_w3(bank, o, tt=tt):
            op("a", lambda e, b=bank, o=o: e.activation(
                out=BOT[:, o, 0:512], in_=b[:], func=AF.Identity,
                scale=FS3[:, o:o + 1], bias=0.0), waits=[sT, (sV, sV.n)],
               inc=sA)
        ffn_mm_phase(w3_bf, DM, 32, FS3, evac_w3, tt, h_chunk,
                     start_wait=[(sPrB, 5120)], wT=w3T)

        def bot_chunk(c):
            return BOT[:, c, 0:512]
        ln_rows_from(bot_chunk, tt, NCH, INV_DM)
        for c in range(NCH):
            op("v", lambda e, c=c, tt=tt: e.tensor_mul(
                SCR[:, 3328:3840], BOT[:, c, 0:512], sc[0][:]),
               waits=[sT], inc=sV)
            op("v", lambda e: e.tensor_sub(SCR[:, 3328:3840],
                                           SCR[:, 3328:3840], sc[1][:]),
               inc=sV)
            op("v", lambda e, c=c, tt=tt: e.tensor_add(
                RES[:, c * TOK + tt * 512: c * TOK + (tt + 1) * 512],
                RES[:, c * TOK + tt * 512: c * TOK + (tt + 1) * 512],
                SCR[:, 3328:3840]), inc=sV)

    Q2 = A1[:, 0:2048]
    KL2 = A1[:, 2048:4352]
    VL2 = A1[:, 4352:6692].rearrange("p (c t) -> p c t", c=18)
    EM1 = A1[:, 6692:9764].rearrange("p (c t) -> p c t", c=6)
    ATT1 = A1[:, 24576:40960].rearrange("p (c t) -> p c t", c=8)
    BO1 = A1[:, 0:16384].rearrange("p (c t) -> p c t", c=8)
    QKVQ = WS[:, 0:2048]
    QKVV = WS[:, 4096:5120]
    CSLD = WS[:, 5120:6144].bitcast(F16)
    xn_dram = nc.dram_tensor("xn_dram", [NCH, 128, TOK], BF16)

    LQB0 = WS[:, 2048:2560]
    LQB1 = WS[:, 2560:3072]
    sXn = kb.sem("sXn")
    sXf = [kb.sem(f"sXf{i}") for i in range(2)]
    sQk = kb.sem("sQk")
    sCs = kb.sem("sCs")
    sBd = kb.sem("sBd")
    sPj = [kb.sem(f"sPj{i}") for i in range(2)]

    for tt in range(NQB):
        def res_cast2(c, tt=tt):
            return RES[:, c * TOK + tt * 512: c * TOK + (tt + 1) * 512]
        for c in range(NCH):
            op("v", lambda e, c=c, rc=res_cast2: e.tensor_copy(XB, rc(c)),
               waits=[(sT, sT.n)] + ([(sXn, sXn.n)] if c == 0 else []),
               inc=sV)
            op("t", lambda e, c=c: e.matmul(
                sc[0][0:1, :], ONC[:], XB, start=(c == 0), stop=(c == NCH - 1)),
               waits=[sV], inc=sT)
            op("a", lambda e, c=c: e.activation(out=SQB[1], in_=XB,
                                                func=AF.Square),
               waits=[(sV, sV.n), (sT, sT.n)], inc=sA)
            op("t", lambda e, c=c: e.matmul(
                sc[1][0:1, :], ONC[:], SQB[1], start=(c == 0),
                stop=(c == NCH - 1)), waits=[sA], inc=sT)
        op("a", lambda e: e.activation(out=SCR[0:1, 0:512], in_=sc[0][0:1, :],
                                       func=AF.Copy, scale=INV_DM),
           waits=[sT], inc=sA)
        op("a", lambda e: e.activation(out=SCR[0:1, 512:1024],
                                       in_=SCR[0:1, 0:512], func=AF.Square),
           waits=[sA.at()], inc=sA)
        op("v", lambda e: e.tensor_scalar(
            out=SCR[0:1, 1024:1536], in0=sc[1][0:1, :], scalar1=INV_DM,
            scalar2=LN_EPS, op0=ALU.mult, op1=ALU.add), waits=[sA], inc=sV)
        op("v", lambda e: e.tensor_sub(SCR[0:1, 1024:1536],
                                       SCR[0:1, 1024:1536],
                                       SCR[0:1, 512:1024]), inc=sV)
        op("v", lambda e: e.reciprocal(SCR[0:1, 1024:1536],
                                       SCR[0:1, 1024:1536]), inc=sV)
        op("a", lambda e: e.activation(out=SCR[0:1, 1024:1536],
                                       in_=SCR[0:1, 1024:1536], func=AF.Sqrt),
           waits=[sV], inc=sA)
        op("v", lambda e: e.tensor_mul(SCR[0:1, 1536:2048], SCR[0:1, 0:512],
                                       SCR[0:1, 1024:1536]), waits=[sA],
           inc=sV)
        op("t", lambda e: e.matmul(sc[0][:], ONRF[:, 0:128],
                                   SCR[0:1, 1024:1536], start=True,
                                   stop=True), waits=[sV], inc=sT)
        op("t", lambda e: e.matmul(sc[1][:], ONRF[:, 0:128],
                                   SCR[0:1, 1536:2048], start=True,
                                   stop=True), inc=sT)
        for c in range(NCH):
            op("v", lambda e, c=c, rc=res_cast2: e.tensor_mul(
                SCR[:, 3328:3840], rc(c), sc[0][:]), waits=[sT], inc=sV)
            op("v", lambda e, c=c: e.tensor_sub(
                SCR[:, 3328:3840], SCR[:, 3328:3840], sc[1][:]), inc=sV)
            op("v", lambda e, c=c: e.tensor_copy(XF[:, c, 0:512],
                                                 SCR[:, 3328:3840]), inc=sV)
            op("sy", lambda e, c=c, tt=tt: e.dma_start(
                out=xn_dram.ap()[c, :, tt * 512:(tt + 1) * 512],
                in_=XF[:, c, 0:512]), waits=[(sV, sV.n)], inc=sXn, dma=True)
    xn_all = sXn.at()

    n_kt1 = 6
    kb._cs_guard = 0
    g_xf1 = [0]
    g_qkv = 0
    for grp in range(8):
        for part, (dst, col0) in enumerate(
                ((QKVQ, grp * 128), (QKVQ, 1024 + grp * 128),
                 (QKVV, 2048 + grp * 128))):
            off = 0 if part < 2 else 0
            for d in range(NCH):
                dsl = dst[:, (part % 2) * 1024 + d * 128:(part % 2) * 1024 + d * 128 + 128] if part < 2 else dst[:, d * 128:(d + 1) * 128]
                op("sy", lambda e, dsl=dsl, col0=col0, d=d: e.dma_start(
                    out=dsl,
                    in_=qkv_bf.ap()[col0:col0 + 128, d * 128:(d + 1) * 128],
                    transpose=True),
                   waits=[(sPrB, prB_qkv_done), (sT, g_qkv),
                          ], inc=sQk, dma=True)
        qkv_ld = sQk.at()

        for tt in range(NQB):
            xsl = sXf[tt % 2]
            for c in range(NCH):
                op("sy", lambda e, c=c, tt=tt: e.dma_start(
                    out=XF[:, c, 0:512],
                    in_=xn_dram.ap()[c, :, tt * 512:(tt + 1) * 512]),
                   waits=[xn_all, (sT, g_xf1[0])], inc=xsl, dma=True)
            xf_ld = xsl.at()
            op("sy", lambda e, tt=tt: e.dma_start(
                out=CSLD[:, 0:512], in_=cosb[:, tt * 512:(tt + 1) * 512]),
               waits=[(sV, kb._cs_guard)], inc=sCs, dma=True)
            op("sy", lambda e, tt=tt: e.dma_start(
                out=CSLD[:, 512:1024], in_=sinb[:, tt * 512:(tt + 1) * 512]),
               inc=sCs, dma=True)
            cs_ld = sCs.at()
            for qk in range(2):
                bank = mm[2 + qk]
                for d in range(NCH):
                    op("t", lambda e, b=bank, qk=qk, d=d, tt=tt: e.matmul(
                        b[:], QKVQ[:, qk * 1024 + d * 128:
                                   qk * 1024 + d * 128 + 128],
                        XF[:, d, 0:512], start=(d == 0), stop=(d == NCH - 1)),
                       waits=[qkv_ld, xf_ld, (sA, sA.n), (sV, sV.n)]
                       if d == 0 else [], inc=sT)
                dst0 = Q2 if qk == 0 else KL2
                colw = tt * 512 if qk == 0 else 256 + tt * 512
                SCRq = SCR[:, 2048:2560]
                SCRt = SCR[:, 2560:3072]
                SCRu = SCR[:, 3072:3584]
                op("a", lambda e, b=bank: e.copy(SCRq, b[:]),
                   waits=[sT, (sPrA, sPrA.n)], inc=sA)
                op("a", lambda e: e.activation(out=SCRt, in_=SCRq,
                                               func=AF.Square),
                   waits=[sA.at()], inc=sA)
                op("v", lambda e: e.tensor_copy(LQB0, SCRq), waits=[sA],
                   inc=sV)
                op("v", lambda e: e.tensor_copy(LQB1, SCRt), inc=sV)
                for hh in range(2):
                    b0 = 64 * hh
                    onc_s = ONC[0:64, :] if b0 == 0 else ONC[64:128, :]
                    op("t", lambda e, b0=b0, onc_s=onc_s: e.matmul(
                        sc[0][0:1, 0:512], onc_s, LQB0[b0:b0 + 64, :],
                        start=True, stop=True), waits=[(sV, sV.n)], inc=sT)
                    op("t", lambda e, b0=b0, onc_s=onc_s: e.matmul(
                        sc[1][0:1, 0:512], onc_s, LQB1[b0:b0 + 64, :],
                        start=True, stop=True), inc=sT)
                    op("a", lambda e: e.activation(
                        out=SCR[0:1, 0:512], in_=sc[0][0:1, :], func=AF.Copy,
                        scale=1.0 / 64), waits=[sT], inc=sA)
                    op("a", lambda e: e.activation(
                        out=SCR[0:1, 512:1024], in_=SCR[0:1, 0:512],
                        func=AF.Square), waits=[sA.at()], inc=sA)
                    op("v", lambda e: e.tensor_scalar(
                        out=SCR[0:1, 1024:1536], in0=sc[1][0:1, :],
                        scalar1=1.0 / 64, scalar2=LN_EPS, op0=ALU.mult,
                        op1=ALU.add), waits=[sA], inc=sV)
                    op("v", lambda e: e.tensor_sub(
                        SCR[0:1, 1024:1536], SCR[0:1, 1024:1536],
                        SCR[0:1, 512:1024]), inc=sV)
                    op("v", lambda e: e.reciprocal(
                        SCR[0:1, 1024:1536], SCR[0:1, 1024:1536]), inc=sV)
                    op("a", lambda e: e.activation(
                        out=SCR[0:1, 1024:1536], in_=SCR[0:1, 1024:1536],
                        func=AF.Sqrt), waits=[sV], inc=sA)
                    op("v", lambda e: e.tensor_mul(
                        SCR[0:1, 1536:2048], SCR[0:1, 0:512],
                        SCR[0:1, 1024:1536]), waits=[sA], inc=sV)
                    op("t", lambda e, b0=b0: e.matmul(
                        bcb[b0:b0 + 64, :][0:64, :] if b0 else bcb[0:64, :],
                        ONRF[:, 0:64], SCR[0:1, 1024:1536],
                        start=True, stop=True, tile_position=(0, b0)),
                       waits=[sV, (sA, sA.n), (sV, sV.n)], inc=sT)
                    op("t", lambda e, b0=b0: e.matmul(
                        mm[0][b0:b0 + 64, :][0:64, :] if b0 else mm[0][0:64, :],
                        ONRF[:, 0:64], SCR[0:1, 1536:2048],
                        start=True, stop=True, tile_position=(0, b0)),
                       inc=sT)
                gcol = QNWB[:, 2 * qk:2 * qk + 1]
                bcol = QNWB[:, 2 * qk + 1:2 * qk + 2]
                op("v", lambda e: e.tensor_mul(SCRt, SCRq, bcb[:]),
                   waits=[sT], inc=sV)
                op("v", lambda e: e.tensor_sub(SCRt, SCRt, mm[0][:]), inc=sV)
                op("a", lambda e, gcol=gcol, bcol=bcol: e.activation(
                    out=SCRq, in_=SCRt, func=AF.Identity, scale=gcol,
                    bias=bcol), waits=[sV], inc=sA)
                op("v", lambda e: e.tensor_copy(LQB1, SCRq), waits=[sA],
                   inc=sV)
                op("t", lambda e: e.matmul(
                    mm[0][:], ROTT[:], LQB1, start=True, stop=True),
                   waits=[sV, (sA, sA.n), (sV, sV.n)], inc=sT)
                op("v", lambda e: e.tensor_mul(
                    SCRt, SCRq, CSLD[:, 0:512]),
                   waits=[(sCs, cs_ld[1])], inc=sV)
                op("v", lambda e: e.tensor_mul(SCRu, mm[0][:],
                                               CSLD[:, 512:1024]),
                   waits=[sT], inc=sV)
                op("v", lambda e, dst0=dst0, colw=colw: e.tensor_add(
                    dst0[:, colw:colw + 512], SCRt, SCRu),
                   waits=[(sT, g_qkv)], inc=sV)
            for sub in range(4):
                bank = mm[1]
                for d in range(NCH):
                    op("t", lambda e, b=bank, sub=sub, d=d: e.matmul(
                        b[:, 0:128], XF[:, d, sub * 128:(sub + 1) * 128],
                        QKVV[:, d * 128:(d + 1) * 128],
                        start=(d == 0), stop=(d == NCH - 1)),
                       waits=[(sV, sV.n)] if d == 0 else [], inc=sT)
                r = 2 + tt * 4 + sub
                op("v", lambda e, b=bank, r=r: e.tensor_copy(
                    VL2[:, r, 0:130].rearrange("p (h z) -> p h z", h=2)[:, :, 0:64],
                    b[:, 0:128].rearrange("p (h d) -> p h d", h=2)),
                   waits=[sT], inc=sV)
            g_xf1[0] = sT.n
            kb._cs_guard = sV.n
        for r in range(18):
            op("v", lambda e, r=r: e.memset(
                VL2[:, r, 0:130].rearrange("p (h z) -> p h z", h=2)[:, :, 64:65],
                1.0), inc=sV)
        op("g", lambda e, grp=grp: e.dma_start(
            out=bdk_in[grp].ap(), in_=KL2[:, 2048:2304]),
           waits=[(sV, sV.n)], inc=sBd, dma=True)
        op("g", lambda e, grp=grp: e.dma_start(
            out=bdv_in[grp].ap()[0:128, :], in_=VL2[:, 16, 0:130]),
           waits=[sBd.at()], inc=sBd, dma=True)
        op("g", lambda e, grp=grp: e.dma_start(
            out=bdv_in[grp].ap()[128:256, :], in_=VL2[:, 17, 0:130]),
           waits=[sBd.at()], inc=sBd, dma=True)
        op("g", lambda e, grp=grp: e.collective_compute(
            "AllGather", ALU.bypass,
            replica_groups=[[0, 1], [2, 3], [4, 5], [6, 7]],
            ins=[bdk_in[grp].ap().opt()], outs=[bdk_out[grp].ap().opt()]),
           waits=[sBd.at()], inc=sG)
        op("g", lambda e, grp=grp: e.collective_compute(
            "AllGather", ALU.bypass,
            replica_groups=[[0, 1], [2, 3], [4, 5], [6, 7]],
            ins=[bdv_in[grp].ap().opt()], outs=[bdv_out[grp].ap().opt()]),
           inc=sG)
        op("g", lambda e, grp=grp: e.dma_start(
            out=KL2[:, 0:256], in_=bdk_out[grp].ap()[0:128, :]),
           waits=[sG.at()], inc=sBd, dma=True)
        op("g", lambda e, grp=grp: e.dma_start(
            out=VL2[:, 0, 0:130], in_=bdv_out[grp].ap()[0:128, :]),
           waits=[sBd.at()], inc=sBd, dma=True)
        op("g", lambda e, grp=grp: e.dma_start(
            out=VL2[:, 1, 0:130], in_=bdv_out[grp].ap()[128:256, :]),
           waits=[sBd.at()], inc=sBd, dma=True)
        bd_done = sBd.at()

        for hh in range(2):
            b0 = 64 * hh
            cnt = 0
            for col0, w in [(0, 512), (512, 512), (1024, 512), (1536, 512)]:
                op("a", lambda e, b0=b0, col0=col0, w=w: e.activation(
                    out=SCR[b0:b0 + 64, 0:512][0:64, 0:w],
                    in_=Q2[b0:b0 + 64, col0:col0 + w], func=AF.Square),
                   waits=[(sV, sV.n)], inc=sA)
                op("v", lambda e, b0=b0, w=w: e.tensor_copy(
                    SQB[0][b0:b0 + 64, 0:w], SCR[b0:b0 + 64, 0:w]),
                   waits=[sA], inc=sV)
                op("t", lambda e, b0=b0, w=w: e.matmul(
                    sc[0][0:1, 0:w], ONC[b0:b0 + 64, :]
                    if b0 == 0 else ONC[64:128, :],
                    SQB[0][b0:b0 + 64, 0:w], start=True, stop=True),
                   waits=[sV], inc=sT)
                op("v", lambda e, hh=hh, cnt=cnt, w=w: e.reduce_max(
                    STT[0:1, 526 + hh * 16 + cnt:527 + hh * 16 + cnt],
                    sc[0][0:1, 0:w], axis=AX.X), waits=[sT], inc=sV)
                cnt += 1
            for col0, w in [(0, 512), (512, 512), (1024, 512), (1536, 512),
                            (2048, 256)]:
                op("a", lambda e, b0=b0, col0=col0, w=w: e.activation(
                    out=SCR[b0:b0 + 64, 0:512][0:64, 0:w],
                    in_=KL2[b0:b0 + 64, col0:col0 + w], func=AF.Square),
                   waits=[(sV, sV.n), bd_done], inc=sA)
                op("v", lambda e, b0=b0, w=w: e.tensor_copy(
                    SQB[0][b0:b0 + 64, 0:w], SCR[b0:b0 + 64, 0:w]),
                   waits=[sA], inc=sV)
                op("t", lambda e, b0=b0, w=w: e.matmul(
                    sc[0][0:1, 0:w], ONC[b0:b0 + 64, :]
                    if b0 == 0 else ONC[64:128, :],
                    SQB[0][b0:b0 + 64, 0:w], start=True, stop=True),
                   waits=[sV], inc=sT)
                op("v", lambda e, hh=hh, cnt=cnt, w=w: e.reduce_max(
                    STT[0:1, 526 + hh * 16 + cnt:527 + hh * 16 + cnt],
                    sc[0][0:1, 0:w], axis=AX.X), waits=[sT], inc=sV)
                cnt += 1
            op("v", lambda e, hh=hh: e.reduce_max(
                STT[0:1, 516 + hh:517 + hh],
                STT[0:1, 526 + hh * 16:526 + hh * 16 + 4], axis=AX.X), inc=sV)
            op("v", lambda e, hh=hh: e.reduce_max(
                STT[0:1, 518 + hh:519 + hh],
                STT[0:1, 526 + hh * 16 + 4:526 + hh * 16 + 9], axis=AX.X),
               inc=sV)
        op("a", lambda e: e.activation(out=STT[0:1, 520:524],
                                       in_=STT[0:1, 516:520], func=AF.Sqrt),
           waits=[sV], inc=sA)
        op("v", lambda e: e.tensor_mul(STT[0:1, 524:526], STT[0:1, 520:522],
                                       STT[0:1, 522:524]), waits=[sA], inc=sV)
        op("v", lambda e: e.tensor_scalar(
            out=STT[0:1, 512:514], in0=STT[0:1, 524:526], scalar1=-0.125,
            scalar2=None, op0=ALU.mult), inc=sV)
        op("v", lambda e, grp=grp: e.tensor_add(
            STT[0:1, 514:516], STT[0:1, 512:514], SNK[:, 2 * grp:2 * grp + 2]),
           inc=sV)
        op("a", lambda e: e.activation(out=STT[0:1, 514:516],
                                       in_=STT[0:1, 514:516], func=AF.Exp),
           waits=[sV], inc=sA)
        op("t", lambda e: e.matmul(bcb[:, 0:2], ONRF[:, 0:128],
                                   STT[0:1, 512:514], start=True, stop=True),
           waits=[sV, sA, (sA, g_negg), (sV, sV.n)], inc=sT)
        op("t", lambda e: e.matmul(bcb[:, 2:4], ONRF[:, 0:128],
                                   STT[0:1, 514:516], start=True, stop=True),
           inc=sT)
        op("a", lambda e, grp=grp: e.copy(NEGG[:, 0:2], bcb[:, 0:2]),
           waits=[sT], inc=sA)
        op("a", lambda e: e.copy(ESNK[:], bcb[:, 2:4]), waits=[sA.at()],
           inc=sA)
        g_negg = sA.n

        for j in range(NQB):
            op("t", lambda e, j=j: e.matmul(
                bcb[:], ONRH[:, 0:128], QIX[:, j * 512:(j + 1) * 512],
                start=True, stop=True), waits=[sA, sV], inc=sT)
            qidx_done = sT.n
            for kt in range(n_kt1):
                vchunk = 4 * j + kt
                msl = EM1[:, 4 + (kt % 2), 0:512]
                op("v", lambda e, vchunk=vchunk: e.tensor_scalar(
                    out=SCR[:, 2048:2560], in0=bcb[:],
                    scalar1=KIX1[:, vchunk:vchunk + 1], scalar2=None,
                    op0=ALU.is_ge), waits=[(sT, qidx_done)], inc=sV)
                op("v", lambda e, vchunk=vchunk: e.tensor_scalar(
                    out=SCR[:, 2560:3072], in0=bcb[:],
                    scalar1=KIX1[:, 18 + vchunk:19 + vchunk], scalar2=None,
                    op0=ALU.is_le), inc=sV)
                op("v", lambda e, msl=msl: e.tensor_mul(
                    msl, SCR[:, 2048:2560], SCR[:, 2560:3072]), inc=sV)
                for sub in range(2):
                    b0 = 64 * sub
                    esl = EM1[:, sub, 0:512]
                    op("t", lambda e, b0=b0, j=j, kt=kt, sub=sub: e.matmul(
                        sc[sub][:],
                        KL2[b0:b0 + 64, 512 * j + 128 * kt:
                            512 * j + 128 * kt + 128],
                        Q2[b0:b0 + 64, j * 512:(j + 1) * 512],
                        start=True, stop=True, tile_position=(b0, 0)),
                       waits=[(sA, sA.n)], inc=sT)
                    op("a", lambda e, sub=sub, esl=esl: e.activation(
                        out=esl, in_=sc[sub][:], func=AF.Exp, scale=0.125,
                        bias=NEGG[:, sub:sub + 1]), waits=[sT], inc=sA)
                    op("v", lambda e, esl=esl, msl=msl: e.tensor_mul(
                        esl, esl, msl), waits=[sA], inc=sV)
                    op("t", lambda e, sub=sub, kt=kt, j=j, esl=esl: e.matmul(
                        mm[sub][0:65, :],
                        VL2[:, 4 * j + kt, sub * 65: sub * 65 + 65],
                        esl, start=(kt == 0), stop=(kt == n_kt1 - 1)),
                       waits=[sV] + ([(sA, sA.n)] if kt == 0 else []),
                       inc=sT)
            for hh in range(2):
                head = 2 * grp + hh
                op("a", lambda e, hh=hh: e.copy(SCR[0:65, 0:512],
                                                mm[hh][0:65, :]),
                   waits=[sT, (sV, sV.n)], inc=sA)
                op("v", lambda e, hh=hh: e.tensor_scalar(
                    out=SCR[64:65, 0:512], in0=SCR[64:65, 0:512],
                    scalar1=ESNK[64:65, hh:hh + 1], scalar2=None,
                    op0=ALU.add), waits=[sA], inc=sV)
                op("v", lambda e: e.reciprocal(SCR[64:65, 512:1024],
                                               SCR[64:65, 0:512]),
                   inc=sV)
                op("t", lambda e: e.matmul(bcb[0:64, :], ONFF[64:65, 0:64],
                                           SCR[64:65, 512:1024],
                                           start=True, stop=True,
                                           tile_position=(64, 0)),
                   waits=[sV], inc=sT)
                dch, dbase = head // 2, 64 * (head % 2)
                if dbase == 0:
                    op("v", lambda e, dch=dch, j=j: e.tensor_mul(
                        ATT1[0:64, dch, j * 512:(j + 1) * 512],
                        SCR[0:64, 0:512], bcb[0:64, :]), waits=[sT], inc=sV)
                else:
                    oslot = kb._oddslot
                    osem = sGAtt[oslot]
                    op("v", lambda e, oslot=oslot: e.tensor_mul(
                        EM1[0:64, 2 + oslot, 0:512], SCR[0:64, 0:512],
                        bcb[0:64, :]), waits=[sT, osem.at()], inc=sV)
                    op("g", lambda e, oslot=oslot, dch=dch, j=j: e.dma_start(
                        out=ATT1[64:128, dch, j * 512:(j + 1) * 512],
                        in_=EM1[0:64, 2 + oslot, 0:512]),
                       waits=[(sV, sV.n)], inc=osem, dma=True)
                    kb._oddslot = 1 - oslot
        g_qkv = sT.n
        g_kv2 = sT.n

    attn1_done = [(sT, sT.n), (sA, sA.n), (sV, sV.n),
                  sGAtt[0].at(), sGAtt[1].at()]
    pj_guard = 0
    for oh in range(2):
        spj = sPj[oh]
        for d in range(8):
            woff = d * 512 if d < 4 else 4096 + (d - 4) * 512
            op("sy", lambda e, oh=oh, d=d, woff=woff: e.dma_start(
                out=WS[:, woff:woff + 512],
                in_=proj_bf.ap()[oh * 512:(oh + 1) * 512,
                                 d * 128:(d + 1) * 128],
                transpose=True),
               waits=[(sPrB, prB_all)] + attn1_done
               + ([(sT, pj_guard)] if oh == 1 else []),
               inc=spj, dma=True)
        for oc in range(4):
            o = oh * 4 + oc
            for tt in range(NQB):
                bank = mm[tt % 2]
                for d in range(8):
                    woff = d * 512 if d < 4 else 4096 + (d - 4) * 512
                    op("t", lambda e, b=bank, d=d, oc=oc, tt=tt, woff=woff:
                       e.matmul(
                           b[:], WS[:, woff + oc * 128: woff + oc * 128 + 128],
                           ATT1[:, d, tt * 512:(tt + 1) * 512],
                           start=(d == 0), stop=(d == 7)),
                       waits=([(spj, spj.n), (sV, sV.n), (sA, sA.n)]
                              + attn1_done if d == 0 else []),
                       inc=sT if d == 7 else None)
                op("a", lambda e, b=bank, o=o, tt=tt: e.activation(
                    out=BO1[:, o, tt * 512:(tt + 1) * 512], in_=b[:],
                    func=AF.Identity, bias=PBC[:, o:o + 1], scale=1.0),
                   waits=[sT], inc=sA)
        if oh == 0:
            pj_guard = sT.n
    post_ln_residual(BO1)

    QMAXI = 1.0 / 32767.0
    XQ = A1[:, 0:16384].rearrange("p (c t) -> p c t", c=8)
    BOTF = A1[:, 0:16384].rearrange("p (c t) -> p c t", c=8)
    TG = A1[:, 16384:28672].rearrange("p (c t) -> p c t", c=24)
    TU = A1[:, 28672:40960].rearrange("p (c t) -> p c t", c=24)
    WQO = A2[:, 0:1024]
    sWq = kb.sem("sWq")
    sWqS = kb.sem("sWqS")
    sSp = kb.sem("sSp")
    sSpH = [kb.sem(f"sSpH{i}") for i in range(2)]
    sRl = [kb.sem(f"sRl{i}") for i in range(2)]
    sMx = kb.sem("sMx")
    sCc = kb.sem("sCc")

    ffn1_begin = [(sT, sT.n), (sA, sA.n), (sV, sV.n)]

    WMX = SCR[:, 1024:1056]
    WSC = SCR[:, 1056:1088]
    WRC = SCR[:, 1088:1120]
    WF = SCR[:, 2048:3072]
    wprep_done = {}

    def lpbq_prep(src, dst, nrow, ncb, eng):
        for i in range(nrow // 128):
            for cb in range(ncb):
                op("sy", lambda e, src=src, i=i, cb=cb: e.dma_start(
                    out=SCR[:, 0:1024],
                    in_=src[i * 128:(i + 1) * 128, cb * 1024:(cb + 1) * 1024]),
                   waits=ffn1_begin + [(kb.self_sems[eng], kb._wp_guard)],
                   inc=sWq, dma=True)
                ldw = sWq.at()
                op(eng, lambda e: e.tensor_copy(WF, SCR[:, 0:1024]),
                   waits=[ldw], inc=None)
                op(eng, lambda e: e.reduce_max(
                    WMX, WF.rearrange("p (g i) -> p g i", i=32),
                    axis=AX.X, apply_absolute_value=True), inc=None)
                op(eng, lambda e: e.tensor_scalar(
                    out=WSC, in0=WMX, scalar1=1.0 / 7.0, scalar2=1e-12,
                    op0=ALU.mult, op1=ALU.max), inc=None)
                op(eng, lambda e: e.reciprocal(WRC, WSC), inc=None)
                op(eng, lambda e: e.tensor_tensor(
                    WF.rearrange("p (g i) -> p g i", i=32), 
                    WF.rearrange("p (g i) -> p g i", i=32),
                    WRC[:, :, None].broadcast_to([128, 32, 32]),
                    op=ALU.mult), inc=None)
                op(eng, lambda e: e.tensor_scalar(
                    out=WF, in0=WF, scalar1=MAGIC, scalar2=MAGIC,
                    op0=ALU.add, op1=ALU.subtract), inc=None)
                op(eng, lambda e: e.tensor_scalar(
                    out=WF, in0=WF, scalar1=7.0, scalar2=-8.0,
                    op0=ALU.min, op1=ALU.max), inc=None)
                op(eng, lambda e: e.tensor_tensor(
                    WQO.rearrange("p (g i) -> p g i", i=32),
                    WF.rearrange("p (g i) -> p g i", i=32),
                    WSC[:, :, None].broadcast_to([128, 32, 32]),
                    op=ALU.mult), waits=[(sWqS, sWqS.n)], inc=None)
                kb._wp_guard = kb.self_sems[eng].n
                op("g", lambda e, dst=dst, i=i, cb=cb: e.dma_start(
                    out=dst.ap()[i * 128:(i + 1) * 128,
                                 cb * 1024:(cb + 1) * 1024],
                    in_=WQO),
                   waits=[(kb.self_sems[eng], kb._wp_guard)], inc=sWqS,
                   dma=True)

    kb._wp_guard = 0
    lpbq_prep(w_gate, wg_bf, 3072, 1, "v")
    wprep_done["g"] = sWqS.at()
    lpbq_prep(w_up, wu_bf, 3072, 1, "v")
    wprep_done["u"] = sWqS.at()
    lpbq_prep(w_down, wd_bf, 1024, 3, "v")
    wprep_done["d"] = sWqS.at()

    MAXC = CW[:, 24:25]
    op("v", lambda e: e.memset(MAXC, 0.0), inc=sV)
    op("v", lambda e: e.memset(STT[0:1, 560:566], 0.0), inc=sV)
    for tt in range(NQB):
        def res_c3(c, tt=tt):
            return RES[:, c * TOK + tt * 512: c * TOK + (tt + 1) * 512]
        for c in range(NCH):
            op("v", lambda e, c=c, rc=res_c3: e.tensor_copy(XB, rc(c)),
               waits=[(sT, sT.n), (sSp, sSp.n)], inc=sV)
            op("t", lambda e, c=c: e.matmul(
                sc[0][0:1, :], ONC[:], XB, start=(c == 0), stop=(c == NCH - 1)),
               waits=[sV], inc=sT)
            op("a", lambda e, c=c: e.activation(out=SQB[1], in_=XB,
                                                func=AF.Square),
               waits=[(sV, sV.n), (sT, sT.n)], inc=sA)
            op("t", lambda e, c=c: e.matmul(
                sc[1][0:1, :], ONC[:], SQB[1], start=(c == 0),
                stop=(c == NCH - 1)), waits=[sA], inc=sT)
        op("a", lambda e: e.activation(out=SCR[0:1, 0:512], in_=sc[0][0:1, :],
                                       func=AF.Copy, scale=INV_DM),
           waits=[sT], inc=sA)
        op("a", lambda e: e.activation(out=SCR[0:1, 512:1024],
                                       in_=SCR[0:1, 0:512], func=AF.Square),
           waits=[sA.at()], inc=sA)
        op("v", lambda e: e.tensor_scalar(
            out=SCR[0:1, 1024:1536], in0=sc[1][0:1, :], scalar1=INV_DM,
            scalar2=LN_EPS, op0=ALU.mult, op1=ALU.add), waits=[sA], inc=sV)
        op("v", lambda e: e.tensor_sub(SCR[0:1, 1024:1536],
                                       SCR[0:1, 1024:1536],
                                       SCR[0:1, 512:1024]), inc=sV)
        op("v", lambda e: e.reciprocal(SCR[0:1, 1024:1536],
                                       SCR[0:1, 1024:1536]), inc=sV)
        op("a", lambda e: e.activation(out=SCR[0:1, 1024:1536],
                                       in_=SCR[0:1, 1024:1536], func=AF.Sqrt),
           waits=[sV], inc=sA)
        op("v", lambda e: e.tensor_mul(SCR[0:1, 1536:2048], SCR[0:1, 0:512],
                                       SCR[0:1, 1024:1536]), waits=[sA],
           inc=sV)
        op("t", lambda e: e.matmul(sc[0][:], ONRF[:, 0:128],
                                   SCR[0:1, 1024:1536], start=True,
                                   stop=True), waits=[sV], inc=sT)
        op("t", lambda e: e.matmul(sc[1][:], ONRF[:, 0:128],
                                   SCR[0:1, 1536:2048], start=True,
                                   stop=True), inc=sT)
        for c in range(NCH):
            op("v", lambda e, c=c, rc=res_c3: e.tensor_mul(
                SCR[:, 3328:3840], rc(c), sc[0][:]), waits=[sT], inc=sV)
            op("v", lambda e, c=c: e.tensor_sub(
                SCR[:, 3328:3840], SCR[:, 3328:3840], sc[1][:]), inc=sV)
            op("v", lambda e, c=c, tt=tt: e.tensor_copy(XQ[:, c, tt * 512:
                                                           (tt + 1) * 512],
                                                        SCR[:, 3328:3840]),
               inc=sV)
            op("v", lambda e: e.reduce_max(CW[:, 25:26], SCR[:, 3328:3840],
                                           axis=AX.X,
                                           apply_absolute_value=True), inc=sV)
            op("v", lambda e: e.tensor_max(MAXC, MAXC, CW[:, 25:26]), inc=sV)

    def finish_max(dst_col, src_col=None):
        op("t", lambda e: e.transpose(bcb[0:1, 0:128].bitcast(F32)
                                      if False else bcb[0:128, 0:128][0:1, 0:128],
                                      MAXC, IDF[:])
           if False else e.transpose(bcb[0:1, 0:128], MAXC, IDF[:]),
           waits=[sV, (sA, sA.n)], inc=sT)
        op("v", lambda e, dc=dst_col: e.reduce_max(
            STT[0:1, 560 + dc:561 + dc], bcb[0:1, 0:128], axis=AX.X),
           waits=[sT], inc=sV)

    finish_max(0)
    op("sy", lambda e: e.dma_start(out=mx_in[0].ap(),
                                   in_=STT[0:1, 560:562]),
       waits=[(sV, sV.n)], inc=sMx, dma=True)
    op("g", lambda e: e.collective_compute(
        "AllReduce", ALU.max, replica_groups=[list(range(8))],
        ins=[mx_in[0].ap().opt()], outs=[mx_out[0].ap().opt()]),
       waits=[sMx.at()], inc=sCc)
    op("sy", lambda e: e.dma_start(out=STT[0:1, 560:562],
                                   in_=mx_out[0].ap()),
       waits=[(sCc, sCc.n)], inc=sMx, dma=True)
    mx1 = sMx.at()

    def scale_cols(src_off, inv_qmax, sx_col, rx_col, extra_wait=()):
        op("v", lambda e, so=src_off, iq=inv_qmax: e.tensor_scalar(
            out=STT[0:1, 564:565], in0=STT[0:1, 560 + so:561 + so],
            scalar1=iq, scalar2=1e-12, op0=ALU.mult, op1=ALU.max),
           waits=list(extra_wait) + [(sT, sT.n)], inc=sV)
        op("v", lambda e: e.reciprocal(STT[0:1, 565:566], STT[0:1, 564:565]),
           inc=sV)
        op("t", lambda e: e.matmul(bcb[:, 0:2], ONRF[:, 0:128],
                                   STT[0:1, 564:566], start=True, stop=True),
           waits=[sV, (sA, sA.n)], inc=sT)
        op("a", lambda e, sx=sx_col, rx=rx_col: e.copy(sx, bcb[:, 0:1]),
           waits=[sT], inc=sA)
        op("a", lambda e, sx=sx_col, rx=rx_col: e.copy(rx, bcb[:, 1:2]),
           waits=[sA.at()], inc=sA)

    SXC, RXC = CW[:, 26:27], CW[:, 27:28]
    SGC, RGC = CW[:, 28:29], CW[:, 29:30]
    SUC, RUC = CW[:, 30:31], CW[:, 31:32]
    S2C, R2C = CW[:, 32:33], CW[:, 33:34]
    SOC, ROC = CW[:, 34:35], CW[:, 35:36]
    scale_cols(0, QMAXI, SXC, RXC, extra_wait=[mx1])
    for tt in range(NQB):
        for c in range(NCH):
            sl = XQ[:, c, tt * 512:(tt + 1) * 512]
            op("v", lambda e, sl=sl: e.tensor_scalar(
                out=SCR[:, 3328:3840], in0=sl, scalar1=RXC, scalar2=MAGIC,
                op0=ALU.mult, op1=ALU.add), waits=[(sA, sA.n)], inc=sV)
            op("v", lambda e: e.tensor_scalar(
                out=SCR[:, 3328:3840], in0=SCR[:, 3328:3840], scalar1=MAGIC,
                scalar2=32767.0, op0=ALU.subtract, op1=ALU.min), inc=sV)
            op("v", lambda e, sl=sl: e.tensor_scalar(
                out=sl, in0=SCR[:, 3328:3840], scalar1=-32768.0, scalar2=SXC,
                op0=ALU.max, op1=ALU.mult), inc=sV)

    if dbg_outs:
        xq_dbg = nc.dram_tensor("xq_dbg", [NCH, 128, TOK], BF16,
                                kind="ExternalOutput")
        for c in range(NCH):
            op("sy", lambda e, c=c: e.dma_start(
                out=xq_dbg.ap()[c], in_=XQ[:, c, :]),
               waits=[(sV, sV.n)], inc=sMx, dma=True)

    def ffn1_mm(wsrc, spill, wready, track_max):
        if track_max:
            op("v", lambda e: e.memset(MAXC, 0.0), inc=sV)
        unit = 0
        for ob in range(12):
            for tt in range(NQB):
                sl = unit % 2
                if tt == 0:
                    for k in range(NCH):
                        woff = (k % 4) * 512 if k < 4 else 4096 + (k - 8 % 4) * 512
                        woff = (k * 512) if k < 4 else 4096 + (k - 4) * 512
                        op("sy", lambda e, ob=ob, k=k, woff=woff, wsrc=wsrc:
                           e.dma_start(
                               out=WS[:, woff:woff + 256],
                               in_=wsrc.ap()[ob * 256:(ob + 1) * 256,
                                             k * 128:(k + 1) * 128],
                               transpose=True),
                           waits=([wready] if (ob == 0 and k == 0) else [])
                           + ([(sT, kb._f1_guard)] if k == 0 else []),
                           inc=sWq, dma=True)
                    kb._f1_ld = sWq.at()
                for oc in range(2):
                    bank = mm[(ob % 2) * 2 + oc]
                    for k in range(NCH):
                        woff = (k * 512) if k < 4 else 4096 + (k - 4) * 512
                        op("t", lambda e, b=bank, k=k, oc=oc, tt=tt,
                           woff=woff: e.matmul(
                               b[:], WS[:, woff + oc * 128:
                                        woff + oc * 128 + 128],
                               XQ[:, k, tt * 512:(tt + 1) * 512],
                               start=(k == 0), stop=(k == NCH - 1)),
                           waits=([kb._f1_ld] if k == 0 else [])
                           + ([(sA, sA.n), (sV, sV.n)] if k == 0 else []),
                           inc=sT)
                    o = ob * 2 + oc
                    op("a", lambda e, b=bank, oc=oc: e.copy(SQB[oc], b[:]),
                       waits=[sT], inc=sA)
                    if track_max:
                        op("v", lambda e, oc=oc: e.reduce_max(
                            CW[:, 25:26], SQB[oc], axis=AX.X,
                            apply_absolute_value=True), waits=[sA], inc=sV)
                        op("v", lambda e: e.tensor_max(MAXC, MAXC,
                                                       CW[:, 25:26]), inc=sV)
                    op("v", lambda e, oc=oc: e.tensor_copy(
                        WQO[:, oc * 512:(oc + 1) * 512], SQB[oc]),
                       waits=[sA, (sWqS, sWqS.n),
                              (sSpH[oc], kb._sp_guard[oc])],
                       inc=sV)
                    op("sy", lambda e, o=o, tt=tt, spill=spill, oc=oc:
                       e.dma_start(
                           out=spill.ap()[o, :, tt * 512:(tt + 1) * 512],
                           in_=WQO[:, oc * 512:(oc + 1) * 512]),
                       waits=[(sV, sV.n)], inc=sSpH[oc], dma=True)
                    kb._sp_guard[oc] = sSpH[oc].n
                kb._f1_guard = sT.n
                unit += 1

    kb._f1_guard = 0
    kb._fd_guard = [0, 0]
    kb._sp_guard = [0, 0]
    kb._f1_ld = (sWq, 0)
    ffn1_mm(wg_bf, gbuf, wprep_done["g"], True)
    finish_max(1)
    ffn1_mm(wu_bf, ubuf, wprep_done["u"], True)
    finish_max(2)
    op("sy", lambda e: e.dma_start(out=mx_in[1].ap(),
                                   in_=STT[0:1, 561:563]),
       waits=[(sV, sV.n)], inc=sMx, dma=True)
    op("g", lambda e: e.collective_compute(
        "AllReduce", ALU.max, replica_groups=[list(range(8))],
        ins=[mx_in[1].ap().opt()], outs=[mx_out[1].ap().opt()]),
       waits=[sMx.at()], inc=sCc)
    op("sy", lambda e: e.dma_start(out=STT[0:1, 561:563],
                                   in_=mx_out[1].ap()),
       waits=[(sCc, sCc.n)], inc=sMx, dma=True)
    mx2 = sMx.at()
    scale_cols(1, QMAXI, SGC, RGC, extra_wait=[mx2])
    scale_cols(2, QMAXI, SUC, RUC)

    op("v", lambda e: e.memset(MAXC, 0.0), waits=[(sA, sA.n)], inc=sV)
    g_rel = [0, 0]
    for oc in range(24):
        for tt in range(NQB):
            u = (oc * NQB + tt) % 2
            srel = sRl[u]
            tg = TG[:, u, 0:512] if False else TG[:, u, 0:512]
            op("sy", lambda e, oc=oc, tt=tt, u=u: e.dma_start(
                out=TG[:, u, 0:512],
                in_=gbuf.ap()[oc, :, tt * 512:(tt + 1) * 512]),
               waits=[(sSp, sSp.n), sSpH[0].at(), sSpH[1].at(),
                      (sV, g_rel[u])], inc=srel, dma=True)
            rl = srel.at()
            op("v", lambda e, u=u: e.tensor_scalar(
                out=SCR[:, 2048:2560], in0=TG[:, u, 0:512], scalar1=RGC,
                scalar2=MAGIC, op0=ALU.mult, op1=ALU.add),
               waits=[rl, (sA, sA.n)], inc=sV)
            op("v", lambda e: e.tensor_scalar(
                out=SCR[:, 2048:2560], in0=SCR[:, 2048:2560], scalar1=MAGIC,
                scalar2=32767.0, op0=ALU.subtract, op1=ALU.min), inc=sV)
            op("v", lambda e: e.tensor_scalar(
                out=SCR[:, 2048:2560], in0=SCR[:, 2048:2560],
                scalar1=-32768.0, scalar2=SGC, op0=ALU.max, op1=ALU.mult),
               inc=sV)
            op("a", lambda e: e.activation(out=SCR[:, 2560:3072],
                                           in_=SCR[:, 2048:2560],
                                           func=AF.Sigmoid),
               waits=[(sV, sV.n)], inc=sA)
            op("v", lambda e: e.tensor_scalar(
                out=SCR[:, 2560:3072], in0=SCR[:, 2560:3072], scalar1=65536.0,
                scalar2=MAGIC, op0=ALU.mult, op1=ALU.add), waits=[sA], inc=sV)
            op("v", lambda e: e.tensor_scalar(
                out=SCR[:, 2560:3072], in0=SCR[:, 2560:3072], scalar1=MAGIC,
                scalar2=65535.0, op0=ALU.subtract, op1=ALU.min), inc=sV)
            op("v", lambda e: e.tensor_scalar(
                out=SCR[:, 2560:3072], in0=SCR[:, 2560:3072], scalar1=0.0,
                scalar2=1.0 / 65536.0, op0=ALU.max, op1=ALU.mult), inc=sV)
            op("v", lambda e, u=u: e.tensor_mul(
                TG[:, u, 0:512], SCR[:, 2048:2560], SCR[:, 2560:3072]),
               inc=sV)
            op("v", lambda e, u=u: e.reduce_max(
                CW[:, 25:26], TG[:, u, 0:512], axis=AX.X,
                apply_absolute_value=True), inc=sV)
            op("v", lambda e: e.tensor_max(MAXC, MAXC, CW[:, 25:26]), inc=sV)
            g_rel[u] = sV.n
            op("sy", lambda e, oc=oc, tt=tt, u=u: e.dma_start(
                out=g2buf.ap()[oc, :, tt * 512:(tt + 1) * 512],
                in_=TG[:, u, 0:512]),
               waits=[(sV, sV.n)], inc=sSp, dma=True)
    finish_max(3)
    op("sy", lambda e: e.dma_start(out=mx_in[2].ap(),
                                   in_=STT[0:1, 563:565])
       if False else e.dma_start(out=mx_in[2].ap(), in_=STT[0:1, 563:565]),
       waits=[(sV, sV.n)], inc=sMx, dma=True)
    op("g", lambda e: e.collective_compute(
        "AllReduce", ALU.max, replica_groups=[list(range(8))],
        ins=[mx_in[2].ap().opt()], outs=[mx_out[2].ap().opt()]),
       waits=[sMx.at()], inc=sCc)
    op("sy", lambda e: e.dma_start(out=STT[0:1, 563:565],
                                   in_=mx_out[2].ap()),
       waits=[(sCc, sCc.n)], inc=sMx, dma=True)
    mx3 = sMx.at()
    scale_cols(3, QMAXI, S2C, R2C, extra_wait=[mx3])

    op("v", lambda e: e.memset(MAXC, 0.0), waits=[(sA, sA.n)], inc=sV)
    o_rel = [0, 0]
    for oc in range(24):
        for tt in range(NQB):
            u = (oc * NQB + tt) % 2
            srel = sRl[u]
            op("sy", lambda e, oc=oc, tt=tt, u=u: e.dma_start(
                out=TG[:, u, 0:512],
                in_=g2buf.ap()[oc, :, tt * 512:(tt + 1) * 512]),
               waits=[(sSp, sSp.n), sSpH[0].at(), sSpH[1].at(),
                      (sV, o_rel[u])], inc=srel, dma=True)
            op("sy", lambda e, oc=oc, tt=tt, u=u: e.dma_start(
                out=TU[:, u, 0:512],
                in_=ubuf.ap()[oc, :, tt * 512:(tt + 1) * 512]),
               inc=srel, dma=True)
            rl = srel.at()
            op("v", lambda e, u=u: e.tensor_scalar(
                out=SCR[:, 2048:2560], in0=TG[:, u, 0:512], scalar1=R2C,
                scalar2=MAGIC, op0=ALU.mult, op1=ALU.add),
               waits=[rl, (sA, sA.n)], inc=sV)
            op("v", lambda e: e.tensor_scalar(
                out=SCR[:, 2048:2560], in0=SCR[:, 2048:2560], scalar1=MAGIC,
                scalar2=32767.0, op0=ALU.subtract, op1=ALU.min), inc=sV)
            op("v", lambda e: e.tensor_scalar(
                out=SCR[:, 2048:2560], in0=SCR[:, 2048:2560],
                scalar1=-32768.0, scalar2=S2C, op0=ALU.max, op1=ALU.mult),
               inc=sV)
            op("v", lambda e, u=u: e.tensor_scalar(
                out=SCR[:, 2560:3072], in0=TU[:, u, 0:512], scalar1=RUC,
                scalar2=MAGIC, op0=ALU.mult, op1=ALU.add), inc=sV)
            op("v", lambda e: e.tensor_scalar(
                out=SCR[:, 2560:3072], in0=SCR[:, 2560:3072], scalar1=MAGIC,
                scalar2=32767.0, op0=ALU.subtract, op1=ALU.min), inc=sV)
            op("v", lambda e: e.tensor_scalar(
                out=SCR[:, 2560:3072], in0=SCR[:, 2560:3072],
                scalar1=-32768.0, scalar2=SUC, op0=ALU.max, op1=ALU.mult),
               inc=sV)
            op("v", lambda e, u=u: e.tensor_mul(
                TG[:, u, 0:512], SCR[:, 2048:2560], SCR[:, 2560:3072]),
               inc=sV)
            op("v", lambda e, u=u: e.reduce_max(
                CW[:, 25:26], TG[:, u, 0:512], axis=AX.X,
                apply_absolute_value=True), inc=sV)
            op("v", lambda e: e.tensor_max(MAXC, MAXC, CW[:, 25:26]), inc=sV)
            o_rel[u] = sV.n
            op("sy", lambda e, oc=oc, tt=tt, u=u: e.dma_start(
                out=obuf.ap()[oc, :, tt * 512:(tt + 1) * 512],
                in_=TG[:, u, 0:512]),
               waits=[(sV, sV.n)], inc=sSp, dma=True)
    finish_max(4)
    op("sy", lambda e: e.dma_start(out=mx_in[3].ap(), in_=STT[0:1, 564:566]),
       waits=[(sV, sV.n)], inc=sMx, dma=True)
    op("g", lambda e: e.collective_compute(
        "AllReduce", ALU.max, replica_groups=[list(range(8))],
        ins=[mx_in[3].ap().opt()], outs=[mx_out[3].ap().opt()]),
       waits=[sMx.at()], inc=sCc)
    op("sy", lambda e: e.dma_start(out=STT[0:1, 564:566],
                                   in_=mx_out[3].ap()),
       waits=[(sCc, sCc.n)], inc=sMx, dma=True)
    mx4 = sMx.at()
    scale_cols(4, QMAXI, SOC, ROC, extra_wait=[mx4])

    o_rel2 = 0
    for tt in range(NQB):
        for oc in range(24):
            op("sy", lambda e, oc=oc, tt=tt: e.dma_start(
                out=TU[:, oc, 0:512],
                in_=obuf.ap()[oc, :, tt * 512:(tt + 1) * 512]),
               waits=[(sSp, sSp.n), sSpH[0].at(), sSpH[1].at(),
                      (sT, o_rel2)], inc=sRl[0], dma=True)
        rl = sRl[0].at()
        for oc in range(24):
            op("v", lambda e, oc=oc: e.tensor_scalar(
                out=SCR[:, 2048:2560], in0=TU[:, oc, 0:512], scalar1=ROC,
                scalar2=MAGIC, op0=ALU.mult, op1=ALU.add),
               waits=[rl, (sA, sA.n)] if oc == 0 else [], inc=sV)
            op("v", lambda e: e.tensor_scalar(
                out=SCR[:, 2048:2560], in0=SCR[:, 2048:2560], scalar1=MAGIC,
                scalar2=32767.0, op0=ALU.subtract, op1=ALU.min), inc=sV)
            op("v", lambda e, oc=oc: e.tensor_scalar(
                out=TU[:, oc, 0:512], in0=SCR[:, 2048:2560],
                scalar1=-32768.0, scalar2=SOC, op0=ALU.max, op1=ALU.mult),
               inc=sV)
        for ob in range(4):
            for kh in range(2):
                for k2 in range(12):
                    k = kh * 12 + k2
                    woff = k2 * 256 if k2 < 8 else 4096 + (k2 - 8) * 256
                    op("sy", lambda e, ob=ob, k=k, woff=woff, k2=k2:
                       e.dma_start(
                           out=WS[:, woff:woff + 256],
                           in_=wd_bf.ap()[ob * 256:(ob + 1) * 256,
                                          k * 128:(k + 1) * 128],
                           transpose=True),
                       waits=([wprep_done["d"]] if (tt == 0 and ob == 0
                                                    and k == 0) else [])
                       + ([(sT, kb._fd_guard[0])] if k2 == 0 else []),
                       inc=sWq, dma=True)
                ldw2 = sWq.at()
                for oc2 in range(2):
                    bank = mm[(ob % 2) * 2 + oc2]
                    for k2 in range(12):
                        k = kh * 12 + k2
                        woff = k2 * 256 if k2 < 8 else 4096 + (k2 - 8) * 256
                        op("t", lambda e, b=bank, k=k, k2=k2, oc2=oc2,
                           tt=tt, woff=woff: e.matmul(
                               b[:], WS[:, woff + oc2 * 128:
                                        woff + oc2 * 128 + 128],
                               TU[:, k, 0:512],
                               start=(k == 0), stop=(k == 23)),
                           waits=([ldw2] if k2 == 0 else [])
                           + ([(sA, sA.n), (sV, sV.n)]
                              if (k == 0 and k2 == 0) else []),
                           inc=sT)
                    if kh == 1:
                        o = ob * 2 + oc2
                        op("a", lambda e, b=bank, o=o, tt=tt: e.copy(
                            BOTF[:, o, tt * 512:(tt + 1) * 512], b[:]),
                           waits=[sT], inc=sA)
                kb._fd_guard[0] = sT.n
        o_rel2 = sT.n
        def bot3(c, tt=tt):
            return BOTF[:, c, tt * 512:(tt + 1) * 512]
        ln_rows_from(bot3, tt, NCH, INV_DM)
        for c in range(NCH):
            op("v", lambda e, c=c, tt=tt: e.tensor_mul(
                SCR[:, 3328:3840], BOTF[:, c, tt * 512:(tt + 1) * 512],
                sc[0][:]), waits=[sT], inc=sV)
            op("v", lambda e: e.tensor_sub(SCR[:, 3328:3840],
                                           SCR[:, 3328:3840], sc[1][:]),
               inc=sV)
            op("v", lambda e, c=c, tt=tt: e.tensor_add(
                RES[:, c * TOK + tt * 512: c * TOK + (tt + 1) * 512],
                RES[:, c * TOK + tt * 512: c * TOK + (tt + 1) * 512],
                SCR[:, 3328:3840]), inc=sV)

    ostg = [SCR[:, 0:1024], SCR[:, 1024:2048]]
    for it in range(TOK // 128):
        ot = ostg[it % 2]
        ss = sSt[it % 2]
        for c in range(NCH):
            op("t", lambda e, c=c, it=it: e.transpose(
                bcb[:, 0:128],
                RES[:, c * TOK + it * 128: c * TOK + (it + 1) * 128],
                IDF[:]), waits=[sA, sV], inc=sT)
            w = [(sT, sT.n)]
            if it >= 2 and c == 0:
                w.append(ss.at())
            op("a", lambda e, ot=ot, c=c: e.copy(
                ot[:, c * 128:(c + 1) * 128], bcb[:, 0:128]),
               waits=w, inc=sA)
        op("sy", lambda e, ot=ot, it=it: e.dma_start(
            out=y[it * 128:(it + 1) * 128, :], in_=ot),
           waits=[sA], inc=ss, dma=True)
    op("sy", lambda e: e.nop(), waits=[sSt[0], sSt[1]])

    kb.replay()
    kb.es.close()
    return kb.nc


_PROG = None


def _get_prog():
    global _PROG
    if _PROG is None:
        _PROG = build_program()
    return _PROG


def make_in_maps(inputs):
    def f32(k):
        return np.ascontiguousarray(np.asarray(inputs[k], dtype=np.float32))
    x = f32("L0_attn_x")
    shared = {
        "W_dq": f32("L0_attn_W_dq"), "W_uq": f32("L0_attn_W_uq"),
        "q_ln_g": f32("L0_attn_q_ln_g"), "q_ln_b": f32("L0_attn_q_ln_b"),
        "W_dkv": f32("L0_attn_W_dkv"), "W_ukv": f32("L0_attn_W_ukv"),
        "kv_ln_g": f32("L0_attn_kv_ln_g"), "kv_ln_b": f32("L0_attn_kv_ln_b"),
        "W_o": f32("L0_attn_W_o"),
        "w1_w": f32("L0_ffn_w1_w"), "w1_s": f32("L0_ffn_w1_s"),
        "w2_w": f32("L0_ffn_w2_w"), "w2_s": f32("L0_ffn_w2_s"),
        "h1_w": f32("L0_ffn_h1_w"), "h1_s": f32("L0_ffn_h1_s"),
        "h2_w": f32("L0_ffn_h2_w"), "h2_s": f32("L0_ffn_h2_s"),
        "w3_w": f32("L0_ffn_w3_w"), "w3_s": f32("L0_ffn_w3_s"),
        "qkv_w": f32("L1_attn_qkv_w"),
        "qn_w": f32("L1_attn_qn_w"), "qn_b": f32("L1_attn_qn_b"),
        "kn_w": f32("L1_attn_kn_w"), "kn_b": f32("L1_attn_kn_b"),
        "sinks": f32("L1_attn_sinks").reshape(1, 16),
        "proj_w": f32("L1_attn_proj_w"), "proj_b": f32("L1_attn_proj_b"),
        "w_gate": f32("L1_ffn_w_gate"), "w_up": f32("L1_ffn_w_up"),
        "w_down": f32("L1_ffn_w_down"),
    }
    inv_freq = 1.0 / (10000.0 ** (np.arange(0, 64, 2, dtype=np.float32) / 64))
    in_maps = []
    for core in range(8):
        b, h = core // 2, core % 2
        m = dict(shared)
        m["x_q"] = np.ascontiguousarray(x[b, h * TOK:(h + 1) * TOK, :])
        m["x_seq"] = np.ascontiguousarray(x[b])
        kx = (np.arange(128)[:, None] + 128 * np.arange(32)[None, :]
              - 2048 * h).astype(np.float32)
        m["kidx"] = kx
        pos = (h * TOK + np.arange(TOK, dtype=np.float32))
        fr = pos[None, :] * inv_freq[:, None]
        m["cosb"] = np.tile(np.cos(fr), (4, 1)).astype(np.float16)
        m["sinb"] = np.tile(np.sin(fr), (4, 1)).astype(np.float16)
        kl = (np.arange(128)[:, None]
              + 128 * (np.arange(18)[None, :] - 2)).astype(np.float32)
        k1 = np.concatenate([kl, kl + 255.0], axis=1)
        if h == 0:
            k1[:, 0:2] = 1e9
            k1[:, 18:20] = 1e9
        m["kix1"] = k1
        in_maps.append(m)
    return in_maps


def kernel(**inputs):
    nc = _get_prog()
    in_maps = make_in_maps(inputs)
    res = run_bass_kernel_spmd(nc, in_maps, core_ids=list(range(8)))
    out = np.empty((B, S, DM), dtype=np.float32)
    for core in range(8):
        b, h = core // 2, core % 2
        out[b, h * TOK:(h + 1) * TOK, :] = res.results[core]["y"]
    return out


# version 06 (final)
# speedup vs baseline: 1.1740x; 1.0585x over previous
import numpy as np
import ml_dtypes
from contextlib import ExitStack

import concourse.bass as bass
import concourse.mybir as mybir
from concourse.bass_utils import run_bass_kernel_spmd

F32 = mybir.dt.float32
BF16 = mybir.dt.bfloat16
F16 = mybir.dt.float16
AF = mybir.ActivationFunctionType
ALU = mybir.AluOpType
AX = mybir.AxisListType
BF16NP = ml_dtypes.bfloat16

B, S, DM = 4, 4096, 1024
TOK = 2048
NCH = DM // 128
NQB = TOK // 512
MAGIC = 12582912.0
LN_EPS = 1e-5
INV_DM = 1.0 / DM

ENGS = ("sy", "v", "a", "t", "g")


class Sem:
    def __init__(self, nc, es, name):
        self.h = es.enter_context(nc.semaphore(name))
        self.n = 0

    def at(self):
        return (self, self.n)


class KB:
    def __init__(self):
        self.nc = bass.Bass("TRN2", target_bir_lowering=False, debug=False)
        self.es = ExitStack()
        self.streams = {e: [] for e in ENGS}
        self.self_sems = {}

    def sb(self, name, shape, dtype):
        return self.es.enter_context(self.nc.sbuf_tensor(name, shape, dtype))

    def psum(self, name, shape, dtype=F32):
        return self.es.enter_context(self.nc.psum_tensor(name, shape, dtype))

    def sem(self, name):
        return Sem(self.nc, self.es, name)

    def op(self, eng, fn, waits=(), inc=None, dma=False):
        wl = []
        auto = self.self_sems.get(eng) if not dma else None
        if auto is not None:
            if auto.n > 0:
                wl.append((auto.h, auto.n))
            if inc is None:
                inc = auto
        for w in waits:
            if isinstance(w, Sem):
                wl.append((w.h, w.n))
            else:
                s, n = w
                if n > 0:
                    wl.append((s.h, n))
        il = None
        if inc is not None:
            k = 16 if dma else 1
            inc.n += k
            il = (inc.h, k)

        def run(e, fn=fn, wl=wl, il=il):
            for h, n in wl:
                if n > 0:
                    e.wait_ge(h, n)
            instr = fn(e)
            if il is not None:
                instr.then_inc(il[0], il[1])
        self.streams[eng].append(run)

    def replay(self):
        block = self.es.enter_context(self.nc.Block())

        def runner(ops):
            def go(e):
                for f in ops:
                    f(e)
            return go
        block.sync(runner(self.streams["sy"]))
        block.vector(runner(self.streams["v"]))
        block.scalar(runner(self.streams["a"]))
        block.tensor(runner(self.streams["t"]))
        block.gpsimd(runner(self.streams["g"]))


def build_program(dbg_outs=False):
    dkind = "ExternalOutput" if dbg_outs else "Internal"
    kb = KB()
    nc = kb.nc
    op = kb.op

    def din(name, shape, dtype=F32):
        return nc.dram_tensor(name, shape, dtype, kind="ExternalInput").ap()

    x_q = din("x_q", [TOK, DM])
    x_seq = din("x_seq", [S, DM])
    kidx = din("kidx", [128, 32])
    W_dq = din("W_dq", [DM, 256])
    W_uq = din("W_uq", [256, DM])
    q_ln_g = din("q_ln_g", [256])
    q_ln_b = din("q_ln_b", [256])
    W_dkv = din("W_dkv", [DM, 256])
    W_ukv = din("W_ukv", [256, 2048])
    kv_ln_g = din("kv_ln_g", [256])
    kv_ln_b = din("kv_ln_b", [256])
    W_o = din("W_o", [DM, DM])
    w1_w = din("w1_w", [4096, DM])
    w1_s = din("w1_s", [4096])
    w2_w = din("w2_w", [4096, DM])
    w2_s = din("w2_s", [4096])
    h1_w = din("h1_w", [4096, 4096])
    h1_s = din("h1_s", [4096])
    h2_w = din("h2_w", [4096, 4096])
    h2_s = din("h2_s", [4096])
    w3_w = din("w3_w", [DM, 4096])
    w3_s = din("w3_s", [DM])
    qkv_w = din("qkv_w", [3072, DM])
    qn_w = din("qn_w", [64])
    qn_b = din("qn_b", [64])
    kn_w = din("kn_w", [64])
    kn_b = din("kn_b", [64])
    sinks = din("sinks", [1, 16])
    proj_w = din("proj_w", [DM, DM])
    proj_b = din("proj_b", [DM])
    w_gate = din("w_gate", [3072, DM])
    w_up = din("w_up", [3072, DM])
    w_down = din("w_down", [DM, 3072])
    cosb = din("cosb", [128, TOK], mybir.dt.float16)
    sinb = din("sinb", [128, TOK], mybir.dt.float16)
    kix1 = din("kix1", [128, 36])
    y = nc.dram_tensor("y", [TOK, DM], F32, kind="ExternalOutput").ap()

    woT_bf = nc.dram_tensor("woT_bf", [DM, DM], BF16)
    w1_bf = nc.dram_tensor("w1_bf", [4096, DM], BF16)
    w2_bf = nc.dram_tensor("w2_bf", [4096, DM], BF16)
    h1_bf = nc.dram_tensor("h1_bf", [4096, 4096], BF16)
    h2_bf = nc.dram_tensor("h2_bf", [4096, 4096], BF16)
    w3_bf = nc.dram_tensor("w3_bf", [DM, 4096], BF16)
    qkv_bf = nc.dram_tensor("qkv_bf", [3072, DM], BF16)
    proj_bf = nc.dram_tensor("proj_bf", [DM, DM], BF16)
    wg_bf = nc.dram_tensor("wg_bf", [3072, DM], BF16, kind=dkind)
    wu_bf = nc.dram_tensor("wu_bf", [3072, DM], BF16, kind=dkind)
    wd_bf = nc.dram_tensor("wd_bf", [DM, 3072], BF16, kind=dkind)
    wdT = nc.dram_tensor("wdT", [3072, DM], BF16)
    gbuf = nc.dram_tensor("gbuf", [24, 128, TOK], BF16, kind=dkind)
    g2buf = nc.dram_tensor("g2buf", [24, 128, TOK], BF16, kind=dkind)
    ubuf = nc.dram_tensor("ubuf", [24, 128, TOK], BF16, kind=dkind)
    obuf = nc.dram_tensor("obuf", [24, 128, TOK], BF16, kind=dkind)
    mx_in = [nc.dram_tensor(f"mx_in{i}", [1, 2], F32) for i in range(4)]
    mx_out = [nc.dram_tensor(f"mx_out{i}", [1, 2], F32, kind=dkind)
              for i in range(4)]
    bdk_in = [nc.dram_tensor(f"bdk_in{g}", [128, 256], BF16) for g in range(8)]
    bdk_out = [nc.dram_tensor(f"bdk_out{g}", [256, 256], BF16) for g in range(8)]
    bdv_in = [nc.dram_tensor(f"bdv_in{g}", [256, 130], BF16) for g in range(8)]
    bdv_out = [nc.dram_tensor(f"bdv_out{g}", [512, 130], BF16) for g in range(8)]

    ident_f32 = nc.inline_tensor(np.eye(128, dtype=np.float32), "c_idf")
    ident_bf = nc.inline_tensor(np.eye(128, dtype=BF16NP), "c_idb")
    ones_col = nc.inline_tensor(np.ones((128, 1), dtype=BF16NP), "c_onc")
    ones_row_bf = nc.inline_tensor(np.ones((1, 128), dtype=BF16NP), "c_onrb")
    ones_row_f = nc.inline_tensor(np.ones((1, 128), dtype=np.float32), "c_onrf")
    ones_f_full = nc.inline_tensor(np.ones((128, 64), dtype=np.float32), "c_onff")
    rmat = np.zeros((128, 128), dtype=np.float32)
    for hh in range(2):
        b0 = 64 * hh
        for i in range(32):
            rmat[b0 + i, b0 + 32 + i] = -1.0
            rmat[b0 + 32 + i, b0 + i] = 1.0
    rotT_c = nc.inline_tensor(rmat.T.astype(BF16NP).copy(), "c_rotT")
    qixr = np.arange(TOK, dtype=np.float16)[None, :]
    qixr_c = nc.inline_tensor(qixr, "c_qixr")
    ones_row_h = nc.inline_tensor(np.ones((1, 128), dtype=np.float16), "c_onrh")

    RES = kb.sb("RES", [128, NCH * TOK], F32)
    A1 = kb.sb("A1", [128, 40960], BF16)
    A2 = kb.sb("A2", [128, 4608], BF16)
    SCR = kb.sb("SCR", [128, 4096], F32)
    STT = kb.sb("STT", [128, 1024], F32)
    WS = kb.sb("WS", [128, 6144], BF16)
    WPERS = kb.sb("WPERS", [128, 6144], BF16)
    CW = kb.sb("CW", [128, 64], F32)
    IDF = kb.sb("IDF", [128, 128], F32)
    IDB = kb.sb("IDB", [128, 128], BF16)
    ONC = kb.sb("ONC", [128, 1], BF16)
    ONRB = kb.sb("ONRB", [1, 128], BF16)
    ONRF = kb.sb("ONRF", [1, 128], F32)
    ONFF = kb.sb("ONFF", [128, 64], F32)
    ONRH = kb.sb("ONRH", [1, 128], F16)
    KIX = kb.sb("KIX", [128, 32], F32)
    QIX = kb.sb("QIX", [1, TOK], F16)

    FS1 = kb.sb("FS1", [128, 32], F32)
    FS2 = kb.sb("FS2", [128, 32], F32)
    FH1 = kb.sb("FH1", [128, 32], F32)
    FH2 = kb.sb("FH2", [128, 32], F32)
    FS3 = kb.sb("FS3", [128, 8], F32)
    ROTT = kb.sb("ROTT", [128, 128], BF16)
    KIX1 = kb.sb("KIX1", [128, 36], F32)
    QNWB = kb.sb("QNWB", [128, 4], F32)
    SNK = kb.sb("SNK", [1, 16], F32)
    PBC = kb.sb("PBC", [128, 8], F32)
    ESNK = kb.sb("ESNK", [128, 2], F32)
    BQC = kb.sb("BQC", [128, 8], F32)
    BKC = kb.sb("BKC", [128, 8], F32)
    BVR = kb.sb("BVR", [1, 1024], BF16)
    NEGG = kb.sb("NEGG", [128, 16], F32)

    WDQ = WPERS[:, 0:2048]
    WDKV = WPERS[:, 2048:4096]
    WUQ = WPERS[:, 0:2048]
    WUKV = WPERS[:, 2048:6144]

    CKVT = A1[:, 0:8192].rearrange("p (c t) -> p c t", c=2)
    KT = A1[:, 8192:12288]
    VA = A1[:, 12288:16448].rearrange("p (c t) -> p c t", c=32)
    CQT = A1[:, 16448:20544].rearrange("p (c t) -> p c t", c=2)
    ATT = A1[:, 20544:36928].rearrange("p (c t) -> p c t", c=8)
    ESL = A1[:, 36928:40000].rearrange("p (c t) -> p c t", c=6)
    BO = A1[:, 0:16384].rearrange("p (c t) -> p c t", c=8)
    QTG = WS[:, 0:2048]

    XNT = [A2[:, s * 1024:(s + 1) * 1024].rearrange("p (c t) -> p c t", c=8)
           for s in range(2)]
    XNS = [A2[:, 2048 + s * 1024: 3072 + s * 1024] for s in range(2)]
    CNS = A2[:, 4096:4352]
    SQB = [A2[:, 2048 + s * 512: 2560 + s * 512] for s in range(2)]

    mm = [kb.psum(f"mm{i}", [128, 512], F32) for i in range(4)]
    sc = [kb.psum(f"sc{i}", [128, 512], F32) for i in range(2)]
    bcb = kb.psum("bcb", [128, 512], F32)
    trp = kb.psum("trp", [128, 256], BF16)

    sId = kb.sem("sId")
    sLd = [kb.sem(f"sLd{i}") for i in range(2)]
    sWa = kb.sem("sWa")
    sWo = [kb.sem(f"sWo{i}") for i in range(2)]
    sGpW = kb.sem("sGpW")
    sV = kb.sem("sV")
    sA = kb.sem("sA")
    sT = kb.sem("sT")
    sG = kb.sem("sG")
    sSt = [kb.sem(f"sSt{i}") for i in range(2)]
    sGAtt = [kb.sem(f"sGAtt{i}") for i in range(2)]
    kb.self_sems = {"v": sV, "a": sA, "g": sG}
    kb._oddslot = 0

    for dst, src in ((IDF, ident_f32), (IDB, ident_bf), (ONC, ones_col),
                     (ONRB, ones_row_bf), (ONRF, ones_row_f),
                     (ONFF, ones_f_full),
                     (QIX, qixr_c), (ONRH, ones_row_h)):
        op("sy", lambda e, d=dst, s=src: e.dma_start(out=d[:], in_=s.ap()),
           inc=sId, dma=True)
    op("sy", lambda e: e.dma_start(out=KIX[:], in_=kidx), inc=sId, dma=True)
    op("sy", lambda e: e.dma_start(out=ROTT[:], in_=rotT_c.ap()), inc=sId,
       dma=True)
    op("sy", lambda e: e.dma_start(out=KIX1[:], in_=kix1), inc=sId, dma=True)
    op("sy", lambda e: e.dma_start(out=SNK[:], in_=sinks), inc=sId, dma=True)
    for col, src in ((0, qn_w), (1, qn_b), (2, kn_w), (3, kn_b)):
        for half in range(2):
            op("sy", lambda e, col=col, src=src, half=half: e.dma_start(
                out=QNWB[half * 64:(half + 1) * 64, col:col + 1],
                in_=src.rearrange("(a d) -> d a", a=1),
                allow_slow_non_contiguous=True), inc=sId, dma=True)
    op("sy", lambda e: e.dma_start(
        out=PBC[:], in_=proj_b.rearrange("(c p) -> p c", p=128),
        allow_slow_non_contiguous=True), inc=sId, dma=True)
    GQC, BQ0, GKC, BK0 = CW[:, 0:2], CW[:, 2:4], CW[:, 4:6], CW[:, 6:8]
    for dst, src in ((GQC, q_ln_g), (BQ0, q_ln_b), (GKC, kv_ln_g),
                     (BK0, kv_ln_b), (FS1[:], w1_s), (FS2[:], w2_s),
                     (FH1[:], h1_s), (FH2[:], h2_s), (FS3[:], w3_s)):
        op("sy", lambda e, d=dst, s=src: e.dma_start(
            out=d, in_=s.rearrange("(c p) -> p c", p=128),
            allow_slow_non_contiguous=True), inc=sId, dma=True)

    op("g", lambda e: e.dma_start(
        out=WDQ[:].rearrange("p (c o) -> p c o", o=256),
        in_=W_dq.rearrange("(c p) o -> p c o", p=128)), inc=sGpW, dma=True)
    op("g", lambda e: e.dma_start(
        out=WDKV[:].rearrange("p (c o) -> p c o", o=256),
        in_=W_dkv.rearrange("(c p) o -> p c o", p=128)), inc=sGpW, dma=True)
    sPrA = kb.sem("sPrA")
    sPrB = kb.sem("sPrB")

    def prep_chain(sem, slab, jobs):
        for src, dst, rows, cols in jobs:
            for i in range(rows // 128):
                for cb in range(cols // 1024):
                    op("g", lambda e, src=src, i=i, cb=cb, slab=slab:
                       e.dma_start(
                           out=slab,
                           in_=src[i * 128:(i + 1) * 128,
                                   cb * 1024:(cb + 1) * 1024]),
                       waits=[sem.at()], inc=sem, dma=True)
                    op("g", lambda e, dst=dst, i=i, cb=cb, slab=slab:
                       e.dma_start(
                           out=dst.ap()[i * 128:(i + 1) * 128,
                                        cb * 1024:(cb + 1) * 1024],
                           in_=slab),
                       waits=[sem.at()], inc=sem, dma=True)

    prep_chain(sPrA, WS[:, 2048:3072],
               [(W_o, woT_bf, DM, DM), (w1_w, w1_bf, 4096, DM),
                (w2_w, w2_bf, 4096, DM), (h1_w, h1_bf, 4096, 4096)])
    prep_chain(sPrB, WS[:, 3072:4096],
               [(h2_w, h2_bf, 4096, 4096), (w3_w, w3_bf, DM, 4096),
                (qkv_w, qkv_bf, 3072, DM), (proj_w, proj_bf, DM, DM)])
    prB_qkv_done = (4096 // 128) * 4 * 32 + (DM // 128) * 4 * 32 \
        + (3072 // 128) * 1 * 32
    prB_all = prB_qkv_done + (DM // 128) * 1 * 32
    prA_wo_done = 8 * 2 * 16

    xstg = [SCR[:, 0:1024], SCR[:, 1024:2048]]
    xsq = SCR[:, 2048:3072]
    csq = SCR[:, 3072:3328]
    MUC, RSC = CW[:, 8:9], CW[:, 9:10]
    T0, T1, T2, T3 = (CW[:, 10:11], CW[:, 11:12], CW[:, 12:13], CW[:, 13:14])
    CM, CR = CW[:, 14:15], CW[:, 15:16]
    C0, C1, C2 = CW[:, 16:17], CW[:, 17:18], CW[:, 18:19]

    g_xstg = [[], []]
    g_xnt = [0, 0]
    g_xns = [0, 0]
    g_cns = [0]

    def tok_ln_tile(seq, it, src_dram, src_row):
        sl = sLd[seq % 2]
        st = xstg[seq % 2]
        op("sy", lambda e, st=st: e.dma_start(
            out=st, in_=src_dram[src_row:src_row + 128, :]),
           waits=list(g_xstg[seq % 2]), inc=sl, dma=True)
        ldw = sl.at()
        op("v", lambda e, st=st: e.reduce_sum(T0, st, axis=AX.X),
           waits=[ldw], inc=sV)
        op("a", lambda e, st=st: e.activation(out=xsq, in_=st, func=AF.Square),
           waits=[ldw, sV.at()], inc=sA)
        op("v", lambda e: e.reduce_sum(T1, xsq, axis=AX.X), waits=[sA], inc=sV)
        op("a", lambda e: e.activation(out=MUC, in_=T0, func=AF.Copy,
                                       scale=INV_DM), waits=[sV], inc=sA)
        op("a", lambda e: e.activation(out=T2, in_=MUC, func=AF.Square),
           waits=[sA.at()], inc=sA)
        op("v", lambda e: e.tensor_scalar(out=T3, in0=T1, scalar1=INV_DM,
                                          scalar2=LN_EPS, op0=ALU.mult,
                                          op1=ALU.add), waits=[sA], inc=sV)
        op("v", lambda e: e.tensor_sub(T3, T3, T2), inc=sV)
        op("v", lambda e: e.reciprocal(T3, T3), inc=sV)
        op("a", lambda e: e.activation(out=RSC, in_=T3, func=AF.Sqrt),
           waits=[sV], inc=sA)
        xns = XNS[seq % 2]
        op("v", lambda e, st=st, xns=xns: e.tensor_scalar(
            out=xns, in0=st, scalar1=MUC, scalar2=RSC,
            op0=ALU.subtract, op1=ALU.mult),
           waits=[sA, (sT, g_xns[seq % 2])], inc=sV)
        g_xstg[seq % 2] = [sV.at()]
        xnt = XNT[seq % 2]
        for c in range(NCH):
            op("t", lambda e, xns=xns, c=c: e.transpose(
                trp[:, 0:128], xns[:, c * 128:(c + 1) * 128], IDB[:]),
               waits=[sV, sA, sId.at()], inc=sT)
            w = [sT.at()]
            if c == 0:
                w.append((sT, g_xnt[seq % 2]))
            op("a", lambda e, xnt=xnt, c=c: e.copy(xnt[:, c, 0:128],
                                                   trp[:, 0:128]),
               waits=w, inc=sA)
        g_xns[seq % 2] = sT.n
        return st

    def raw_res_transpose(seq, st, it2):
        for c in range(NCH):
            op("t", lambda e, st=st, c=c: e.transpose(
                bcb[:, 0:128], st[:, c * 128:(c + 1) * 128], IDF[:]),
               waits=[sA], inc=sT)
            op("a", lambda e, c=c, it2=it2: e.copy(
                RES[:, c * TOK + it2 * 128: c * TOK + (it2 + 1) * 128],
                bcb[:, 0:128]), waits=[sT], inc=sA)
        g_xstg[seq % 2].append(sT.at())

    def proj_c_tile(seq, wsb, dst_ct, dst_col):
        xnt = XNT[seq % 2]
        bank = mm[seq % 2]
        for c in range(NCH):
            op("t", lambda e, b=bank, xnt=xnt, c=c, w=wsb: e.matmul(
                b[:, 0:256], xnt[:, c, 0:128], w[:, c * 256:(c + 1) * 256],
                start=(c == 0), stop=(c == NCH - 1)),
               waits=[sA, sV, (sGpW, 32)] if c == 0 else [],
               inc=sT if c == NCH - 1 else None)
        g_xnt[seq % 2] = sT.n
        op("v", lambda e, b=bank: e.reduce_sum(C0, b[:, 0:256], axis=AX.X),
           waits=[sT], inc=sV)
        op("a", lambda e, b=bank: e.activation(out=csq, in_=b[:, 0:256],
                                               func=AF.Square),
           waits=[sT, sV.at()], inc=sA)
        op("v", lambda e: e.reduce_sum(C1, csq, axis=AX.X), waits=[sA], inc=sV)
        inv = 1.0 / 256.0
        op("a", lambda e: e.activation(out=C0, in_=C0, func=AF.Copy,
                                       scale=inv), waits=[sV], inc=sA)
        op("a", lambda e: e.activation(out=C2, in_=C0, func=AF.Square),
           waits=[sA.at()], inc=sA)
        op("v", lambda e: e.tensor_scalar(out=C1, in0=C1, scalar1=inv,
                                          scalar2=LN_EPS, op0=ALU.mult,
                                          op1=ALU.add), waits=[sA], inc=sV)
        op("v", lambda e: e.tensor_sub(C1, C1, C2), inc=sV)
        op("v", lambda e: e.reciprocal(C1, C1), inc=sV)
        op("a", lambda e: e.activation(out=C1, in_=C1, func=AF.Sqrt),
           waits=[sV], inc=sA)
        op("v", lambda e, b=bank: e.tensor_scalar(
            out=CNS, in0=b[:, 0:256], scalar1=C0, scalar2=C1,
            op0=ALU.subtract, op1=ALU.mult),
           waits=[sA, (sT, g_cns[0])], inc=sV)
        for c2 in range(2):
            op("t", lambda e, c2=c2: e.transpose(
                trp[:, 128:256], CNS[:, c2 * 128:(c2 + 1) * 128], IDB[:]),
               waits=[sV, sA], inc=sT)
            op("a", lambda e, c2=c2, dst=dst_ct, dc=dst_col: e.copy(
                dst[:, c2, dc:dc + 128], trp[:, 128:256]), waits=[sT], inc=sA)
        g_cns[0] = sT.n

    seq = 0
    for it in range(S // 128):
        tok_ln_tile(seq, it, x_seq, it * 128)
        proj_c_tile(seq, WDKV, CKVT, it * 128)
        seq += 1
    for it2 in range(TOK // 128):
        st = tok_ln_tile(seq, it2, x_q, it2 * 128)
        raw_res_transpose(seq, st, it2)
        proj_c_tile(seq, WDQ, CQT, it2 * 128)
        seq += 1

    fold_f32 = SCR[:, 0:2048]
    QROW = STT[0:1, 0:1024]
    KVROW = SCR[0:1, 2048:3072]
    VROW = SCR[0:1, 3072:4096]
    BQB = kb.sb("BQB", [128, 2], BF16)
    BKB = kb.sb("BKB", [128, 2], BF16)
    op("v", lambda e: e.tensor_copy(BQB[:], BQ0), waits=[sId.at()], inc=sV)
    op("v", lambda e: e.tensor_copy(BKB[:], BK0), inc=sV)
    for c in range(2):
        op("sy", lambda e, c=c: e.dma_start(
            out=fold_f32[:, 0:1024], in_=W_uq[c * 128:(c + 1) * 128, :]),
           waits=[(sV, sV.n), (sA, sA.n), (sT, sT.n)], inc=sWa, dma=True)
        op("v", lambda e, c=c: e.tensor_scalar(
            out=WUQ[:, c * 1024:(c + 1) * 1024], in0=fold_f32[:, 0:1024],
            scalar1=GQC[:, c:c + 1], scalar2=None, op0=ALU.mult),
           waits=[sWa, sId.at()], inc=sV)
        op("v", lambda e: e.tensor_copy(WS[:, 0:1024], fold_f32[:, 0:1024]),
           inc=sV)
        for half in range(2):
            op("t", lambda e, half=half, c=c: e.matmul(
                sc[half][0:1, 0:512], BQB[:, c:c + 1],
                WS[:, half * 512: half * 512 + 512],
                start=(c == 0), stop=(c == 1)),
               waits=[(sV, sV.n)] if half == 0 else [], inc=sT)
    for half in range(2):
        op("a", lambda e, half=half: e.copy(
            QROW[:, half * 512:(half + 1) * 512], sc[half][0:1, 0:512]),
           waits=[sT], inc=sA)
    for c in range(2):
        op("sy", lambda e, c=c: e.dma_start(
            out=fold_f32, in_=W_ukv[c * 128:(c + 1) * 128, :]),
           waits=[(sV, sV.n), (sA, sA.n), (sT, sT.n)], inc=sWa, dma=True)
        op("v", lambda e, c=c: e.tensor_scalar(
            out=WUKV[:, c * 2048:(c + 1) * 2048], in0=fold_f32,
            scalar1=GKC[:, c:c + 1], scalar2=None, op0=ALU.mult),
           waits=[sWa], inc=sV)
        op("v", lambda e: e.tensor_copy(WS[:, 0:2048], fold_f32), inc=sV)
        for half in range(4):
            op("t", lambda e, half=half, c=c: e.matmul(
                mm[half][0:1, 0:512], BKB[:, c:c + 1],
                WS[:, half * 512: half * 512 + 512],
                start=(c == 0), stop=(c == 1)),
               waits=[(sV, sV.n)] if half == 0 else [], inc=sT)
    for half in range(4):
        dst = (KVROW if half < 2 else VROW)
        col = (half % 2) * 512
        op("a", lambda e, dst=dst, col=col, half=half: e.copy(
            dst[:, col:col + 512], mm[half][0:1, 0:512]), waits=[sT], inc=sA)
    for oc in range(8):
        op("t", lambda e, oc=oc: e.matmul(
            bcb[:, oc:oc + 1], QROW[:, oc * 128:(oc + 1) * 128],
            ONRF[:, 0:1], start=True, stop=True),
           waits=[sA] if oc == 0 else [], inc=sT if oc == 7 else None)
    op("a", lambda e: e.copy(BQC[:], bcb[:, 0:8]), waits=[sT], inc=sA)
    for oc in range(8):
        op("t", lambda e, oc=oc: e.matmul(
            bcb[:, oc:oc + 1], KVROW[:, oc * 128:(oc + 1) * 128],
            ONRF[:, 0:1], start=True, stop=True),
           waits=[sA] if oc == 0 else [], inc=sT if oc == 7 else None)
    op("a", lambda e: e.copy(BKC[:], bcb[:, 0:8]), waits=[sT], inc=sA)
    op("v", lambda e: e.tensor_copy(BVR[:], VROW), waits=[sA], inc=sV)

    ws_raw_free = (sT, sT.n)

    for kc in range(32):
        op("v", lambda e, kc=kc: e.memset(
            VA[:, kc, 0:130].rearrange("p (h z) -> p h z", h=2)[:, :, 64:65],
            1.0), inc=sV)

    w1T = nc.dram_tensor("w1T", [DM, 4096], BF16)
    w2T = nc.dram_tensor("w2T", [DM, 4096], BF16)
    h1T = nc.dram_tensor("h1T", [4096, 4096], BF16)
    h2T = nc.dram_tensor("h2T", [4096, 4096], BF16)
    w3T = nc.dram_tensor("w3T", [4096, DM], BF16)
    sGst = [kb.sem(f"sGst{i}") for i in range(2)]
    sGd = kb.sem("sGd")

    n_kt = [20 + 4 * j for j in range(NQB)]
    g_kt = 0
    g_negg = 0

    for grp in range(8):
        for tt in range(NQB):
            bank = mm[2 + tt % 2]
            for c in range(2):
                op("t", lambda e, b=bank, grp=grp, tt=tt, c=c: e.matmul(
                    b[:], WUQ[:, c * 1024 + grp * 128: c * 1024 + grp * 128 + 128],
                    CQT[:, c, tt * 512:(tt + 1) * 512],
                    start=(c == 0), stop=(c == 1)),
                   waits=([(sA, sA.n), (sV, sV.n)] if c == 0 else []) +
                         ([(sT, g_kt)] if (tt == 0 and c == 0) else []),
                   inc=sT if c == 1 else None)
            op("a", lambda e, b=bank, grp=grp, tt=tt: e.activation(
                out=QTG[:, tt * 512:(tt + 1) * 512], in_=b[:],
                func=AF.Identity, bias=BQC[:, grp:grp + 1], scale=1.0),
               waits=[sT], inc=sA)
        for tt in range(8):
            bank = mm[2 + tt % 2]
            for c in range(2):
                op("t", lambda e, b=bank, grp=grp, tt=tt, c=c: e.matmul(
                    b[:], WUKV[:, c * 2048 + grp * 128: c * 2048 + grp * 128 + 128],
                    CKVT[:, c, tt * 512:(tt + 1) * 512],
                    start=(c == 0), stop=(c == 1)),
                   waits=[(sA, sA.n)] if c == 0 else [],
                   inc=sT if c == 1 else None)
            op("a", lambda e, b=bank, grp=grp, tt=tt: e.activation(
                out=KT[:, tt * 512:(tt + 1) * 512], in_=b[:],
                func=AF.Identity, bias=BKC[:, grp:grp + 1], scale=1.0),
               waits=[sT], inc=sA)
        for kc in range(32):
            bank = mm[2 + kc % 2]
            for c in range(2):
                op("t", lambda e, b=bank, grp=grp, kc=kc, c=c: e.matmul(
                    b[:, 0:128], CKVT[:, c, kc * 128:(kc + 1) * 128],
                    WUKV[:, c * 2048 + 1024 + grp * 128:
                         c * 2048 + 1024 + grp * 128 + 128],
                    start=(c == 0), stop=False),
                   waits=[(sV, sV.n), (sA, sA.n)] if c == 0 else [],
                   inc=None)
            op("t", lambda e, b=bank, grp=grp: e.matmul(
                b[:, 0:128], ONRB[:, 0:128],
                BVR[:, grp * 128:(grp + 1) * 128],
                start=False, stop=True), inc=sT)
            op("v", lambda e, b=bank, kc=kc: e.tensor_copy(
                VA[:, kc, 0:130].rearrange("p (h z) -> p h z", h=2)[:, :, 0:64],
                b[:, 0:128].rearrange("p (h d) -> p h d", h=2)),
               waits=[sT], inc=sV)

        for hh in range(2):
            base = 64 * hh
            for tt in range(NQB):
                op("a", lambda e, base=base, tt=tt: e.activation(
                    out=SCR[base:base + 64, 0:512],
                    in_=QTG[base:base + 64, tt * 512:(tt + 1) * 512],
                    func=AF.Square), waits=[(sV, sV.n)], inc=sA)
                op("v", lambda e, base=base: e.tensor_copy(
                    SQB[0][base:base + 64, :], SCR[base:base + 64, 0:512]),
                   waits=[sA], inc=sV)
                op("t", lambda e, base=base: e.matmul(
                    sc[0][0:1, 0:512], ONC[base:base + 64, :],
                    SQB[0][base:base + 64, :], start=True, stop=True),
                   waits=[sV], inc=sT)
                op("v", lambda e, hh=hh, tt=tt: e.reduce_max(
                    STT[0:1, 526 + hh * 16 + tt: 527 + hh * 16 + tt],
                    sc[0][0:1, 0:512], axis=AX.X), waits=[sT], inc=sV)
            for tt in range(8):
                op("a", lambda e, base=base, tt=tt: e.activation(
                    out=SCR[base:base + 64, 0:512],
                    in_=KT[base:base + 64, tt * 512:(tt + 1) * 512],
                    func=AF.Square), waits=[(sV, sV.n)], inc=sA)
                op("v", lambda e, base=base: e.tensor_copy(
                    SQB[0][base:base + 64, :], SCR[base:base + 64, 0:512]),
                   waits=[sA], inc=sV)
                op("t", lambda e, base=base: e.matmul(
                    sc[0][0:1, 0:512], ONC[base:base + 64, :],
                    SQB[0][base:base + 64, :], start=True, stop=True),
                   waits=[sV], inc=sT)
                op("v", lambda e, hh=hh, tt=tt: e.reduce_max(
                    STT[0:1, 534 + hh * 16 + tt: 535 + hh * 16 + tt],
                    sc[0][0:1, 0:512], axis=AX.X), waits=[sT], inc=sV)
            op("v", lambda e, hh=hh: e.reduce_max(
                STT[0:1, 516 + hh:517 + hh], STT[0:1, 526 + hh * 16:526 + hh * 16 + NQB],
                axis=AX.X), inc=sV)
            op("v", lambda e, hh=hh: e.reduce_max(
                STT[0:1, 518 + hh:519 + hh], STT[0:1, 534 + hh * 16:534 + hh * 16 + 8],
                axis=AX.X), inc=sV)
        op("a", lambda e: e.activation(out=STT[0:1, 520:524], in_=STT[0:1, 516:520],
                                       func=AF.Sqrt), waits=[sV], inc=sA)
        op("v", lambda e: e.tensor_mul(STT[0:1, 524:526], STT[0:1, 520:522],
                                       STT[0:1, 522:524]), waits=[sA], inc=sV)
        op("v", lambda e: e.tensor_scalar(
            out=STT[0:1, 512:514], in0=STT[0:1, 524:526], scalar1=-0.125,
            scalar2=None, op0=ALU.mult), inc=sV)
        op("t", lambda e: e.matmul(bcb[:, 0:2], ONRF[:, 0:128],
                                   STT[0:1, 512:514], start=True, stop=True),
           waits=[sV, (sA, g_negg)], inc=sT)
        op("a", lambda e, grp=grp: e.copy(NEGG[:, grp * 2:(grp + 1) * 2],
                                          bcb[:, 0:2]), waits=[sT], inc=sA)
        g_negg = sA.n

        for j in range(NQB):
            op("t", lambda e, j=j: e.matmul(
                bcb[:], ONRH[:, 0:128], QIX[:, j * 512:(j + 1) * 512],
                start=True, stop=True), waits=[sA, sV], inc=sT)
            qidx_done = sT.n
            nkt = n_kt[j]

            def emit_msl(kt, gw):
                msl = ESL[:, 4 + (kt % 2), 0:512]
                op("v", lambda e, msl=msl, kt=kt: e.tensor_scalar(
                    out=msl, in0=bcb[:], scalar1=KIX[:, kt:kt + 1],
                    scalar2=None, op0=ALU.is_ge),
                   waits=[(sT, qidx_done), (sG, gw)], inc=sV)

            def emit_score(kt, sub, aw):
                base = 64 * sub
                op("t", lambda e, base=base, j=j, kt=kt, sub=sub:
                   e.matmul(sc[sub][:],
                            KT[base:base + 64, kt * 128:(kt + 1) * 128],
                            QTG[base:base + 64, j * 512:(j + 1) * 512],
                            start=True, stop=True,
                            tile_position=(base, 0)),
                   waits=[(sA, aw)], inc=sT)

            g0 = sG.n
            emit_msl(0, g0)
            emit_score(0, 0, sA.n)
            emit_score(0, 1, sA.n)
            exp_cnt = [sA.n, sA.n]
            mul_cnt = [sG.n, sG.n]
            msl_rd = {}
            for kt in range(nkt):
                msl = ESL[:, 4 + (kt % 2), 0:512]
                for sub in range(2):
                    head = 2 * grp + sub
                    esl = ESL[:, sub, 0:512]
                    op("a", lambda e, sub=sub, esl=esl, head=head:
                       e.activation(out=esl, in_=sc[sub][:], func=AF.Exp,
                                    scale=0.125, bias=NEGG[:, head:head + 1]),
                       waits=[(sT, sT.n), (sG, mul_cnt[sub])], inc=sA)
                    exp_cnt[sub] = sA.n
                for sub in range(2):
                    esl = ESL[:, sub, 0:512]
                    op("g", lambda e, esl=esl, msl=msl: e.tensor_mul(
                        esl, esl, msl), waits=[(sA, exp_cnt[sub]),
                                               (sV, sV.n)], inc=sG)
                    mul_cnt[sub] = sG.n
                msl_rd[kt] = sG.n
                if kt + 1 < nkt:
                    emit_msl(kt + 1, msl_rd.get(kt - 1, g0))
                    emit_score(kt + 1, 0, exp_cnt[0])
                    emit_score(kt + 1, 1, exp_cnt[1])
                for sub in range(2):
                    esl = ESL[:, sub, 0:512]
                    op("t", lambda e, sub=sub, kt=kt, j=j, esl=esl,
                       nkt=nkt: e.matmul(
                           mm[sub][0:65, :],
                           VA[:, kt, sub * 65: sub * 65 + 65],
                           esl, start=(kt == 0), stop=(kt == nkt - 1)),
                       waits=[(sG, mul_cnt[sub])]
                       + ([(sA, sA.n)] if kt == 0 else []),
                       inc=sT)
            for hh in range(2):
                head = 2 * grp + hh
                op("a", lambda e, hh=hh: e.copy(SCR[0:65, 0:512],
                                                mm[hh][0:65, :]),
                   waits=[sT, (sV, sV.n)], inc=sA)
                op("v", lambda e: e.reciprocal(SCR[64:65, 512:1024],
                                               SCR[64:65, 0:512]),
                   waits=[sA], inc=sV)
                op("t", lambda e: e.matmul(bcb[0:64, :], ONFF[64:65, 0:64],
                                           SCR[64:65, 512:1024],
                                           start=True, stop=True,
                                           tile_position=(64, 0)),
                   waits=[sV], inc=sT)
                dch, dbase = head // 2, 64 * (head % 2)
                if dbase == 0:
                    op("v", lambda e, dch=dch, j=j: e.tensor_mul(
                        ATT[0:64, dch, j * 512:(j + 1) * 512],
                        SCR[0:64, 0:512], bcb[0:64, :]), waits=[sT], inc=sV)
                else:
                    oslot = kb._oddslot
                    osem = sGAtt[oslot]
                    op("v", lambda e, oslot=oslot: e.tensor_mul(
                        ESL[0:64, 2 + oslot, 0:512], SCR[0:64, 0:512],
                        bcb[0:64, :]), waits=[sT, osem.at()], inc=sV)
                    op("g", lambda e, oslot=oslot, dch=dch, j=j: e.dma_start(
                        out=ATT[64:128, dch, j * 512:(j + 1) * 512],
                        in_=ESL[0:64, 2 + oslot, 0:512]),
                       waits=[(sV, sV.n)], inc=osem, dma=True)
                    kb._oddslot = 1 - oslot
        g_kt = sT.n

    attn_done = [(sT, sT.n), (sA, sA.n), (sV, sV.n)]
    wo_guard = 0
    for oh in range(2):
        swo = sWo[oh]
        for d in range(8):
            woff = d * 512 if d < 4 else 4096 + (d - 4) * 512
            op("sy", lambda e, oh=oh, d=d, woff=woff: e.dma_start(
                out=WS[:, woff:woff + 512],
                in_=woT_bf.ap()[oh * 512:(oh + 1) * 512,
                                d * 128:(d + 1) * 128],
                transpose=True),
               waits=[(sPrA, prA_wo_done), ws_raw_free] + attn_done
               + ([(sT, wo_guard)] if oh == 1 else []),
               inc=swo, dma=True)
        for oc in range(4):
            o = oh * 4 + oc
            for tt in range(NQB):
                bank = mm[tt % 2]
                for d in range(8):
                    woff = d * 512 if d < 4 else 4096 + (d - 4) * 512
                    op("t", lambda e, b=bank, d=d, oc=oc, tt=tt, woff=woff:
                       e.matmul(
                        b[:], WS[:, woff + oc * 128: woff + oc * 128 + 128],
                        ATT[:, d, tt * 512:(tt + 1) * 512],
                        start=(d == 0), stop=(d == 7)),
                       waits=([(swo, swo.n), (sV, sV.n), (sA, sA.n),
                               sGAtt[0].at(), sGAtt[1].at()]
                              if d == 0 else []),
                       inc=sT if d == 7 else None)
                op("a", lambda e, b=bank, o=o, tt=tt: e.copy(
                    BO[:, o, tt * 512:(tt + 1) * 512], b[:]),
                   waits=[sT], inc=sA)
        if oh == 0:
            wo_guard = sT.n

    def post_ln_residual(src_bf):
        for tt in range(NQB):
            for c in range(NCH):
                op("a", lambda e, c=c, tt=tt, src=src_bf: e.activation(
                    out=SQB[c % 2], in_=src[:, c, tt * 512:(tt + 1) * 512],
                    func=AF.Square), waits=[(sT, sT.n), (sV, sV.n)], inc=sA)
                op("t", lambda e, c=c, tt=tt, src=src_bf: e.matmul(
                    sc[0][0:1, :], ONC[:], src[:, c, tt * 512:(tt + 1) * 512],
                    start=(c == 0), stop=(c == NCH - 1)),
                   waits=[(sV, sV.n)] if c == 0 else [], inc=sT)
                op("t", lambda e, c=c: e.matmul(
                    sc[1][0:1, :], ONC[:], SQB[c % 2],
                    start=(c == 0), stop=(c == NCH - 1)),
                   waits=[sA], inc=sT)
            op("a", lambda e: e.activation(out=SCR[0:1, 0:512],
                                           in_=sc[0][0:1, :], func=AF.Copy,
                                           scale=INV_DM), waits=[sT], inc=sA)
            op("a", lambda e: e.activation(out=SCR[0:1, 512:1024],
                                           in_=SCR[0:1, 0:512],
                                           func=AF.Square),
               waits=[sA.at()], inc=sA)
            op("v", lambda e: e.tensor_scalar(
                out=SCR[0:1, 1024:1536], in0=sc[1][0:1, :], scalar1=INV_DM,
                scalar2=LN_EPS, op0=ALU.mult, op1=ALU.add),
               waits=[sA], inc=sV)
            op("v", lambda e: e.tensor_sub(SCR[0:1, 1024:1536],
                                           SCR[0:1, 1024:1536],
                                           SCR[0:1, 512:1024]), inc=sV)
            op("v", lambda e: e.reciprocal(SCR[0:1, 1024:1536],
                                           SCR[0:1, 1024:1536]), inc=sV)
            op("a", lambda e: e.activation(out=SCR[0:1, 1024:1536],
                                           in_=SCR[0:1, 1024:1536],
                                           func=AF.Sqrt),
               waits=[sV], inc=sA)
            op("v", lambda e: e.tensor_mul(SCR[0:1, 1536:2048],
                                           SCR[0:1, 0:512],
                                           SCR[0:1, 1024:1536]), waits=[sA],
               inc=sV)
            op("t", lambda e: e.matmul(sc[0][:], ONRF[:, 0:128],
                                       SCR[0:1, 1024:1536], start=True,
                                       stop=True), waits=[sV], inc=sT)
            op("t", lambda e: e.matmul(sc[1][:], ONRF[:, 0:128],
                                       SCR[0:1, 1536:2048], start=True,
                                       stop=True), inc=sT)
            for c in range(NCH):
                op("v", lambda e, c=c, tt=tt, src=src_bf: e.tensor_mul(
                    SCR[:, 3328:3840], src[:, c, tt * 512:(tt + 1) * 512],
                    sc[0][:]), waits=[sT], inc=sV)
                op("v", lambda e: e.tensor_sub(
                    SCR[:, 3328:3840], SCR[:, 3328:3840], sc[1][:]), inc=sV)
                op("v", lambda e, c=c, tt=tt: e.tensor_add(
                    RES[:, c * TOK + tt * 512: c * TOK + (tt + 1) * 512],
                    RES[:, c * TOK + tt * 512: c * TOK + (tt + 1) * 512],
                    SCR[:, 3328:3840]), inc=sV)

    post_ln_residual(BO)

    H_R = A1[:, 0:16384].rearrange("p (c t) -> p c t", c=32)
    G_R = A1[:, 16384:32768].rearrange("p (c t) -> p c t", c=32)
    FSL = [A1[:, 32768 + i * 4096: 32768 + (i + 1) * 4096] for i in range(2)]
    XF = A2[:, 0:4096].rearrange("p (c t) -> p c t", c=8)
    BOT = A1[:, 16384:20480].rearrange("p (c t) -> p c t", c=8)
    XB = SQB[0]

    sFF = [kb.sem(f"sFF{i}") for i in range(2)]

    def ln_rows_from(stat_src_bf, tt, chunks, inv):
        for c in range(chunks):
            op("a", lambda e, c=c: e.activation(
                out=SQB[1], in_=stat_src_bf(c), func=AF.Square),
               waits=[(sT, sT.n), (sV, sV.n)], inc=sA)
            op("t", lambda e, c=c: e.matmul(
                sc[0][0:1, :], ONC[:], stat_src_bf(c),
                start=(c == 0), stop=(c == chunks - 1)),
               waits=[(sV, sV.n)] if c == 0 else [], inc=sT)
            op("t", lambda e, c=c: e.matmul(
                sc[1][0:1, :], ONC[:], SQB[1],
                start=(c == 0), stop=(c == chunks - 1)),
               waits=[sA], inc=sT)
        op("a", lambda e: e.activation(out=SCR[0:1, 0:512],
                                       in_=sc[0][0:1, :], func=AF.Copy,
                                       scale=inv), waits=[sT], inc=sA)
        op("a", lambda e: e.activation(out=SCR[0:1, 512:1024],
                                       in_=SCR[0:1, 0:512], func=AF.Square),
           waits=[sA.at()], inc=sA)
        op("v", lambda e: e.tensor_scalar(
            out=SCR[0:1, 1024:1536], in0=sc[1][0:1, :], scalar1=inv,
            scalar2=LN_EPS, op0=ALU.mult, op1=ALU.add), waits=[sA], inc=sV)
        op("v", lambda e: e.tensor_sub(SCR[0:1, 1024:1536],
                                       SCR[0:1, 1024:1536],
                                       SCR[0:1, 512:1024]), inc=sV)
        op("v", lambda e: e.reciprocal(SCR[0:1, 1024:1536],
                                       SCR[0:1, 1024:1536]), inc=sV)
        op("a", lambda e: e.activation(out=SCR[0:1, 1024:1536],
                                       in_=SCR[0:1, 1024:1536],
                                       func=AF.Sqrt), waits=[sV], inc=sA)
        op("v", lambda e: e.tensor_mul(SCR[0:1, 1536:2048], SCR[0:1, 0:512],
                                       SCR[0:1, 1024:1536]), waits=[sA],
           inc=sV)
        op("t", lambda e: e.matmul(sc[0][:], ONRF[:, 0:128],
                                   SCR[0:1, 1024:1536], start=True,
                                   stop=True), waits=[sV], inc=sT)
        op("t", lambda e: e.matmul(sc[1][:], ONRF[:, 0:128],
                                   SCR[0:1, 1536:2048], start=True,
                                   stop=True), inc=sT)

    def ffn_mm_phase(wsrc, n_out, n_k, scol, evac, tt, act_chunks,
                     start_wait=(), wT=None):
        n_ob = n_out // 256
        n_kh = max(1, n_k // 16)
        kh_sz = min(n_k, 16)
        unit = 0
        wkey = id(wsrc)
        for ob in range(n_ob):
            for kh in range(n_kh):
                sl = unit % 2
                sem = sFF[sl]
                for k2 in range(kh_sz):
                    k = kh * kh_sz + k2
                    if tt == 0 or wT is None:
                        op("sy", lambda e, ob=ob, k=k, k2=k2, sl=sl,
                           wsrc=wsrc: e.dma_start(
                               out=FSL[sl][:, k2 * 256:(k2 + 1) * 256],
                               in_=wsrc.ap()[ob * 256:(ob + 1) * 256,
                                             k * 128:(k + 1) * 128],
                               transpose=True),
                           waits=(list(start_wait)
                                  + [(sT, kb._ffn_guard[sl]),
                                     (sGst[sl], kb._gst_slot[sl])])
                           if k2 == 0 else [],
                           inc=sem, dma=True)
                    else:
                        op("sy", lambda e, ob=ob, k=k, k2=k2, sl=sl,
                           wT=wT: e.dma_start(
                               out=FSL[sl][:, k2 * 256:(k2 + 1) * 256],
                               in_=wT.ap()[k * 128:(k + 1) * 128,
                                           ob * 256:(ob + 1) * 256]),
                           waits=(list(start_wait)
                                  + [(sT, kb._ffn_guard[sl]),
                                     (sGst[sl], kb._gst_slot[sl])]
                                  + kb._gst_full.get(wkey, []))
                           if k2 == 0 else [],
                           inc=sem, dma=True)
                ldw = sem.at()
                if tt == 0 and wT is not None:
                    for k2 in range(kh_sz):
                        k = kh * kh_sz + k2
                        op("g", lambda e, ob=ob, k=k, k2=k2, sl=sl, wT=wT:
                           e.dma_start(
                               out=wT.ap()[k * 128:(k + 1) * 128,
                                           ob * 256:(ob + 1) * 256],
                               in_=FSL[sl][:, k2 * 256:(k2 + 1) * 256]),
                           waits=[ldw],
                           inc=sGst[sl], dma=True)
                    kb._gst_slot[sl] = sGst[sl].n
                for oc in range(2):
                    bank = mm[(2 * (unit % 2) + oc) % 4] if False else mm[
                        (ob % 2) * 2 + oc]
                    for k2 in range(kh_sz):
                        k = kh * kh_sz + k2
                        op("t", lambda e, b=bank, k2=k2, oc=oc, sl=sl, tt=tt,
                           k=k, ac=act_chunks: e.matmul(
                               b[:], FSL[sl][:, k2 * 256 + oc * 128:
                                             k2 * 256 + oc * 128 + 128],
                               ac(k),
                               start=(k == 0), stop=(k == n_k - 1)),
                           waits=([ldw] if k2 == 0 else [])
                           + ([(sA, sA.n), (sV, sV.n)]
                              if (k == 0 and k2 == 0) else []),
                           inc=sT)
                    if kh == n_kh - 1:
                        evac(bank, ob * 2 + oc)
                kb._ffn_guard[sl] = sT.n
                unit += 1
        if tt == 0 and wT is not None:
            kb._gst_full[wkey] = [(sGst[0], sGst[0].n), (sGst[1], sGst[1].n)]

    kb._ffn_guard = [0, 0]
    kb._gst_slot = [0, 0]
    kb._gst_full = {}

    kb._ffn_guard = [sT.n, sT.n]
    for tt in range(NQB):
        def res_cast(c, tt=tt):
            return RES[:, c * TOK + tt * 512: c * TOK + (tt + 1) * 512]
        for c in range(NCH):
            op("v", lambda e, c=c, rc=res_cast: e.tensor_copy(XB, rc(c)),
               waits=[(sT, sT.n)], inc=sV)
            op("t", lambda e, c=c: e.matmul(
                sc[0][0:1, :], ONC[:], XB, start=(c == 0), stop=(c == NCH - 1)),
               waits=[sV], inc=sT)
            op("a", lambda e, c=c: e.activation(out=SQB[1], in_=XB,
                                                func=AF.Square),
               waits=[(sV, sV.n)], inc=sA)
            op("t", lambda e, c=c: e.matmul(
                sc[1][0:1, :], ONC[:], SQB[1], start=(c == 0),
                stop=(c == NCH - 1)), waits=[sA], inc=sT)
        op("a", lambda e: e.activation(out=SCR[0:1, 0:512], in_=sc[0][0:1, :],
                                       func=AF.Copy, scale=INV_DM),
           waits=[sT], inc=sA)
        op("a", lambda e: e.activation(out=SCR[0:1, 512:1024],
                                       in_=SCR[0:1, 0:512], func=AF.Square),
           waits=[sA.at()], inc=sA)
        op("v", lambda e: e.tensor_scalar(
            out=SCR[0:1, 1024:1536], in0=sc[1][0:1, :], scalar1=INV_DM,
            scalar2=LN_EPS, op0=ALU.mult, op1=ALU.add), waits=[sA], inc=sV)
        op("v", lambda e: e.tensor_sub(SCR[0:1, 1024:1536],
                                       SCR[0:1, 1024:1536],
                                       SCR[0:1, 512:1024]), inc=sV)
        op("v", lambda e: e.reciprocal(SCR[0:1, 1024:1536],
                                       SCR[0:1, 1024:1536]), inc=sV)
        op("a", lambda e: e.activation(out=SCR[0:1, 1024:1536],
                                       in_=SCR[0:1, 1024:1536], func=AF.Sqrt),
           waits=[sV], inc=sA)
        op("v", lambda e: e.tensor_mul(SCR[0:1, 1536:2048], SCR[0:1, 0:512],
                                       SCR[0:1, 1024:1536]), waits=[sA],
           inc=sV)
        op("t", lambda e: e.matmul(sc[0][:], ONRF[:, 0:128],
                                   SCR[0:1, 1024:1536], start=True,
                                   stop=True), waits=[sV], inc=sT)
        op("t", lambda e: e.matmul(sc[1][:], ONRF[:, 0:128],
                                   SCR[0:1, 1536:2048], start=True,
                                   stop=True), inc=sT)
        for c in range(NCH):
            op("v", lambda e, c=c, rc=res_cast: e.tensor_mul(
                SCR[:, 3328:3840], rc(c), sc[0][:]), waits=[sT], inc=sV)
            op("v", lambda e, c=c: e.tensor_sub(
                SCR[:, 3328:3840], SCR[:, 3328:3840], sc[1][:]), inc=sV)
            op("v", lambda e, c=c: e.tensor_copy(XF[:, c, 0:512],
                                                 SCR[:, 3328:3840]), inc=sV)

        def xf_chunk(k):
            return XF[:, k, 0:512]

        def evac_u1(bank, o, tt=tt):
            op("a", lambda e, b=bank, o=o: e.activation(
                out=G_R[:, o, 0:512], in_=b[:], func=AF.Silu,
                scale=FS1[:, o:o + 1]), waits=[sT], inc=sA)
        ffn_mm_phase(w1_bf, 4096, NCH, FS1, evac_u1, tt, xf_chunk,
                     start_wait=[(sPrA, 1280)], wT=w1T)
        def evac_u2(bank, o, tt=tt):
            op("v", lambda e, b=bank, o=o: e.scalar_tensor_tensor(
                out=H_R[:, o, 0:512], in0=b[:], scalar=FS2[:, o:o + 1],
                in1=G_R[:, o, 0:512], op0=ALU.mult, op1=ALU.mult),
               waits=[sT, (sA, sA.n)], inc=sV)
        ffn_mm_phase(w2_bf, 4096, NCH, FS2, evac_u2, tt, xf_chunk,
                     start_wait=[(sPrA, 2304)], wT=w2T)

        def h_chunk(k):
            return H_R[:, k, 0:512]

        def g_chunk(k):
            return G_R[:, k, 0:512]

        def evac_g1(bank, o, tt=tt):
            op("a", lambda e, b=bank, o=o: e.activation(
                out=G_R[:, o, 0:512], in_=b[:], func=AF.Gelu,
                scale=FH1[:, o:o + 1]), waits=[sT, (sV, sV.n)], inc=sA)
        ffn_mm_phase(h1_bf, 4096, 32, FH1, evac_g1, tt, h_chunk,
                     start_wait=[(sPrA, 6400)], wT=h1T)
        def evac_g2(bank, o, tt=tt):
            op("a", lambda e, b=bank, o=o: e.activation(
                out=H_R[:, o, 0:512], in_=b[:], func=AF.Gelu,
                scale=FH2[:, o:o + 1]), waits=[sT, (sV, sV.n)], inc=sA)
        ffn_mm_phase(h2_bf, 4096, 32, FH2, evac_g2, tt, g_chunk,
                     start_wait=[(sPrB, 4096)], wT=h2T)
        def evac_w3(bank, o, tt=tt):
            op("a", lambda e, b=bank, o=o: e.activation(
                out=BOT[:, o, 0:512], in_=b[:], func=AF.Identity,
                scale=FS3[:, o:o + 1], bias=0.0), waits=[sT, (sV, sV.n)],
               inc=sA)
        ffn_mm_phase(w3_bf, DM, 32, FS3, evac_w3, tt, h_chunk,
                     start_wait=[(sPrB, 5120)], wT=w3T)

        def bot_chunk(c):
            return BOT[:, c, 0:512]
        ln_rows_from(bot_chunk, tt, NCH, INV_DM)
        for c in range(NCH):
            op("v", lambda e, c=c, tt=tt: e.tensor_mul(
                SCR[:, 3328:3840], BOT[:, c, 0:512], sc[0][:]),
               waits=[sT], inc=sV)
            op("v", lambda e: e.tensor_sub(SCR[:, 3328:3840],
                                           SCR[:, 3328:3840], sc[1][:]),
               inc=sV)
            op("v", lambda e, c=c, tt=tt: e.tensor_add(
                RES[:, c * TOK + tt * 512: c * TOK + (tt + 1) * 512],
                RES[:, c * TOK + tt * 512: c * TOK + (tt + 1) * 512],
                SCR[:, 3328:3840]), inc=sV)

    Q2 = A1[:, 0:2048]
    KL2 = A1[:, 2048:4352]
    VL2 = A1[:, 4352:6692].rearrange("p (c t) -> p c t", c=18)
    EM1 = A1[:, 6692:9764].rearrange("p (c t) -> p c t", c=6)
    ATT1 = A1[:, 24576:40960].rearrange("p (c t) -> p c t", c=8)
    BO1 = A1[:, 0:16384].rearrange("p (c t) -> p c t", c=8)
    QKVQ = WS[:, 0:2048]
    QKVV = WS[:, 4096:5120]
    CSLD = WS[:, 5120:6144].bitcast(F16)
    xn_dram = nc.dram_tensor("xn_dram", [NCH, 128, TOK], BF16)

    LQB0 = WS[:, 2048:2560]
    LQB1 = WS[:, 2560:3072]
    sXn = kb.sem("sXn")
    sXf = [kb.sem(f"sXf{i}") for i in range(2)]
    sQk = kb.sem("sQk")
    sCs = kb.sem("sCs")
    sBd = kb.sem("sBd")
    sPj = [kb.sem(f"sPj{i}") for i in range(2)]

    for tt in range(NQB):
        def res_cast2(c, tt=tt):
            return RES[:, c * TOK + tt * 512: c * TOK + (tt + 1) * 512]
        for c in range(NCH):
            op("v", lambda e, c=c, rc=res_cast2: e.tensor_copy(XB, rc(c)),
               waits=[(sT, sT.n)] + ([(sXn, sXn.n)] if c == 0 else []),
               inc=sV)
            op("t", lambda e, c=c: e.matmul(
                sc[0][0:1, :], ONC[:], XB, start=(c == 0), stop=(c == NCH - 1)),
               waits=[sV], inc=sT)
            op("a", lambda e, c=c: e.activation(out=SQB[1], in_=XB,
                                                func=AF.Square),
               waits=[(sV, sV.n), (sT, sT.n)], inc=sA)
            op("t", lambda e, c=c: e.matmul(
                sc[1][0:1, :], ONC[:], SQB[1], start=(c == 0),
                stop=(c == NCH - 1)), waits=[sA], inc=sT)
        op("a", lambda e: e.activation(out=SCR[0:1, 0:512], in_=sc[0][0:1, :],
                                       func=AF.Copy, scale=INV_DM),
           waits=[sT], inc=sA)
        op("a", lambda e: e.activation(out=SCR[0:1, 512:1024],
                                       in_=SCR[0:1, 0:512], func=AF.Square),
           waits=[sA.at()], inc=sA)
        op("v", lambda e: e.tensor_scalar(
            out=SCR[0:1, 1024:1536], in0=sc[1][0:1, :], scalar1=INV_DM,
            scalar2=LN_EPS, op0=ALU.mult, op1=ALU.add), waits=[sA], inc=sV)
        op("v", lambda e: e.tensor_sub(SCR[0:1, 1024:1536],
                                       SCR[0:1, 1024:1536],
                                       SCR[0:1, 512:1024]), inc=sV)
        op("v", lambda e: e.reciprocal(SCR[0:1, 1024:1536],
                                       SCR[0:1, 1024:1536]), inc=sV)
        op("a", lambda e: e.activation(out=SCR[0:1, 1024:1536],
                                       in_=SCR[0:1, 1024:1536], func=AF.Sqrt),
           waits=[sV], inc=sA)
        op("v", lambda e: e.tensor_mul(SCR[0:1, 1536:2048], SCR[0:1, 0:512],
                                       SCR[0:1, 1024:1536]), waits=[sA],
           inc=sV)
        op("t", lambda e: e.matmul(sc[0][:], ONRF[:, 0:128],
                                   SCR[0:1, 1024:1536], start=True,
                                   stop=True), waits=[sV], inc=sT)
        op("t", lambda e: e.matmul(sc[1][:], ONRF[:, 0:128],
                                   SCR[0:1, 1536:2048], start=True,
                                   stop=True), inc=sT)
        for c in range(NCH):
            op("v", lambda e, c=c, rc=res_cast2: e.tensor_mul(
                SCR[:, 3328:3840], rc(c), sc[0][:]), waits=[sT], inc=sV)
            op("v", lambda e, c=c: e.tensor_sub(
                SCR[:, 3328:3840], SCR[:, 3328:3840], sc[1][:]), inc=sV)
            op("v", lambda e, c=c: e.tensor_copy(XF[:, c, 0:512],
                                                 SCR[:, 3328:3840]), inc=sV)
            op("sy", lambda e, c=c, tt=tt: e.dma_start(
                out=xn_dram.ap()[c, :, tt * 512:(tt + 1) * 512],
                in_=XF[:, c, 0:512]), waits=[(sV, sV.n)], inc=sXn, dma=True)
    xn_all = sXn.at()

    n_kt1 = 6
    kb._cs_guard = 0
    g_xf1 = [0]
    g_qkv = 0
    for grp in range(8):
        for part, (dst, col0) in enumerate(
                ((QKVQ, grp * 128), (QKVQ, 1024 + grp * 128),
                 (QKVV, 2048 + grp * 128))):
            off = 0 if part < 2 else 0
            for d in range(NCH):
                dsl = dst[:, (part % 2) * 1024 + d * 128:(part % 2) * 1024 + d * 128 + 128] if part < 2 else dst[:, d * 128:(d + 1) * 128]
                op("sy", lambda e, dsl=dsl, col0=col0, d=d: e.dma_start(
                    out=dsl,
                    in_=qkv_bf.ap()[col0:col0 + 128, d * 128:(d + 1) * 128],
                    transpose=True),
                   waits=[(sPrB, prB_qkv_done), (sT, g_qkv),
                          ], inc=sQk, dma=True)
        qkv_ld = sQk.at()

        for tt in range(NQB):
            xsl = sXf[tt % 2]
            for c in range(NCH):
                op("sy", lambda e, c=c, tt=tt: e.dma_start(
                    out=XF[:, c, 0:512],
                    in_=xn_dram.ap()[c, :, tt * 512:(tt + 1) * 512]),
                   waits=[xn_all, (sT, g_xf1[0])], inc=xsl, dma=True)
            xf_ld = xsl.at()
            op("sy", lambda e, tt=tt: e.dma_start(
                out=CSLD[:, 0:512], in_=cosb[:, tt * 512:(tt + 1) * 512]),
               waits=[(sV, kb._cs_guard)], inc=sCs, dma=True)
            op("sy", lambda e, tt=tt: e.dma_start(
                out=CSLD[:, 512:1024], in_=sinb[:, tt * 512:(tt + 1) * 512]),
               inc=sCs, dma=True)
            cs_ld = sCs.at()
            for qk in range(2):
                bank = mm[2 + qk]
                for d in range(NCH):
                    op("t", lambda e, b=bank, qk=qk, d=d, tt=tt: e.matmul(
                        b[:], QKVQ[:, qk * 1024 + d * 128:
                                   qk * 1024 + d * 128 + 128],
                        XF[:, d, 0:512], start=(d == 0), stop=(d == NCH - 1)),
                       waits=[qkv_ld, xf_ld, (sA, sA.n), (sV, sV.n)]
                       if d == 0 else [], inc=sT)
                dst0 = Q2 if qk == 0 else KL2
                colw = tt * 512 if qk == 0 else 256 + tt * 512
                SCRq = SCR[:, 2048:2560]
                SCRt = SCR[:, 2560:3072]
                SCRu = SCR[:, 3072:3584]
                op("a", lambda e, b=bank: e.copy(SCRq, b[:]),
                   waits=[sT, (sPrA, sPrA.n)], inc=sA)
                op("a", lambda e: e.activation(out=SCRt, in_=SCRq,
                                               func=AF.Square),
                   waits=[sA.at()], inc=sA)
                op("v", lambda e: e.tensor_copy(LQB0, SCRq), waits=[sA],
                   inc=sV)
                op("v", lambda e: e.tensor_copy(LQB1, SCRt), inc=sV)
                for hh in range(2):
                    b0 = 64 * hh
                    onc_s = ONC[0:64, :] if b0 == 0 else ONC[64:128, :]
                    op("t", lambda e, b0=b0, onc_s=onc_s: e.matmul(
                        sc[0][0:1, 0:512], onc_s, LQB0[b0:b0 + 64, :],
                        start=True, stop=True), waits=[(sV, sV.n)], inc=sT)
                    op("t", lambda e, b0=b0, onc_s=onc_s: e.matmul(
                        sc[1][0:1, 0:512], onc_s, LQB1[b0:b0 + 64, :],
                        start=True, stop=True), inc=sT)
                    op("a", lambda e: e.activation(
                        out=SCR[0:1, 0:512], in_=sc[0][0:1, :], func=AF.Copy,
                        scale=1.0 / 64), waits=[sT], inc=sA)
                    op("a", lambda e: e.activation(
                        out=SCR[0:1, 512:1024], in_=SCR[0:1, 0:512],
                        func=AF.Square), waits=[sA.at()], inc=sA)
                    op("v", lambda e: e.tensor_scalar(
                        out=SCR[0:1, 1024:1536], in0=sc[1][0:1, :],
                        scalar1=1.0 / 64, scalar2=LN_EPS, op0=ALU.mult,
                        op1=ALU.add), waits=[sA], inc=sV)
                    op("v", lambda e: e.tensor_sub(
                        SCR[0:1, 1024:1536], SCR[0:1, 1024:1536],
                        SCR[0:1, 512:1024]), inc=sV)
                    op("v", lambda e: e.reciprocal(
                        SCR[0:1, 1024:1536], SCR[0:1, 1024:1536]), inc=sV)
                    op("a", lambda e: e.activation(
                        out=SCR[0:1, 1024:1536], in_=SCR[0:1, 1024:1536],
                        func=AF.Sqrt), waits=[sV], inc=sA)
                    op("v", lambda e: e.tensor_mul(
                        SCR[0:1, 1536:2048], SCR[0:1, 0:512],
                        SCR[0:1, 1024:1536]), waits=[sA], inc=sV)
                    op("t", lambda e, b0=b0: e.matmul(
                        bcb[b0:b0 + 64, :][0:64, :] if b0 else bcb[0:64, :],
                        ONRF[:, 0:64], SCR[0:1, 1024:1536],
                        start=True, stop=True, tile_position=(0, b0)),
                       waits=[sV, (sA, sA.n), (sV, sV.n)], inc=sT)
                    op("t", lambda e, b0=b0: e.matmul(
                        mm[0][b0:b0 + 64, :][0:64, :] if b0 else mm[0][0:64, :],
                        ONRF[:, 0:64], SCR[0:1, 1536:2048],
                        start=True, stop=True, tile_position=(0, b0)),
                       inc=sT)
                gcol = QNWB[:, 2 * qk:2 * qk + 1]
                bcol = QNWB[:, 2 * qk + 1:2 * qk + 2]
                op("v", lambda e: e.tensor_mul(SCRt, SCRq, bcb[:]),
                   waits=[sT], inc=sV)
                op("v", lambda e: e.tensor_sub(SCRt, SCRt, mm[0][:]), inc=sV)
                op("a", lambda e, gcol=gcol, bcol=bcol: e.activation(
                    out=SCRq, in_=SCRt, func=AF.Identity, scale=gcol,
                    bias=bcol), waits=[sV], inc=sA)
                op("v", lambda e: e.tensor_copy(LQB1, SCRq), waits=[sA],
                   inc=sV)
                op("t", lambda e: e.matmul(
                    mm[0][:], ROTT[:], LQB1, start=True, stop=True),
                   waits=[sV, (sA, sA.n), (sV, sV.n)], inc=sT)
                op("v", lambda e: e.tensor_mul(
                    SCRt, SCRq, CSLD[:, 0:512]),
                   waits=[(sCs, cs_ld[1])], inc=sV)
                op("v", lambda e: e.tensor_mul(SCRu, mm[0][:],
                                               CSLD[:, 512:1024]),
                   waits=[sT], inc=sV)
                op("v", lambda e, dst0=dst0, colw=colw: e.tensor_add(
                    dst0[:, colw:colw + 512], SCRt, SCRu),
                   waits=[(sT, g_qkv)], inc=sV)
            for sub in range(4):
                bank = mm[1]
                for d in range(NCH):
                    op("t", lambda e, b=bank, sub=sub, d=d: e.matmul(
                        b[:, 0:128], XF[:, d, sub * 128:(sub + 1) * 128],
                        QKVV[:, d * 128:(d + 1) * 128],
                        start=(d == 0), stop=(d == NCH - 1)),
                       waits=[(sV, sV.n)] if d == 0 else [], inc=sT)
                r = 2 + tt * 4 + sub
                op("v", lambda e, b=bank, r=r: e.tensor_copy(
                    VL2[:, r, 0:130].rearrange("p (h z) -> p h z", h=2)[:, :, 0:64],
                    b[:, 0:128].rearrange("p (h d) -> p h d", h=2)),
                   waits=[sT], inc=sV)
            g_xf1[0] = sT.n
            kb._cs_guard = sV.n
        for r in range(18):
            op("v", lambda e, r=r: e.memset(
                VL2[:, r, 0:130].rearrange("p (h z) -> p h z", h=2)[:, :, 64:65],
                1.0), inc=sV)
        op("g", lambda e, grp=grp: e.dma_start(
            out=bdk_in[grp].ap(), in_=KL2[:, 2048:2304]),
           waits=[(sV, sV.n)], inc=sBd, dma=True)
        op("g", lambda e, grp=grp: e.dma_start(
            out=bdv_in[grp].ap()[0:128, :], in_=VL2[:, 16, 0:130]),
           waits=[sBd.at()], inc=sBd, dma=True)
        op("g", lambda e, grp=grp: e.dma_start(
            out=bdv_in[grp].ap()[128:256, :], in_=VL2[:, 17, 0:130]),
           waits=[sBd.at()], inc=sBd, dma=True)
        op("g", lambda e, grp=grp: e.collective_compute(
            "AllGather", ALU.bypass,
            replica_groups=[[0, 1], [2, 3], [4, 5], [6, 7]],
            ins=[bdk_in[grp].ap().opt()], outs=[bdk_out[grp].ap().opt()]),
           waits=[sBd.at()], inc=sG)
        op("g", lambda e, grp=grp: e.collective_compute(
            "AllGather", ALU.bypass,
            replica_groups=[[0, 1], [2, 3], [4, 5], [6, 7]],
            ins=[bdv_in[grp].ap().opt()], outs=[bdv_out[grp].ap().opt()]),
           inc=sG)
        op("g", lambda e, grp=grp: e.dma_start(
            out=KL2[:, 0:256], in_=bdk_out[grp].ap()[0:128, :]),
           waits=[sG.at()], inc=sBd, dma=True)
        op("g", lambda e, grp=grp: e.dma_start(
            out=VL2[:, 0, 0:130], in_=bdv_out[grp].ap()[0:128, :]),
           waits=[sBd.at()], inc=sBd, dma=True)
        op("g", lambda e, grp=grp: e.dma_start(
            out=VL2[:, 1, 0:130], in_=bdv_out[grp].ap()[128:256, :]),
           waits=[sBd.at()], inc=sBd, dma=True)
        bd_done = sBd.at()

        for hh in range(2):
            b0 = 64 * hh
            cnt = 0
            for col0, w in [(0, 512), (512, 512), (1024, 512), (1536, 512)]:
                op("a", lambda e, b0=b0, col0=col0, w=w: e.activation(
                    out=SCR[b0:b0 + 64, 0:512][0:64, 0:w],
                    in_=Q2[b0:b0 + 64, col0:col0 + w], func=AF.Square),
                   waits=[(sV, sV.n)], inc=sA)
                op("v", lambda e, b0=b0, w=w: e.tensor_copy(
                    SQB[0][b0:b0 + 64, 0:w], SCR[b0:b0 + 64, 0:w]),
                   waits=[sA], inc=sV)
                op("t", lambda e, b0=b0, w=w: e.matmul(
                    sc[0][0:1, 0:w], ONC[b0:b0 + 64, :]
                    if b0 == 0 else ONC[64:128, :],
                    SQB[0][b0:b0 + 64, 0:w], start=True, stop=True),
                   waits=[sV], inc=sT)
                op("v", lambda e, hh=hh, cnt=cnt, w=w: e.reduce_max(
                    STT[0:1, 526 + hh * 16 + cnt:527 + hh * 16 + cnt],
                    sc[0][0:1, 0:w], axis=AX.X), waits=[sT], inc=sV)
                cnt += 1
            for col0, w in [(0, 512), (512, 512), (1024, 512), (1536, 512),
                            (2048, 256)]:
                op("a", lambda e, b0=b0, col0=col0, w=w: e.activation(
                    out=SCR[b0:b0 + 64, 0:512][0:64, 0:w],
                    in_=KL2[b0:b0 + 64, col0:col0 + w], func=AF.Square),
                   waits=[(sV, sV.n), bd_done], inc=sA)
                op("v", lambda e, b0=b0, w=w: e.tensor_copy(
                    SQB[0][b0:b0 + 64, 0:w], SCR[b0:b0 + 64, 0:w]),
                   waits=[sA], inc=sV)
                op("t", lambda e, b0=b0, w=w: e.matmul(
                    sc[0][0:1, 0:w], ONC[b0:b0 + 64, :]
                    if b0 == 0 else ONC[64:128, :],
                    SQB[0][b0:b0 + 64, 0:w], start=True, stop=True),
                   waits=[sV], inc=sT)
                op("v", lambda e, hh=hh, cnt=cnt, w=w: e.reduce_max(
                    STT[0:1, 526 + hh * 16 + cnt:527 + hh * 16 + cnt],
                    sc[0][0:1, 0:w], axis=AX.X), waits=[sT], inc=sV)
                cnt += 1
            op("v", lambda e, hh=hh: e.reduce_max(
                STT[0:1, 516 + hh:517 + hh],
                STT[0:1, 526 + hh * 16:526 + hh * 16 + 4], axis=AX.X), inc=sV)
            op("v", lambda e, hh=hh: e.reduce_max(
                STT[0:1, 518 + hh:519 + hh],
                STT[0:1, 526 + hh * 16 + 4:526 + hh * 16 + 9], axis=AX.X),
               inc=sV)
        op("a", lambda e: e.activation(out=STT[0:1, 520:524],
                                       in_=STT[0:1, 516:520], func=AF.Sqrt),
           waits=[sV], inc=sA)
        op("v", lambda e: e.tensor_mul(STT[0:1, 524:526], STT[0:1, 520:522],
                                       STT[0:1, 522:524]), waits=[sA], inc=sV)
        op("v", lambda e: e.tensor_scalar(
            out=STT[0:1, 512:514], in0=STT[0:1, 524:526], scalar1=-0.125,
            scalar2=None, op0=ALU.mult), inc=sV)
        op("v", lambda e, grp=grp: e.tensor_add(
            STT[0:1, 514:516], STT[0:1, 512:514], SNK[:, 2 * grp:2 * grp + 2]),
           inc=sV)
        op("a", lambda e: e.activation(out=STT[0:1, 514:516],
                                       in_=STT[0:1, 514:516], func=AF.Exp),
           waits=[sV], inc=sA)
        op("t", lambda e: e.matmul(bcb[:, 0:2], ONRF[:, 0:128],
                                   STT[0:1, 512:514], start=True, stop=True),
           waits=[sV, sA, (sA, g_negg), (sV, sV.n)], inc=sT)
        op("t", lambda e: e.matmul(bcb[:, 2:4], ONRF[:, 0:128],
                                   STT[0:1, 514:516], start=True, stop=True),
           inc=sT)
        op("a", lambda e, grp=grp: e.copy(NEGG[:, 0:2], bcb[:, 0:2]),
           waits=[sT], inc=sA)
        op("a", lambda e: e.copy(ESNK[:], bcb[:, 2:4]), waits=[sA.at()],
           inc=sA)
        g_negg = sA.n

        for j in range(NQB):
            op("t", lambda e, j=j: e.matmul(
                bcb[:], ONRH[:, 0:128], QIX[:, j * 512:(j + 1) * 512],
                start=True, stop=True), waits=[sA, sV], inc=sT)
            qidx_done = sT.n
            for kt in range(n_kt1):
                vchunk = 4 * j + kt
                msl = EM1[:, 4 + (kt % 2), 0:512]
                op("v", lambda e, vchunk=vchunk: e.tensor_scalar(
                    out=SCR[:, 2048:2560], in0=bcb[:],
                    scalar1=KIX1[:, vchunk:vchunk + 1], scalar2=None,
                    op0=ALU.is_ge), waits=[(sT, qidx_done)], inc=sV)
                op("v", lambda e, vchunk=vchunk: e.tensor_scalar(
                    out=SCR[:, 2560:3072], in0=bcb[:],
                    scalar1=KIX1[:, 18 + vchunk:19 + vchunk], scalar2=None,
                    op0=ALU.is_le), inc=sV)
                op("v", lambda e, msl=msl: e.tensor_mul(
                    msl, SCR[:, 2048:2560], SCR[:, 2560:3072]), inc=sV)
                for sub in range(2):
                    b0 = 64 * sub
                    esl = EM1[:, sub, 0:512]
                    op("t", lambda e, b0=b0, j=j, kt=kt, sub=sub: e.matmul(
                        sc[sub][:],
                        KL2[b0:b0 + 64, 512 * j + 128 * kt:
                            512 * j + 128 * kt + 128],
                        Q2[b0:b0 + 64, j * 512:(j + 1) * 512],
                        start=True, stop=True, tile_position=(b0, 0)),
                       waits=[(sA, sA.n)], inc=sT)
                    op("a", lambda e, sub=sub, esl=esl: e.activation(
                        out=esl, in_=sc[sub][:], func=AF.Exp, scale=0.125,
                        bias=NEGG[:, sub:sub + 1]), waits=[sT], inc=sA)
                    op("v", lambda e, esl=esl, msl=msl: e.tensor_mul(
                        esl, esl, msl), waits=[sA], inc=sV)
                    op("t", lambda e, sub=sub, kt=kt, j=j, esl=esl: e.matmul(
                        mm[sub][0:65, :],
                        VL2[:, 4 * j + kt, sub * 65: sub * 65 + 65],
                        esl, start=(kt == 0), stop=(kt == n_kt1 - 1)),
                       waits=[sV] + ([(sA, sA.n)] if kt == 0 else []),
                       inc=sT)
            for hh in range(2):
                head = 2 * grp + hh
                op("a", lambda e, hh=hh: e.copy(SCR[0:65, 0:512],
                                                mm[hh][0:65, :]),
                   waits=[sT, (sV, sV.n)], inc=sA)
                op("v", lambda e, hh=hh: e.tensor_scalar(
                    out=SCR[64:65, 0:512], in0=SCR[64:65, 0:512],
                    scalar1=ESNK[64:65, hh:hh + 1], scalar2=None,
                    op0=ALU.add), waits=[sA], inc=sV)
                op("v", lambda e: e.reciprocal(SCR[64:65, 512:1024],
                                               SCR[64:65, 0:512]),
                   inc=sV)
                op("t", lambda e: e.matmul(bcb[0:64, :], ONFF[64:65, 0:64],
                                           SCR[64:65, 512:1024],
                                           start=True, stop=True,
                                           tile_position=(64, 0)),
                   waits=[sV], inc=sT)
                dch, dbase = head // 2, 64 * (head % 2)
                if dbase == 0:
                    op("v", lambda e, dch=dch, j=j: e.tensor_mul(
                        ATT1[0:64, dch, j * 512:(j + 1) * 512],
                        SCR[0:64, 0:512], bcb[0:64, :]), waits=[sT], inc=sV)
                else:
                    oslot = kb._oddslot
                    osem = sGAtt[oslot]
                    op("v", lambda e, oslot=oslot: e.tensor_mul(
                        EM1[0:64, 2 + oslot, 0:512], SCR[0:64, 0:512],
                        bcb[0:64, :]), waits=[sT, osem.at()], inc=sV)
                    op("g", lambda e, oslot=oslot, dch=dch, j=j: e.dma_start(
                        out=ATT1[64:128, dch, j * 512:(j + 1) * 512],
                        in_=EM1[0:64, 2 + oslot, 0:512]),
                       waits=[(sV, sV.n)], inc=osem, dma=True)
                    kb._oddslot = 1 - oslot
        g_qkv = sT.n
        g_kv2 = sT.n

    attn1_done = [(sT, sT.n), (sA, sA.n), (sV, sV.n),
                  sGAtt[0].at(), sGAtt[1].at()]
    pj_guard = 0
    for oh in range(2):
        spj = sPj[oh]
        for d in range(8):
            woff = d * 512 if d < 4 else 4096 + (d - 4) * 512
            op("sy", lambda e, oh=oh, d=d, woff=woff: e.dma_start(
                out=WS[:, woff:woff + 512],
                in_=proj_bf.ap()[oh * 512:(oh + 1) * 512,
                                 d * 128:(d + 1) * 128],
                transpose=True),
               waits=[(sPrB, prB_all)] + attn1_done
               + ([(sT, pj_guard)] if oh == 1 else []),
               inc=spj, dma=True)
        for oc in range(4):
            o = oh * 4 + oc
            for tt in range(NQB):
                bank = mm[tt % 2]
                for d in range(8):
                    woff = d * 512 if d < 4 else 4096 + (d - 4) * 512
                    op("t", lambda e, b=bank, d=d, oc=oc, tt=tt, woff=woff:
                       e.matmul(
                           b[:], WS[:, woff + oc * 128: woff + oc * 128 + 128],
                           ATT1[:, d, tt * 512:(tt + 1) * 512],
                           start=(d == 0), stop=(d == 7)),
                       waits=([(spj, spj.n), (sV, sV.n), (sA, sA.n)]
                              + attn1_done if d == 0 else []),
                       inc=sT if d == 7 else None)
                op("a", lambda e, b=bank, o=o, tt=tt: e.activation(
                    out=BO1[:, o, tt * 512:(tt + 1) * 512], in_=b[:],
                    func=AF.Identity, bias=PBC[:, o:o + 1], scale=1.0),
                   waits=[sT], inc=sA)
        if oh == 0:
            pj_guard = sT.n
    post_ln_residual(BO1)

    QMAXI = 1.0 / 32767.0
    XQ = A1[:, 0:16384].rearrange("p (c t) -> p c t", c=8)
    BOTF = A1[:, 0:16384].rearrange("p (c t) -> p c t", c=8)
    TG = A1[:, 16384:28672].rearrange("p (c t) -> p c t", c=24)
    TU = A1[:, 28672:40960].rearrange("p (c t) -> p c t", c=24)
    WQO = A2[:, 0:1024]
    sWq = kb.sem("sWq")
    sWqS = kb.sem("sWqS")
    sSp = kb.sem("sSp")
    sSpH = [kb.sem(f"sSpH{i}") for i in range(2)]
    sRl = [kb.sem(f"sRl{i}") for i in range(2)]
    sMx = kb.sem("sMx")
    sCc = kb.sem("sCc")

    ffn1_begin = [(sT, sT.n), (sA, sA.n), (sV, sV.n)]

    WMX = SCR[:, 1024:1056]
    WSC = SCR[:, 1056:1088]
    WRC = SCR[:, 1088:1120]
    WF = SCR[:, 2048:3072]
    wprep_done = {}

    def lpbq_prep(src, dst, nrow, ncb, eng):
        for i in range(nrow // 128):
            for cb in range(ncb):
                op("sy", lambda e, src=src, i=i, cb=cb: e.dma_start(
                    out=SCR[:, 0:1024],
                    in_=src[i * 128:(i + 1) * 128, cb * 1024:(cb + 1) * 1024]),
                   waits=ffn1_begin + [(kb.self_sems[eng], kb._wp_guard)],
                   inc=sWq, dma=True)
                ldw = sWq.at()
                op(eng, lambda e: e.tensor_copy(WF, SCR[:, 0:1024]),
                   waits=[ldw], inc=None)
                op(eng, lambda e: e.reduce_max(
                    WMX, WF.rearrange("p (g i) -> p g i", i=32),
                    axis=AX.X, apply_absolute_value=True), inc=None)
                op(eng, lambda e: e.tensor_scalar(
                    out=WSC, in0=WMX, scalar1=1.0 / 7.0, scalar2=1e-12,
                    op0=ALU.mult, op1=ALU.max), inc=None)
                op(eng, lambda e: e.reciprocal(WRC, WSC), inc=None)
                op(eng, lambda e: e.tensor_tensor(
                    WF.rearrange("p (g i) -> p g i", i=32), 
                    WF.rearrange("p (g i) -> p g i", i=32),
                    WRC[:, :, None].broadcast_to([128, 32, 32]),
                    op=ALU.mult), inc=None)
                op(eng, lambda e: e.tensor_scalar(
                    out=WF, in0=WF, scalar1=MAGIC, scalar2=MAGIC,
                    op0=ALU.add, op1=ALU.subtract), inc=None)
                op(eng, lambda e: e.tensor_scalar(
                    out=WF, in0=WF, scalar1=7.0, scalar2=-8.0,
                    op0=ALU.min, op1=ALU.max), inc=None)
                op(eng, lambda e: e.tensor_tensor(
                    WQO.rearrange("p (g i) -> p g i", i=32),
                    WF.rearrange("p (g i) -> p g i", i=32),
                    WSC[:, :, None].broadcast_to([128, 32, 32]),
                    op=ALU.mult), waits=[(sWqS, sWqS.n)], inc=None)
                kb._wp_guard = kb.self_sems[eng].n
                op("g", lambda e, dst=dst, i=i, cb=cb: e.dma_start(
                    out=dst.ap()[i * 128:(i + 1) * 128,
                                 cb * 1024:(cb + 1) * 1024],
                    in_=WQO),
                   waits=[(kb.self_sems[eng], kb._wp_guard)], inc=sWqS,
                   dma=True)

    kb._wp_guard = 0
    lpbq_prep(w_gate, wg_bf, 3072, 1, "v")
    wprep_done["g"] = sWqS.at()
    lpbq_prep(w_up, wu_bf, 3072, 1, "v")
    wprep_done["u"] = sWqS.at()
    lpbq_prep(w_down, wd_bf, 1024, 3, "v")
    wprep_done["d"] = sWqS.at()

    MAXC = CW[:, 24:25]
    op("v", lambda e: e.memset(MAXC, 0.0), inc=sV)
    op("v", lambda e: e.memset(STT[0:1, 560:566], 0.0), inc=sV)
    for tt in range(NQB):
        def res_c3(c, tt=tt):
            return RES[:, c * TOK + tt * 512: c * TOK + (tt + 1) * 512]
        for c in range(NCH):
            op("v", lambda e, c=c, rc=res_c3: e.tensor_copy(XB, rc(c)),
               waits=[(sT, sT.n), (sSp, sSp.n)], inc=sV)
            op("t", lambda e, c=c: e.matmul(
                sc[0][0:1, :], ONC[:], XB, start=(c == 0), stop=(c == NCH - 1)),
               waits=[sV], inc=sT)
            op("a", lambda e, c=c: e.activation(out=SQB[1], in_=XB,
                                                func=AF.Square),
               waits=[(sV, sV.n), (sT, sT.n)], inc=sA)
            op("t", lambda e, c=c: e.matmul(
                sc[1][0:1, :], ONC[:], SQB[1], start=(c == 0),
                stop=(c == NCH - 1)), waits=[sA], inc=sT)
        op("a", lambda e: e.activation(out=SCR[0:1, 0:512], in_=sc[0][0:1, :],
                                       func=AF.Copy, scale=INV_DM),
           waits=[sT], inc=sA)
        op("a", lambda e: e.activation(out=SCR[0:1, 512:1024],
                                       in_=SCR[0:1, 0:512], func=AF.Square),
           waits=[sA.at()], inc=sA)
        op("v", lambda e: e.tensor_scalar(
            out=SCR[0:1, 1024:1536], in0=sc[1][0:1, :], scalar1=INV_DM,
            scalar2=LN_EPS, op0=ALU.mult, op1=ALU.add), waits=[sA], inc=sV)
        op("v", lambda e: e.tensor_sub(SCR[0:1, 1024:1536],
                                       SCR[0:1, 1024:1536],
                                       SCR[0:1, 512:1024]), inc=sV)
        op("v", lambda e: e.reciprocal(SCR[0:1, 1024:1536],
                                       SCR[0:1, 1024:1536]), inc=sV)
        op("a", lambda e: e.activation(out=SCR[0:1, 1024:1536],
                                       in_=SCR[0:1, 1024:1536], func=AF.Sqrt),
           waits=[sV], inc=sA)
        op("v", lambda e: e.tensor_mul(SCR[0:1, 1536:2048], SCR[0:1, 0:512],
                                       SCR[0:1, 1024:1536]), waits=[sA],
           inc=sV)
        op("t", lambda e: e.matmul(sc[0][:], ONRF[:, 0:128],
                                   SCR[0:1, 1024:1536], start=True,
                                   stop=True), waits=[sV], inc=sT)
        op("t", lambda e: e.matmul(sc[1][:], ONRF[:, 0:128],
                                   SCR[0:1, 1536:2048], start=True,
                                   stop=True), inc=sT)
        for c in range(NCH):
            op("v", lambda e, c=c, rc=res_c3: e.tensor_mul(
                SCR[:, 3328:3840], rc(c), sc[0][:]), waits=[sT], inc=sV)
            op("v", lambda e, c=c: e.tensor_sub(
                SCR[:, 3328:3840], SCR[:, 3328:3840], sc[1][:]), inc=sV)
            op("v", lambda e, c=c, tt=tt: e.tensor_copy(XQ[:, c, tt * 512:
                                                           (tt + 1) * 512],
                                                        SCR[:, 3328:3840]),
               inc=sV)
            op("v", lambda e: e.reduce_max(CW[:, 25:26], SCR[:, 3328:3840],
                                           axis=AX.X,
                                           apply_absolute_value=True), inc=sV)
            op("v", lambda e: e.tensor_max(MAXC, MAXC, CW[:, 25:26]), inc=sV)

    def finish_max(dst_col, src_col=None):
        op("t", lambda e: e.transpose(bcb[0:1, 0:128].bitcast(F32)
                                      if False else bcb[0:128, 0:128][0:1, 0:128],
                                      MAXC, IDF[:])
           if False else e.transpose(bcb[0:1, 0:128], MAXC, IDF[:]),
           waits=[sV, (sA, sA.n)], inc=sT)
        op("v", lambda e, dc=dst_col: e.reduce_max(
            STT[0:1, 560 + dc:561 + dc], bcb[0:1, 0:128], axis=AX.X),
           waits=[sT], inc=sV)

    finish_max(0)
    op("sy", lambda e: e.dma_start(out=mx_in[0].ap(),
                                   in_=STT[0:1, 560:562]),
       waits=[(sV, sV.n)], inc=sMx, dma=True)
    op("g", lambda e: e.collective_compute(
        "AllReduce", ALU.max, replica_groups=[list(range(8))],
        ins=[mx_in[0].ap().opt()], outs=[mx_out[0].ap().opt()]),
       waits=[sMx.at()], inc=sCc)
    op("sy", lambda e: e.dma_start(out=STT[0:1, 560:562],
                                   in_=mx_out[0].ap()),
       waits=[(sCc, sCc.n)], inc=sMx, dma=True)
    mx1 = sMx.at()

    def scale_cols(src_off, inv_qmax, sx_col, rx_col, extra_wait=()):
        op("v", lambda e, so=src_off, iq=inv_qmax: e.tensor_scalar(
            out=STT[0:1, 564:565], in0=STT[0:1, 560 + so:561 + so],
            scalar1=iq, scalar2=1e-12, op0=ALU.mult, op1=ALU.max),
           waits=list(extra_wait) + [(sT, sT.n)], inc=sV)
        op("v", lambda e: e.reciprocal(STT[0:1, 565:566], STT[0:1, 564:565]),
           inc=sV)
        op("t", lambda e: e.matmul(bcb[:, 0:2], ONRF[:, 0:128],
                                   STT[0:1, 564:566], start=True, stop=True),
           waits=[sV, (sA, sA.n)], inc=sT)
        op("a", lambda e, sx=sx_col, rx=rx_col: e.copy(sx, bcb[:, 0:1]),
           waits=[sT], inc=sA)
        op("a", lambda e, sx=sx_col, rx=rx_col: e.copy(rx, bcb[:, 1:2]),
           waits=[sA.at()], inc=sA)

    SXC, RXC = CW[:, 26:27], CW[:, 27:28]
    SGC, RGC = CW[:, 28:29], CW[:, 29:30]
    SUC, RUC = CW[:, 30:31], CW[:, 31:32]
    S2C, R2C = CW[:, 32:33], CW[:, 33:34]
    SOC, ROC = CW[:, 34:35], CW[:, 35:36]
    scale_cols(0, QMAXI, SXC, RXC, extra_wait=[mx1])
    for tt in range(NQB):
        for c in range(NCH):
            sl = XQ[:, c, tt * 512:(tt + 1) * 512]
            op("v", lambda e, sl=sl: e.tensor_scalar(
                out=SCR[:, 3328:3840], in0=sl, scalar1=RXC, scalar2=MAGIC,
                op0=ALU.mult, op1=ALU.add), waits=[(sA, sA.n)], inc=sV)
            op("v", lambda e: e.tensor_scalar(
                out=SCR[:, 3328:3840], in0=SCR[:, 3328:3840], scalar1=MAGIC,
                scalar2=32767.0, op0=ALU.subtract, op1=ALU.min), inc=sV)
            op("v", lambda e, sl=sl: e.tensor_scalar(
                out=sl, in0=SCR[:, 3328:3840], scalar1=-32768.0, scalar2=SXC,
                op0=ALU.max, op1=ALU.mult), inc=sV)

    if dbg_outs:
        xq_dbg = nc.dram_tensor("xq_dbg", [NCH, 128, TOK], BF16,
                                kind="ExternalOutput")
        for c in range(NCH):
            op("sy", lambda e, c=c: e.dma_start(
                out=xq_dbg.ap()[c], in_=XQ[:, c, :]),
               waits=[(sV, sV.n)], inc=sMx, dma=True)

    def ffn1_mm(wsrc, spill, wready, track_max):
        if track_max:
            op("v", lambda e: e.memset(MAXC, 0.0), inc=sV)
        unit = 0
        for ob in range(12):
            for tt in range(NQB):
                sl = unit % 2
                if tt == 0:
                    for k in range(NCH):
                        woff = (k % 4) * 512 if k < 4 else 4096 + (k - 8 % 4) * 512
                        woff = (k * 512) if k < 4 else 4096 + (k - 4) * 512
                        op("sy", lambda e, ob=ob, k=k, woff=woff, wsrc=wsrc:
                           e.dma_start(
                               out=WS[:, woff:woff + 256],
                               in_=wsrc.ap()[ob * 256:(ob + 1) * 256,
                                             k * 128:(k + 1) * 128],
                               transpose=True),
                           waits=([wready] if (ob == 0 and k == 0) else [])
                           + ([(sT, kb._f1_guard)] if k == 0 else []),
                           inc=sWq, dma=True)
                    kb._f1_ld = sWq.at()
                for oc in range(2):
                    bank = mm[(ob % 2) * 2 + oc]
                    for k in range(NCH):
                        woff = (k * 512) if k < 4 else 4096 + (k - 4) * 512
                        op("t", lambda e, b=bank, k=k, oc=oc, tt=tt,
                           woff=woff: e.matmul(
                               b[:], WS[:, woff + oc * 128:
                                        woff + oc * 128 + 128],
                               XQ[:, k, tt * 512:(tt + 1) * 512],
                               start=(k == 0), stop=(k == NCH - 1)),
                           waits=([kb._f1_ld] if k == 0 else [])
                           + ([(sA, sA.n), (sV, sV.n)] if k == 0 else []),
                           inc=sT)
                    o = ob * 2 + oc
                    op("a", lambda e, b=bank, oc=oc: e.copy(SQB[oc], b[:]),
                       waits=[sT], inc=sA)
                    if track_max:
                        op("v", lambda e, oc=oc: e.reduce_max(
                            CW[:, 25:26], SQB[oc], axis=AX.X,
                            apply_absolute_value=True), waits=[sA], inc=sV)
                        op("v", lambda e: e.tensor_max(MAXC, MAXC,
                                                       CW[:, 25:26]), inc=sV)
                    op("v", lambda e, oc=oc: e.tensor_copy(
                        WQO[:, oc * 512:(oc + 1) * 512], SQB[oc]),
                       waits=[sA, (sWqS, sWqS.n),
                              (sSpH[oc], kb._sp_guard[oc])],
                       inc=sV)
                    op("sy", lambda e, o=o, tt=tt, spill=spill, oc=oc:
                       e.dma_start(
                           out=spill.ap()[o, :, tt * 512:(tt + 1) * 512],
                           in_=WQO[:, oc * 512:(oc + 1) * 512]),
                       waits=[(sV, sV.n)], inc=sSpH[oc], dma=True)
                    kb._sp_guard[oc] = sSpH[oc].n
                kb._f1_guard = sT.n
                unit += 1

    kb._f1_guard = 0
    kb._fd_guard = [0, 0]
    kb._gd_slot = 0
    kb._sp_guard = [0, 0]
    kb._f1_ld = (sWq, 0)
    ffn1_mm(wg_bf, gbuf, wprep_done["g"], True)
    finish_max(1)
    ffn1_mm(wu_bf, ubuf, wprep_done["u"], True)
    finish_max(2)
    op("sy", lambda e: e.dma_start(out=mx_in[1].ap(),
                                   in_=STT[0:1, 561:563]),
       waits=[(sV, sV.n)], inc=sMx, dma=True)
    op("g", lambda e: e.collective_compute(
        "AllReduce", ALU.max, replica_groups=[list(range(8))],
        ins=[mx_in[1].ap().opt()], outs=[mx_out[1].ap().opt()]),
       waits=[sMx.at()], inc=sCc)
    op("sy", lambda e: e.dma_start(out=STT[0:1, 561:563],
                                   in_=mx_out[1].ap()),
       waits=[(sCc, sCc.n)], inc=sMx, dma=True)
    mx2 = sMx.at()
    scale_cols(1, QMAXI, SGC, RGC, extra_wait=[mx2])
    scale_cols(2, QMAXI, SUC, RUC)

    TG2 = A1[:, 16384:28672].rearrange("p (c t) -> p c t", c=6)
    TU2 = A1[:, 28672:40960].rearrange("p (c t) -> p c t", c=6)
    SGQ = SCR[:, 0:2048]
    SSG = SCR[:, 2048:4096]
    op("v", lambda e: e.memset(MAXC, 0.0), waits=[(sA, sA.n)], inc=sV)
    g_rel = [0, 0]
    for oc in range(24):
        u = oc % 2
        srel = sRl[u]
        op("sy", lambda e, oc=oc, u=u: e.dma_start(
            out=TG2[:, u, :], in_=gbuf.ap()[oc, :, 0:2048]),
           waits=[(sSp, sSp.n), sSpH[0].at(), sSpH[1].at(),
                  (sV, g_rel[u])], inc=srel, dma=True)
        rl = srel.at()
        op("v", lambda e, u=u: e.tensor_scalar(
            out=SGQ, in0=TG2[:, u, :], scalar1=RGC, scalar2=MAGIC,
            op0=ALU.mult, op1=ALU.add), waits=[rl, (sA, sA.n)], inc=sV)
        op("v", lambda e: e.tensor_scalar(
            out=SGQ, in0=SGQ, scalar1=MAGIC, scalar2=32767.0,
            op0=ALU.subtract, op1=ALU.min), inc=sV)
        op("v", lambda e: e.tensor_scalar(
            out=SGQ, in0=SGQ, scalar1=-32768.0, scalar2=SGC,
            op0=ALU.max, op1=ALU.mult), inc=sV)
        op("a", lambda e: e.activation(out=SSG, in_=SGQ, func=AF.Sigmoid),
           waits=[(sV, sV.n)], inc=sA)
        op("v", lambda e: e.tensor_scalar(
            out=SSG, in0=SSG, scalar1=65536.0, scalar2=MAGIC,
            op0=ALU.mult, op1=ALU.add), waits=[sA], inc=sV)
        op("v", lambda e: e.tensor_scalar(
            out=SSG, in0=SSG, scalar1=MAGIC, scalar2=65535.0,
            op0=ALU.subtract, op1=ALU.min), inc=sV)
        op("v", lambda e: e.tensor_scalar(
            out=SSG, in0=SSG, scalar1=0.0, scalar2=1.0 / 65536.0,
            op0=ALU.max, op1=ALU.mult), inc=sV)
        op("v", lambda e, u=u: e.tensor_mul(TG2[:, u, :], SGQ, SSG), inc=sV)
        op("v", lambda e, u=u: e.reduce_max(
            CW[:, 25:26], TG2[:, u, :], axis=AX.X,
            apply_absolute_value=True), inc=sV)
        op("v", lambda e: e.tensor_max(MAXC, MAXC, CW[:, 25:26]), inc=sV)
        g_rel[u] = sV.n
        op("sy", lambda e, oc=oc, u=u: e.dma_start(
            out=g2buf.ap()[oc, :, 0:2048], in_=TG2[:, u, :]),
           waits=[(sV, sV.n)], inc=sSp, dma=True)
    finish_max(3)
    op("sy", lambda e: e.dma_start(out=mx_in[2].ap(),
                                   in_=STT[0:1, 563:565])
       if False else e.dma_start(out=mx_in[2].ap(), in_=STT[0:1, 563:565]),
       waits=[(sV, sV.n)], inc=sMx, dma=True)
    op("g", lambda e: e.collective_compute(
        "AllReduce", ALU.max, replica_groups=[list(range(8))],
        ins=[mx_in[2].ap().opt()], outs=[mx_out[2].ap().opt()]),
       waits=[sMx.at()], inc=sCc)
    op("sy", lambda e: e.dma_start(out=STT[0:1, 563:565],
                                   in_=mx_out[2].ap()),
       waits=[(sCc, sCc.n)], inc=sMx, dma=True)
    mx3 = sMx.at()
    scale_cols(3, QMAXI, S2C, R2C, extra_wait=[mx3])

    op("v", lambda e: e.memset(MAXC, 0.0), waits=[(sA, sA.n)], inc=sV)
    o_rel = [0, 0]
    for oc in range(24):
        u = oc % 2
        srel = sRl[u]
        op("sy", lambda e, oc=oc, u=u: e.dma_start(
            out=TG2[:, u, :], in_=g2buf.ap()[oc, :, 0:2048]),
           waits=[(sSp, sSp.n), sSpH[0].at(), sSpH[1].at(),
                  (sV, o_rel[u])], inc=srel, dma=True)
        op("sy", lambda e, oc=oc, u=u: e.dma_start(
            out=TU2[:, u, :], in_=ubuf.ap()[oc, :, 0:2048]),
           inc=srel, dma=True)
        rl = srel.at()
        op("v", lambda e, u=u: e.tensor_scalar(
            out=SGQ, in0=TG2[:, u, :], scalar1=R2C, scalar2=MAGIC,
            op0=ALU.mult, op1=ALU.add), waits=[rl, (sA, sA.n)], inc=sV)
        op("v", lambda e: e.tensor_scalar(
            out=SGQ, in0=SGQ, scalar1=MAGIC, scalar2=32767.0,
            op0=ALU.subtract, op1=ALU.min), inc=sV)
        op("v", lambda e: e.tensor_scalar(
            out=SGQ, in0=SGQ, scalar1=-32768.0, scalar2=S2C,
            op0=ALU.max, op1=ALU.mult), inc=sV)
        op("v", lambda e, u=u: e.tensor_scalar(
            out=SSG, in0=TU2[:, u, :], scalar1=RUC, scalar2=MAGIC,
            op0=ALU.mult, op1=ALU.add), inc=sV)
        op("v", lambda e: e.tensor_scalar(
            out=SSG, in0=SSG, scalar1=MAGIC, scalar2=32767.0,
            op0=ALU.subtract, op1=ALU.min), inc=sV)
        op("v", lambda e: e.tensor_scalar(
            out=SSG, in0=SSG, scalar1=-32768.0, scalar2=SUC,
            op0=ALU.max, op1=ALU.mult), inc=sV)
        op("v", lambda e, u=u: e.tensor_mul(TG2[:, u, :], SGQ, SSG), inc=sV)
        op("v", lambda e, u=u: e.reduce_max(
            CW[:, 25:26], TG2[:, u, :], axis=AX.X,
            apply_absolute_value=True), inc=sV)
        op("v", lambda e: e.tensor_max(MAXC, MAXC, CW[:, 25:26]), inc=sV)
        o_rel[u] = sV.n
        op("sy", lambda e, oc=oc, u=u: e.dma_start(
            out=obuf.ap()[oc, :, 0:2048], in_=TG2[:, u, :]),
           waits=[(sV, sV.n)], inc=sSp, dma=True)
    finish_max(4)
    op("sy", lambda e: e.dma_start(out=mx_in[3].ap(), in_=STT[0:1, 564:566]),
       waits=[(sV, sV.n)], inc=sMx, dma=True)
    op("g", lambda e: e.collective_compute(
        "AllReduce", ALU.max, replica_groups=[list(range(8))],
        ins=[mx_in[3].ap().opt()], outs=[mx_out[3].ap().opt()]),
       waits=[sMx.at()], inc=sCc)
    op("sy", lambda e: e.dma_start(out=STT[0:1, 564:566],
                                   in_=mx_out[3].ap()),
       waits=[(sCc, sCc.n)], inc=sMx, dma=True)
    mx4 = sMx.at()
    scale_cols(4, QMAXI, SOC, ROC, extra_wait=[mx4])

    o_rel2 = 0
    for tt in range(NQB):
        for oc in range(24):
            op("sy", lambda e, oc=oc, tt=tt: e.dma_start(
                out=TU[:, oc, 0:512],
                in_=obuf.ap()[oc, :, tt * 512:(tt + 1) * 512]),
               waits=[(sSp, sSp.n), sSpH[0].at(), sSpH[1].at(),
                      (sT, o_rel2)], inc=sRl[0], dma=True)
        rl = sRl[0].at()
        for oc in range(24):
            op("v", lambda e, oc=oc: e.tensor_scalar(
                out=SCR[:, 2048:2560], in0=TU[:, oc, 0:512], scalar1=ROC,
                scalar2=MAGIC, op0=ALU.mult, op1=ALU.add),
               waits=[rl, (sA, sA.n)] if oc == 0 else [], inc=sV)
            op("v", lambda e: e.tensor_scalar(
                out=SCR[:, 2048:2560], in0=SCR[:, 2048:2560], scalar1=MAGIC,
                scalar2=32767.0, op0=ALU.subtract, op1=ALU.min), inc=sV)
            op("v", lambda e, oc=oc: e.tensor_scalar(
                out=TU[:, oc, 0:512], in0=SCR[:, 2048:2560],
                scalar1=-32768.0, scalar2=SOC, op0=ALU.max, op1=ALU.mult),
               inc=sV)
        for ob in range(4):
            for kh in range(2):
                for k2 in range(12):
                    k = kh * 12 + k2
                    woff = k2 * 256 if k2 < 8 else 4096 + (k2 - 8) * 256
                    if tt == 0:
                        op("sy", lambda e, ob=ob, k=k, woff=woff, k2=k2:
                           e.dma_start(
                               out=WS[:, woff:woff + 256],
                               in_=wd_bf.ap()[ob * 256:(ob + 1) * 256,
                                              k * 128:(k + 1) * 128],
                               transpose=True),
                           waits=([wprep_done["d"]] if (ob == 0 and k == 0)
                                  else [])
                           + ([(sT, kb._fd_guard[0]),
                               (sGd, kb._gd_slot)] if k2 == 0 else []),
                           inc=sWq, dma=True)
                    else:
                        op("sy", lambda e, ob=ob, k=k, woff=woff, k2=k2:
                           e.dma_start(
                               out=WS[:, woff:woff + 256],
                               in_=wdT.ap()[k * 128:(k + 1) * 128,
                                            ob * 256:(ob + 1) * 256]),
                           waits=([(sT, kb._fd_guard[0]),
                                   (sGd, kb._gd_slot)] if k2 == 0 else []),
                           inc=sWq, dma=True)
                ldw2 = sWq.at()
                if tt == 0:
                    for k2 in range(12):
                        k = kh * 12 + k2
                        woff = k2 * 256 if k2 < 8 else 4096 + (k2 - 8) * 256
                        op("g", lambda e, ob=ob, k=k, woff=woff:
                           e.dma_start(
                               out=wdT.ap()[k * 128:(k + 1) * 128,
                                            ob * 256:(ob + 1) * 256],
                               in_=WS[:, woff:woff + 256]),
                           waits=[ldw2], inc=sGd, dma=True)
                    kb._gd_slot = sGd.n
                for oc2 in range(2):
                    bank = mm[(ob % 2) * 2 + oc2]
                    for k2 in range(12):
                        k = kh * 12 + k2
                        woff = k2 * 256 if k2 < 8 else 4096 + (k2 - 8) * 256
                        op("t", lambda e, b=bank, k=k, k2=k2, oc2=oc2,
                           tt=tt, woff=woff: e.matmul(
                               b[:], WS[:, woff + oc2 * 128:
                                        woff + oc2 * 128 + 128],
                               TU[:, k, 0:512],
                               start=(k == 0), stop=(k == 23)),
                           waits=([ldw2] if k2 == 0 else [])
                           + ([(sA, sA.n), (sV, sV.n)]
                              if (k == 0 and k2 == 0) else []),
                           inc=sT)
                    if kh == 1:
                        o = ob * 2 + oc2
                        op("a", lambda e, b=bank, o=o, tt=tt: e.copy(
                            BOTF[:, o, tt * 512:(tt + 1) * 512], b[:]),
                           waits=[sT], inc=sA)
                kb._fd_guard[0] = sT.n
        o_rel2 = sT.n
        def bot3(c, tt=tt):
            return BOTF[:, c, tt * 512:(tt + 1) * 512]
        ln_rows_from(bot3, tt, NCH, INV_DM)
        for c in range(NCH):
            op("v", lambda e, c=c, tt=tt: e.tensor_mul(
                SCR[:, 3328:3840], BOTF[:, c, tt * 512:(tt + 1) * 512],
                sc[0][:]), waits=[sT], inc=sV)
            op("v", lambda e: e.tensor_sub(SCR[:, 3328:3840],
                                           SCR[:, 3328:3840], sc[1][:]),
               inc=sV)
            op("v", lambda e, c=c, tt=tt: e.tensor_add(
                RES[:, c * TOK + tt * 512: c * TOK + (tt + 1) * 512],
                RES[:, c * TOK + tt * 512: c * TOK + (tt + 1) * 512],
                SCR[:, 3328:3840]), inc=sV)

    ostg = [SCR[:, 0:1024], SCR[:, 1024:2048]]
    for it in range(TOK // 128):
        ot = ostg[it % 2]
        ss = sSt[it % 2]
        for c in range(NCH):
            op("t", lambda e, c=c, it=it: e.transpose(
                bcb[:, 0:128],
                RES[:, c * TOK + it * 128: c * TOK + (it + 1) * 128],
                IDF[:]), waits=[sA, sV], inc=sT)
            w = [(sT, sT.n)]
            if it >= 2 and c == 0:
                w.append(ss.at())
            op("a", lambda e, ot=ot, c=c: e.copy(
                ot[:, c * 128:(c + 1) * 128], bcb[:, 0:128]),
               waits=w, inc=sA)
        op("sy", lambda e, ot=ot, it=it: e.dma_start(
            out=y[it * 128:(it + 1) * 128, :], in_=ot),
           waits=[sA], inc=ss, dma=True)
    op("sy", lambda e: e.nop(), waits=[sSt[0], sSt[1]])

    kb.replay()
    kb.es.close()
    return kb.nc


_PROG = None


def _get_prog():
    global _PROG
    if _PROG is None:
        _PROG = build_program()
    return _PROG


def make_in_maps(inputs):
    def f32(k):
        return np.ascontiguousarray(np.asarray(inputs[k], dtype=np.float32))
    x = f32("L0_attn_x")
    shared = {
        "W_dq": f32("L0_attn_W_dq"), "W_uq": f32("L0_attn_W_uq"),
        "q_ln_g": f32("L0_attn_q_ln_g"), "q_ln_b": f32("L0_attn_q_ln_b"),
        "W_dkv": f32("L0_attn_W_dkv"), "W_ukv": f32("L0_attn_W_ukv"),
        "kv_ln_g": f32("L0_attn_kv_ln_g"), "kv_ln_b": f32("L0_attn_kv_ln_b"),
        "W_o": f32("L0_attn_W_o"),
        "w1_w": f32("L0_ffn_w1_w"), "w1_s": f32("L0_ffn_w1_s"),
        "w2_w": f32("L0_ffn_w2_w"), "w2_s": f32("L0_ffn_w2_s"),
        "h1_w": f32("L0_ffn_h1_w"), "h1_s": f32("L0_ffn_h1_s"),
        "h2_w": f32("L0_ffn_h2_w"), "h2_s": f32("L0_ffn_h2_s"),
        "w3_w": f32("L0_ffn_w3_w"), "w3_s": f32("L0_ffn_w3_s"),
        "qkv_w": f32("L1_attn_qkv_w"),
        "qn_w": f32("L1_attn_qn_w"), "qn_b": f32("L1_attn_qn_b"),
        "kn_w": f32("L1_attn_kn_w"), "kn_b": f32("L1_attn_kn_b"),
        "sinks": f32("L1_attn_sinks").reshape(1, 16),
        "proj_w": f32("L1_attn_proj_w"), "proj_b": f32("L1_attn_proj_b"),
        "w_gate": f32("L1_ffn_w_gate"), "w_up": f32("L1_ffn_w_up"),
        "w_down": f32("L1_ffn_w_down"),
    }
    inv_freq = 1.0 / (10000.0 ** (np.arange(0, 64, 2, dtype=np.float32) / 64))
    in_maps = []
    for core in range(8):
        b, h = core // 2, core % 2
        m = dict(shared)
        m["x_q"] = np.ascontiguousarray(x[b, h * TOK:(h + 1) * TOK, :])
        m["x_seq"] = np.ascontiguousarray(x[b])
        kx = (np.arange(128)[:, None] + 128 * np.arange(32)[None, :]
              - 2048 * h).astype(np.float32)
        m["kidx"] = kx
        pos = (h * TOK + np.arange(TOK, dtype=np.float32))
        fr = pos[None, :] * inv_freq[:, None]
        m["cosb"] = np.tile(np.cos(fr), (4, 1)).astype(np.float16)
        m["sinb"] = np.tile(np.sin(fr), (4, 1)).astype(np.float16)
        kl = (np.arange(128)[:, None]
              + 128 * (np.arange(18)[None, :] - 2)).astype(np.float32)
        k1 = np.concatenate([kl, kl + 255.0], axis=1)
        if h == 0:
            k1[:, 0:2] = 1e9
            k1[:, 18:20] = 1e9
        m["kix1"] = k1
        in_maps.append(m)
    return in_maps


def kernel(**inputs):
    nc = _get_prog()
    in_maps = make_in_maps(inputs)
    res = run_bass_kernel_spmd(nc, in_maps, core_ids=list(range(8)))
    out = np.empty((B, S, DM), dtype=np.float32)
    for core in range(8):
        b, h = core // 2, core % 2
        out[b, h * TOK:(h + 1) * TOK, :] = res.results[core]["y"]
    return out
